# version 06
# baseline (speedup 1.0000x reference)
import math
from contextlib import ExitStack

import numpy as np
import concourse.bass as bass
import concourse.mybir as mybir
from concourse.bass_utils import run_bass_kernel_spmd

F32 = mybir.dt.float32
BF16 = mybir.dt.bfloat16
AF = mybir.ActivationFunctionType
ALU = mybir.AluOpType
AX = mybir.AxisListType


_ENG_ATTR = {"pe": "tensor", "dve": "vector", "act": "scalar", "pool": "gpsimd", "sp": "sync"}
_ENGINES = list(_ENG_ATTR)

_DT_SIZE = {"dt.float32": 4, "dt.bfloat16": 2, "dt.float16": 2, "dt.int32": 4,
            "dt.uint32": 4, "dt.int8": 1, "dt.uint8": 1, "dt.float32r": 4,
            "dt.int16": 2, "dt.uint16": 2, "dt.int64": 8, "dt.uint64": 8}


def _ap_region(ap):
    if not hasattr(ap, "tensor"):
        ap = ap[:]
    t = ap.tensor
    name = t.name
    tn = type(t).__name__
    if "PSum" in tn:
        space = "psum"
    elif "SB" in tn:
        space = "sbuf"
    else:
        return (name, "dram", 0, 1, 0, 1)
    shape = list(t.shape)
    esz = _DT_SIZE[str(t.dtype)]
    rowpitch = 1
    for s in shape[1:]:
        rowpitch *= s
    pairs = [list(p) for p in ap.ap]
    off = ap.offset
    pidx = None
    for i, (st, ct) in enumerate(pairs):
        if st == rowpitch and rowpitch > 1:
            pidx = i
            break
    if pidx is None:
        if shape[0] == 1 or rowpitch == 1:
            pidx = -1
        else:
            maxspan = off + sum(abs(st) * (ct - 1) for st, ct in pairs)
            if maxspan < rowpitch * (off // rowpitch + 1):
                pidx = -1
            else:
                return (name, space, 0, shape[0], 0, rowpitch * esz)
    plo = off // rowpitch
    f0 = off % rowpitch
    fspan = 0
    pcnt = 1
    for i, (st, ct) in enumerate(pairs):
        if i == pidx:
            pcnt = ct
        else:
            fspan += abs(st) * (ct - 1)
    if f0 + fspan >= rowpitch:
        return (name, space, 0, shape[0], 0, rowpitch * esz)
    blo, bhi = f0 * esz, (f0 + fspan + 1) * esz
    if space == "psum":
        blo = (blo // 2048) * 2048
        bhi = ((bhi + 2047) // 2048) * 2048
    return (name, space, plo, plo + pcnt, blo, bhi)


def _overlap(r1, r2):
    return (r1[0] == r2[0] and r1[2] < r2[3] and r2[2] < r1[3]
            and r1[4] < r2[5] and r2[4] < r1[5])


def _covers(w, r):
    return (w[0] == r[0] and w[2] <= r[2] and w[3] >= r[3]
            and w[4] <= r[4] and w[5] >= r[5])


class _Op:
    __slots__ = ("idx", "eng", "emit", "reads", "writes", "deps", "need_inc",
                 "is_dma", "sem_key", "seq", "tag")

    def __init__(self, idx, eng, emit, reads, writes, is_dma, sem_key, tag):
        self.idx = idx
        self.eng = eng
        self.emit = emit
        self.reads = reads
        self.writes = writes
        self.deps = []
        self.need_inc = is_dma
        self.is_dma = is_dma
        self.sem_key = sem_key
        self.seq = None
        self.tag = tag


class Mk:
    SERIALIZE = False

    def __init__(self, nc):
        self.nc = nc
        self.ops = []
        self.w_recs = {}
        self.r_recs = {}
        self._es = ExitStack()
        self.dma_counts = {}
        self._const_deps = []
        self._const_cols = {}

    def sbuf(self, name, shape, dtype):
        return self._es.enter_context(self.nc.sbuf_tensor(name, shape, dtype))

    def psum(self, name, shape, dtype=F32):
        return self._es.enter_context(self.nc.psum_tensor(name, shape, dtype))

    def dram(self, name, shape, dtype, kind="Internal"):
        return self.nc.dram_tensor(name, shape, dtype, kind=kind).ap()

    def init_consts(self, vals):
        vals = sorted(set(float(v) for v in vals) - {0.0, 1.0})
        if not vals:
            return
        n = len(vals)
        cf = self.sbuf("mk_consts_f32", [128, n], F32)
        cb = self.sbuf("mk_consts_bf16", [128, n], BF16)
        for i, v in enumerate(vals):
            self.nc.const_aps.register(F32, v, cf[:, i:i + 1])
            self.nc.const_aps.register(BF16, v, cb[:, i:i + 1])
            o1 = self.op("pool", lambda g, _cf=cf, _i=i, _v=v: g.memset(_cf[:, _i:_i + 1], _v),
                         writes=[cf[:, i:i + 1]], tag="const")
            o2 = self.op("pool", lambda g, _cb=cb, _i=i, _v=v: g.memset(_cb[:, _i:_i + 1], _v),
                         writes=[cb[:, i:i + 1]], tag="const")
            self._const_deps += [o1.idx, o2.idx]

    def _record(self, eng, emit, reads, writes, is_dma=False, sem_key=None, tag=""):
        idx = len(self.ops)
        rr = [_ap_region(a) for a in reads if a is not None]
        ww = [_ap_region(a) for a in writes if a is not None]
        op = _Op(idx, eng, emit, rr, ww, is_dma, sem_key, tag)
        deps = set()
        if not is_dma and tag != "const":
            deps.update(self._const_deps)
        for r in rr:
            for rec in self.w_recs.get(r[0], ()):
                if _overlap(r, rec[0]):
                    deps.add(rec[1])
            if r[1] == "psum" and eng in ("dve", "act"):
                other = "act" if eng == "dve" else "dve"
                rec = self.r_recs.get(r[0], {}).get(other)
                if rec is not None and _overlap(r, rec[0]):
                    deps.add(rec[1])
        for w in ww:
            for rec in self.w_recs.get(w[0], ()):
                if _overlap(w, rec[0]):
                    deps.add(rec[1])
            for rec in self.r_recs.get(w[0], {}).values():
                if _overlap(w, rec[0]):
                    deps.add(rec[1])
        if Mk.SERIALIZE and idx > 0:
            deps.add(idx - 1)
        op.deps = sorted(deps)
        key = sem_key if is_dma else eng
        for r in rr:
            d = self.r_recs.setdefault(r[0], {})
            if key in d:
                old = d[key][0]
                d[key] = [(r[0], r[1], min(old[2], r[2]), max(old[3], r[3]),
                           min(old[4], r[4]), max(old[5], r[5])), idx]
            else:
                d[key] = [r, idx]
        for w in ww:
            lst = self.w_recs.setdefault(w[0], [])
            lst[:] = [rec for rec in lst if not _covers(w, rec[0])]
            lst.append([w, idx])
            if len(lst) > 96:
                self._consolidate(lst)
            d = self.r_recs.get(w[0])
            if d:
                for k in list(d):
                    if _covers(w, d[k][0]):
                        del d[k]
        self.ops.append(op)
        return op

    def _consolidate(self, lst):
        byk = {}
        out = []
        half = lst[: len(lst) // 2]
        for rec in half:
            op = self.ops[rec[1]]
            k = op.sem_key if op.is_dma else op.eng
            if k in byk:
                o = byk[k]
                r0, r1 = o[0], rec[0]
                o[0] = (r0[0], r0[1], min(r0[2], r1[2]), max(r0[3], r1[3]),
                        min(r0[4], r1[4]), max(r0[5], r1[5]))
                o[1] = max(o[1], rec[1])
            else:
                byk[k] = rec
                out.append(rec)
        lst[:] = out + lst[len(lst) // 2:]

    def op(self, eng, emit, reads=(), writes=(), tag=""):
        return self._record(eng, emit, list(reads), list(writes), tag=tag)

    def dma(self, queue, out, in_, sem_key=None, tag="", **kw):
        if sem_key is None:
            sem_key = (queue, _ap_region(out)[0])
        else:
            sem_key = (queue, sem_key)
        emit = lambda e, _o=out, _i=in_, _k=dict(kw): e.dma_start(out=_o, in_=_i, **_k)
        return self._record(queue, emit, [in_], [out], is_dma=True, sem_key=sem_key, tag=tag)

    def emit_program(self):
        nc = self.nc
        ops = self.ops
        for op in ops:
            for d in op.deps:
                ops[d].need_inc = True
        eng_seq = {e: 0 for e in _ENGINES}
        cnt = {}
        for op in ops:
            if op.is_dma:
                cnt[op.sem_key] = cnt.get(op.sem_key, 0) + 1
                op.seq = cnt[op.sem_key]
            elif op.need_inc:
                eng_seq[op.eng] += 1
                op.seq = eng_seq[op.eng]
        self.dma_counts = cnt
        es = self._es
        eng_sems = {e: es.enter_context(nc.semaphore(f"se_{e}")) for e in _ENGINES}
        sem_keys = sorted(cnt, key=str)
        dma_sems = {k: es.enter_context(nc.semaphore(f"sd_{i}")) for i, k in enumerate(sem_keys)}

        per_eng = {e: [] for e in _ENGINES}
        for op in ops:
            per_eng[op.eng].append(op)

        block = es.enter_context(nc.Block())
        mkops = self.ops

        def make_body(ename, elist, final=False):
            def body(eng):
                clock = {}
                for op in elist:
                    if op.is_dma and op.seq >= 2:
                        kk = ("d", op.sem_key)
                        v = 16 * (op.seq - 1)
                        if clock.get(kk, -1) < v:
                            eng.wait_ge(dma_sems[op.sem_key], v)
                            clock[kk] = v
                    for d in op.deps:
                        p = mkops[d]
                        if (not p.is_dma) and p.eng == "pe" and ename == "pe":
                            continue
                        if p.is_dma:
                            kk, v, sem = ("d", p.sem_key), 16 * p.seq, dma_sems[p.sem_key]
                        else:
                            kk, v, sem = ("e", p.eng), p.seq, eng_sems[p.eng]
                        if clock.get(kk, -1) < v:
                            eng.wait_ge(sem, v)
                            clock[kk] = v
                    inst = op.emit(eng)
                    if op.is_dma:
                        inst.then_inc(dma_sems[op.sem_key], 16)
                    elif op.need_inc:
                        inst.then_inc(eng_sems[op.eng], 1)
                if final:
                    for k in sem_keys:
                        eng.wait_ge(dma_sems[k], 16 * cnt[k])
            return body

        for ename in _ENGINES:
            if per_eng[ename] or ename == "sp":
                body = make_body(ename, per_eng[ename], final=(ename == "sp"))
                getattr(block, _ENG_ATTR[ename])(body)
        es.close()
        return nc


def mm(mk, out, lhsT, rhs, start=True, stop=True, tile_position=None, tag="mm", skip=False):
    return mk.op("pe", lambda t, _o=out, _l=lhsT, _r=rhs, _s=start, _e=stop, _p=tile_position,
                 _k=skip:
                 t.matmul(_o, _l, _r, start=_s, stop=_e, tile_position=_p, skip_group_check=_k),
                 reads=[lhsT, rhs], writes=[out], tag=tag)


def transpose(mk, out, in_, ident, tag="tr"):
    return mk.op("pe", lambda t, _o=out, _i=in_, _d=ident: t.transpose(_o, _i, _d),
                 reads=[in_, ident], writes=[out], tag=tag)


def act(mk, out, in_, func, bias=None, scale=None, accum_out=None, tag="act"):
    kw = {}
    if bias is not None:
        kw["bias"] = bias
    if scale is not None:
        kw["scale"] = scale
    if accum_out is not None:
        kw["accum_out"] = accum_out
    reads = [in_] + ([bias] if hasattr(bias, "tensor") else [])
    writes = [out] + ([accum_out] if accum_out is not None else [])
    return mk.op("act", lambda s, _o=out, _i=in_, _f=func, _k=dict(kw):
                 s.activation(out=_o, in_=_i, func=_f, **_k), reads=reads, writes=writes, tag=tag)


def tt(mk, eng, out, in0, in1, op, tag="tt"):
    return mk.op(eng, lambda v, _o=out, _a=in0, _b=in1, _p=op: v.tensor_tensor(_o, _a, _b, _p),
                 reads=[in0, in1], writes=[out], tag=tag)


def ts(mk, eng, out, in0, s1, op0, s2=None, op1=None, tag="ts"):
    def emitfn(v, _o=out, _i=in0, _s1=s1, _s2=s2, _o0=op0, _o1=op1):
        if _s2 is None:
            return v.tensor_scalar(_o, _i, _s1, None, _o0)
        return v.tensor_scalar(_o, _i, _s1, _s2, _o0, _o1)
    reads = [in0] + [s for s in (s1, s2) if hasattr(s, "tensor")]
    return mk.op(eng, emitfn, reads=reads, writes=[out], tag=tag)


def copy(mk, eng, out, in_, tag="copy"):
    if eng == "act":
        return mk.op(eng, lambda s, _o=out, _i=in_: s.activation(out=_o, in_=_i, func=AF.Identity),
                     reads=[in_], writes=[out], tag=tag)
    return mk.op(eng, lambda v, _o=out, _i=in_: v.tensor_copy(_o, _i), reads=[in_], writes=[out], tag=tag)


def reduce(mk, out, in_, op, tag="red"):
    return mk.op("dve", lambda v, _o=out, _i=in_, _p=op:
                 v.tensor_reduce(_o, _i, AX.X, _p), reads=[in_], writes=[out], tag=tag)


def recip(mk, out, in_, tag="recip"):
    return mk.op("dve", lambda v, _o=out, _i=in_: v.reciprocal(_o, _i),
                 reads=[in_], writes=[out], tag=tag)


def scan(mk, out, d0, d1, init, op0, op1, tag="scan"):
    return mk.op("dve", lambda v, _o=out, _a=d0, _b=d1, _n=init, _p=op0, _q=op1:
                 v.tensor_tensor_scan(_o, _a, _b, _n, _p, _q),
                 reads=[d0, d1], writes=[out], tag=tag)


def stt(mk, out, in0, scalar, in1, op0, op1, tag="stt"):
    reads = [in0, in1] + ([scalar] if hasattr(scalar, "tensor") else [])
    return mk.op("dve", lambda v, _o=out, _a=in0, _s=scalar, _b=in1, _p=op0, _q=op1:
                 v.scalar_tensor_tensor(_o, _a, _s, _b, _p, _q),
                 reads=reads, writes=[out], tag=tag)


def memset(mk, eng, ap_, val, tag="memset"):
    return mk.op(eng, lambda v, _o=ap_, _v=val: v.memset(_o, _v), writes=[ap_], tag=tag)



D = 1024
NH0, HS0 = 16, 64
NH1, HS1, M1 = 32, 32, 2
E0, F0 = 8, 4096
LN_EPS = 1e-5
SOFT_M = 40.0
GLA_EPS = 1e-6
BMIN, BMAX = 0.85, 0.9995


ALL_CONSTS = [1.0 / D, LN_EPS, -1.0, -0.5, -SOFT_M, 0.125, GLA_EPS, BMIN, BMAX,
              2.0, 1.0 / F0, 0.5]


def ph_ln0(mk, io, T, layer=0):
    nt = T // 128
    sfx = f"_l{layer}"
    xsrc = io["x"] if layer == 0 else io["x1"]
    x_t = io["af"].alloc(2, D)
    xc = io["af"].alloc(2, D)
    sq = io["af"].alloc(D)
    st = mk.sbuf(f"ln_st{sfx}", [128, 8], F32)
    io["_st8_shared"] = st
    xnt32 = io["af"].alloc(D)
    xnf = io["ab"].alloc(D // 128, T)
    ps_tr = io["PS_F"]
    if layer == 0:
        wr = mk.sbuf("ln_wr", [128, 8, E0], BF16)
        mk.dma("pool", wr[:], io["w_router"].rearrange("(c p) e -> p c e", p=128), tag="wr")
        ps_r = io["PS_E"][:, 0:E0]
        lg = mk.sbuf("ln_lg", [128, E0], F32)
        eq = mk.sbuf("ln_eq", [128, E0], F32)
        cum = mk.sbuf("ln_cum", [128, E0], F32)
        zer8 = mk.sbuf("ln_z8", [128, E0], F32)
        msk = mk.sbuf("ln_msk", [128, nt, E0], F32)
        io["_msk_sb"] = msk
        memset(mk, "pool", zer8[:], 0.0)

    for i in range(nt):
        xi = x_t[:, i % 2, :]
        xci = xc[:, i % 2, :]
        mk.dma("sp", xi, xsrc[i * 128:(i + 1) * 128, :], sem_key=f"x{sfx}{i % 2}", tag="xload")
        reduce(mk, st[:, 0:1], xi, ALU.add)
        ts(mk, "dve", st[:, 1:2], st[:, 0:1], 1.0 / D, ALU.mult)
        ts(mk, "dve", xci, xi, st[:, 1:2], ALU.subtract)
        act(mk, sq[:], xci, AF.Square, accum_out=st[:, 2:3])
        act(mk, st[:, 3:4], st[:, 2:3], AF.Ln, scale=1.0 / D, bias=LN_EPS)
        act(mk, st[:, 4:5], st[:, 3:4], AF.Exp, scale=-0.5)
        ts(mk, "dve", xnt32[:], xci, st[:, 4:5], ALU.mult)
        if layer == 1 and io.get("xn1t") is not None:
            mk.dma("sp", io["xn1t"][i * 128:(i + 1) * 128, :], xnt32[:],
                   sem_key=f"xn1t{i % 2}", tag="xn1t")
        for hb in range(2):
            for b2 in range(4):
                b = hb * 4 + b2
                transpose(mk, ps_tr[:, 128 * b2:128 * (b2 + 1)],
                          xnt32[:, 128 * b:128 * (b + 1)], io["identf_sb"][:])
            for b2 in range(4):
                b = hb * 4 + b2
                copy(mk, "dve" if b % 2 == 0 else "act", xnf[:, b, i * 128:(i + 1) * 128],
                     ps_tr[:, 128 * b2:128 * (b2 + 1)])
        if layer == 0:
            for c in range(8):
                mm(mk, ps_r, xnf[:, c, i * 128:(i + 1) * 128], wr[:, c, :],
                   start=(c == 0), stop=(c == 7))
            reduce(mk, st[:, 5:6], ps_r, ALU.max)
            ts(mk, "dve", st[:, 6:7], st[:, 5:6], -1.0, ALU.mult)
            act(mk, lg[:], ps_r, AF.Exp, bias=st[:, 6:7])
            reduce(mk, st[:, 7:8], lg[:], ALU.add)
            recip(mk, st[:, 7:8], st[:, 7:8])
            tt(mk, "dve", eq[:], ps_r, st[:, 5:6].broadcast_to((128, E0)), ALU.is_equal)
            scan(mk, cum[:], eq[:], zer8[:], 0.0, ALU.add, ALU.add)
            ts(mk, "dve", cum[:], cum[:], 1.0, ALU.is_equal)
            tt(mk, "dve", eq[:], eq[:], cum[:], ALU.mult)
            ts(mk, "dve", msk[:, i, :], eq[:], st[:, 7:8], ALU.mult)

    mk._xnf_sb = xnf
    if "xnf" in io and io["xnf"] is not None:
        for b in range(8):
            mk.dma("sp", io["xnf"][128 * b:128 * (b + 1), :], xnf[:, b, :],
                   sem_key=f"xnf{sfx}", tag="xnf_store")
    if layer == 0:
        mk.dma("sp", io["msk"].rearrange("(i p) e -> p i e", p=128), msk[:], tag="msk_store")


def build_phase_a(T=4096):
    nc = bass.Bass("TRN2", target_bir_lowering=False)
    io = {}
    mk = Mk(nc)
    io["x"] = nc.dram_tensor("x", [T, D], F32, kind="ExternalInput").ap()
    io["w_router"] = nc.dram_tensor("w_router", [D, E0], F32, kind="ExternalInput").ap()
    io["xnf"] = nc.dram_tensor("xnf", [D, T], BF16, kind="ExternalOutput").ap()
    io["msk"] = nc.dram_tensor("msk", [T, E0], F32, kind="ExternalOutput").ap()
    _mk_shared(mk, nc, io, T)
    ph_ln0(mk, io, T, layer=0)
    mk.emit_program()
    return nc



class Arena:
    def __init__(self, mk, name, n_elems, dtype):
        self.t = mk.sbuf(name, [128, n_elems], dtype)
        self.n = n_elems
        self.off = 0

    def reset(self):
        self.off = 0

    def alloc(self, *shape):
        n = 1
        for s in shape:
            n *= s
        assert self.off + n <= self.n, f"arena overflow: {self.off}+{n} > {self.n}"
        v = self.t[:, self.off:self.off + n]
        self.off += n
        if len(shape) > 1:
            dims = " ".join(f"d{i}" for i in range(len(shape)))
            v = v.rearrange(f"p ({dims}) -> p {dims}",
                            **{f"d{i}": s for i, s in enumerate(shape)})
        return v


def ph_l0proj(mk, io, T):
    ab, af = io["ab"], io["af"]
    xnf = io["xnf_sb"]
    wbuf = [ab.alloc(8, 1024) for _ in range(2)]
    bq = mk.sbuf("b_qkv", [128, 3, 8], F32)
    for j, nm in enumerate(("bq", "bk", "bv")):
        mk.dma("sp", bq[:, j, :], io[nm], tag="bias")
    onesb = io["onesb_sb"]
    bvrow = io["rowb_sb"]
    mk.dma("pool", bvrow[:], io["bv_row"], tag="bvrow")
    ps = io["PS_A"]
    ps2 = io["PS_B"]
    qstage = ab.alloc(2, 512)

    for j, (wname, out_h) in enumerate((("wq", "qf"), ("wk", "kf"))):
        wb = wbuf[j % 2]
        mk.dma("pool", wb, io[wname], tag=f"w_{wname}")
        for c in range(8):
            for tj in range(T // 512):
                p = ps[:, 0:512] if (c * (T // 512) + tj) % 2 == 0 else ps2[:, 0:512]
                for dc in range(8):
                    mm(mk, p, wb[:, dc, c * 128:(c + 1) * 128],
                       xnf[:, dc, tj * 512:(tj + 1) * 512], start=(dc == 0), stop=(dc == 7))
                stg = qstage[:, tj % 2, :]
                if tj % 2 == 0:
                    ts(mk, "dve", stg, p, bq[:, j, c:c + 1], ALU.add)
                else:
                    act(mk, stg, p, AF.Identity, bias=bq[:, j, c:c + 1])
                mk.dma("sp", io[out_h][c * 128:(c + 1) * 128, tj * 512:(tj + 1) * 512], stg,
                       sem_key=f"{out_h}{tj % 2}", tag=f"{out_h}_store")

    wb = wbuf[0]
    mk.dma("pool", wb, io["wv"], tag="w_wv")
    vstage = ab.alloc(2, 1040)
    for ti in range(T // 128):
        p = ps[:, 0:512] if ti % 2 == 0 else ps2[:, 0:512]
        stg = vstage[:, ti % 2, :].rearrange("p (h d) -> p h d", d=65)
        for o in range(2):
            for dc in range(8):
                mm(mk, p, xnf[:, dc, ti * 128:(ti + 1) * 128], wb[:, dc, o * 512:(o + 1) * 512],
                   start=(dc == 0), stop=False)
            mm(mk, p, onesb[:], bvrow[:, o * 512:(o + 1) * 512], start=False, stop=True)
            copy(mk, "dve" if ti % 2 == 0 else "act",
                 stg[:, o * 8:(o + 1) * 8, 0:64],
                 p.rearrange("p (h d) -> p h d", d=64))
        memset(mk, "pool", stg[:, :, 64:65], 1.0)
        mk.dma("sp", io["v0"].rearrange("h (i p) c -> p i h c", p=128)[:, ti, :, :],
               vstage[:, ti % 2, :].rearrange("p (h c) -> p h c", c=130),
               sem_key=f"v0{ti % 2}", tag="v0_store")


def ph_l0attn(mk, io, T):
    ab, af = io["ab"], io["af"]
    qp = [ab.alloc(T) for _ in range(2)]
    kp = [ab.alloc(T) for _ in range(2)]
    vp = [ab.alloc(T // 128, 130) for _ in range(2)]
    pt = [ab.alloc(4, 512) for _ in range(2)]
    stage = [ab.alloc(512) for _ in range(2)]
    rden = af.alloc(512)
    rdb = af.alloc(2, 512)
    onesf = io["onesf_sb"]
    PS_A, PS_B = io["PS_A"], io["PS_B"]
    PS_O = (io["PS_C"][0:65, :], io["PS_D"][0:65, :])
    PS_E = io["PS_E"]
    nq = T // 512
    ubb = io["ubb_sb"]

    for hp in range(8):
        b = hp % 2
        mk.dma("sp", qp[b], io["qf"][128 * hp:128 * (hp + 1), :], sem_key=f"qp{b}", tag="qload")
        mk.dma("sp", kp[b], io["kf"][128 * hp:128 * (hp + 1), :], sem_key=f"kp{b}", tag="kload")
        mk.dma("pool", vp[b], io["v0"].rearrange("h (i p) c -> p i h c", p=128)[:, :, hp, :],
               sem_key=f"vp{b}", tag="vload")
        for qb in range(nq):
            nk = 4 * (qb + 1)
            qA = qp[b][0:64, qb * 512:(qb + 1) * 512]
            qB = qp[b][64:128, qb * 512:(qb + 1) * 512]
            for kb in range(nk):
                pA = PS_A[:, (kb % 2) * 512:(kb % 2 + 1) * 512]
                pB = PS_B[:, (kb % 2) * 512:(kb % 2 + 1) * 512]
                mm(mk, pA, kp[b][0:64, kb * 128:(kb + 1) * 128], qA, tile_position=(0, 0))
                mm(mk, pB, kp[b][64:128, kb * 128:(kb + 1) * 128], qB, tile_position=(64, 0))
                if kb % 2 == 1:
                    for h, ps_ in ((0, PS_A), (1, PS_B)):
                        r = (kb - 1) % 4
                        act(mk, pt[h][:, r:r + 2, :].rearrange("p a t -> p (a t)"),
                            ps_[:], AF.Exp, scale=0.125, bias=-SOFT_M)
                        for kk in (kb - 1, kb):
                            j = kk - 4 * qb
                            if 0 <= j < 4:
                                sl_ = pt[h][:, kk % 4, :]
                                if j > 0:
                                    memset(mk, "dve", sl_[:, 0:128 * j], 0.0, tag="diagz")
                                tt(mk, "dve", sl_[:, 128 * j:128 * (j + 1)],
                                   sl_[:, 128 * j:128 * (j + 1)], ubb[:],
                                   ALU.mult, tag="diagmask")
                        for kk in (kb - 1, kb):
                            mm(mk, PS_O[h], vp[b][:, kk, 65 * h:65 * h + 65],
                               pt[h][:, kk % 4, :], start=(kk == 0), stop=(kk == nk - 1))
            for h in range(2):
                psO = PS_O[h]
                recip(mk, rden[64:65, :], psO[64:65, :])
                mm(mk, PS_E[0:65, :], onesf[64:65, 0:65], rden[64:65, :],
                   tile_position=(64, 0))
                copy(mk, "act", rdb[0:65, h, :], PS_E[0:65, :])
                tt(mk, "dve", stage[h][0:65, :], psO[0:65, :], rdb[0:65, h, :], ALU.mult)
                mk.dma("sp", io["attnf"][64 * (2 * hp + h):64 * (2 * hp + h + 1),
                                         qb * 512:(qb + 1) * 512],
                       stage[h][0:64, :], sem_key=f"at{h}", tag="attn_store")


def ph_l0moe(mk, io, T):
    ab, af = io["ab"], io["af"]
    nt = T // 256
    w1a = ab.alloc(8, 2048)
    w1b = ab.alloc(8, 2048)
    w2e = ab.alloc(8, 4, 1024)
    _x2 = ab.alloc(8, 256)
    xnt2 = [_x2, _x2]
    h_sb = [ab.alloc(4, 256) for _ in range(2)]
    ystage = [af.alloc(2, 1024) for _ in range(2)]
    b1c = mk.sbuf("moe_b1", [128, 8, 32], F32)
    b2r = io["rowb_sb"].rearrange("a (u d) -> a u d", u=1)
    onesb = io["onesb_sb"]
    if "_msk_sb" in io:
        mskb = io["_msk_sb"]
    else:
        mskb = mk.sbuf("moe_msk", [128, T // 128, 8], F32)
        mk.dma("sp", mskb[:], io["msk"].rearrange("(i p) e -> p i e", p=128), tag="mskload")
    mk.dma("sp", b1c[:], io["b1c"], tag="b1load")
    PS_Y = (io["PS_A"], io["PS_B"])
    PS_H = (io["PS_C"], io["PS_E"])

    for e in range(8):
        mk.dma("pool", b2r[:, 0, :], io["b2r"][e:e + 1, :], sem_key="b2r", tag="b2load")
        mk.dma("pool", w1a, io["w1h"][e, :, :, 0:2048], sem_key="w1a", tag="w1a")
        mk.dma("pool", w1b, io["w1h"][e, :, :, 2048:4096], sem_key="w1b", tag="w1b")
        mk.dma("pool", w2e, io["w2h"][e, :, :, :, :], sem_key="w2e", tag="w2e")
        for i in range(nt):
            t0 = i * 256
            xnf = xnt2[i % 2]
            mk.dma("sp", xnf,
                   io["xnf"].rearrange("(c p) t -> p c t", p=128)[:, :, t0:t0 + 256],
                   sem_key=f"xnt2{i % 2}", tag="xnf_tile")
            for fc in range(8):
                w1h = w1a if fc < 4 else w1b
                fo = (fc % 4) * 512
                hb = h_sb[fc % 2]
                for half in range(2):
                    ph = PS_H[half]
                    for fs2 in range(2):
                        fs = half * 2 + fs2
                        for dc in range(8):
                            mm(mk, ph[:, fs2 * 256:(fs2 + 1) * 256],
                               w1h[:, dc, fo + fs * 128:fo + (fs + 1) * 128],
                               xnf[:, dc, :], start=(dc == 0), stop=(dc == 7))
                for fs in range(4):
                    half, fs2 = fs // 2, fs % 2
                    ph = PS_H[half]
                    col = b1c[:, e, fc * 4 + fs:fc * 4 + fs + 1]
                    hv = hb[:, fs, :]
                    if fs % 2 == 0:
                        ts(mk, "dve", hv, ph[:, fs2 * 256:(fs2 + 1) * 256],
                           col, ALU.add, 0.0, ALU.max, tag="relu")
                    else:
                        act(mk, hv, ph[:, fs2 * 256:(fs2 + 1) * 256],
                            AF.Relu, bias=col, tag="relu")
                for fs in range(4):
                    for tsub in range(2):
                        for n in range(2):
                            mm(mk, PS_Y[tsub][:, n * 512:(n + 1) * 512],
                               hb[:, fs, tsub * 128:(tsub + 1) * 128],
                               w2e[:, fc, fs, n * 512:(n + 1) * 512],
                               start=(fc == 0 and fs == 0), stop=False)
            for tsub in range(2):
                for n in range(2):
                    mm(mk, PS_Y[tsub][:, n * 512:(n + 1) * 512], onesb[:],
                       b2r[:, 0, n * 512:(n + 1) * 512], start=False, stop=True)
                ts(mk, "dve", ystage[i % 2][:, tsub, :], PS_Y[tsub][:],
                   mskb[:, 2 * i + tsub, e:e + 1], ALU.mult, tag="ymask")
            kw = {} if e == 0 else {"accum_op": ALU.add}
            mk.dma("pool", io["g32"].rearrange("(i p) d -> p i d", p=128)[:, 2 * i:2 * i + 2, :],
                   ystage[i % 2][:], sem_key=f"g{i % 3}", tag="gaccum", **kw)


def _mk_shared(mk, nc, io, T, ab_elems=71680, af_elems=14336):
    mk.init_consts(ALL_CONSTS)
    io["ab"] = Arena(mk, "pool_bf16", ab_elems, BF16)
    io["af"] = Arena(mk, "pool_f32", af_elems, F32)
    io["cident"] = nc.dram_tensor("cident", [128, 128], F32, kind="ExternalInput").ap()
    identf = mk.sbuf("identf_sb", [128, 128], F32)
    mk.dma("sp", identf[:], io["cident"], tag="identf")
    io["identf_sb"] = identf
    onesf = mk.sbuf("onesf_sb", [128, 128], F32)
    memset(mk, "pool", onesf[:], 1.0)
    io["onesf_sb"] = onesf
    onesb = mk.sbuf("onesb_sb", [1, 128], BF16)
    memset(mk, "pool", onesb[:], 1.0)
    io["onesb_sb"] = onesb
    io["ubc"] = nc.dram_tensor("ubc", [128, 128], F32, kind="ExternalInput").ap()
    ubf = mk.sbuf("ub_sb", [128, 128], F32)
    mk.dma("sp", ubf[:], io["ubc"], tag="ubf")
    io["ub_sb"] = ubf
    ubb = mk.sbuf("ubb_sb", [128, 128], BF16)
    mk.dma("pool", ubb[:], io["ubc"], tag="ubb")
    io["ubb_sb"] = ubb
    io["rowb_sb"] = mk.sbuf("rowb_sb", [1, D], BF16)
    io["PS_A"] = mk.psum("PS_A", [128, 1024], F32)
    io["PS_B"] = mk.psum("PS_B", [128, 1024], F32)
    io["PS_C"] = mk.psum("PS_C", [128, 512], F32)
    io["PS_D"] = mk.psum("PS_D", [128, 512], F32)
    io["PS_E"] = mk.psum("PS_E", [128, 512], F32)
    io["PS_F"] = mk.psum("PS_F", [128, 512], F32)


def build_l0(T=4096, phases=("a", "b", "c"), ab_elems=71680, af_elems=6144):
    nc = bass.Bass("TRN2", target_bir_lowering=False)
    io = {}
    EI, EO = "ExternalInput", "ExternalOutput"
    mk = Mk(nc)
    _mk_shared(mk, nc, io, T, ab_elems=ab_elems, af_elems=af_elems)
    if "a" in phases:
        io["x"] = nc.dram_tensor("x", [T, D], F32, kind=EI).ap()
        io["w_router"] = nc.dram_tensor("w_router", [D, E0], F32, kind=EI).ap()
        io["xnf"] = nc.dram_tensor("xnf", [D, T], BF16, kind=EO).ap()
        io["msk"] = nc.dram_tensor("msk", [T, E0], F32, kind=EO).ap()
    else:
        io["xnf"] = nc.dram_tensor("xnf", [D, T], BF16, kind=EI).ap()
    if "b" in phases:
        for nm in ("wq", "wk", "wv"):
            io[nm] = nc.dram_tensor(nm, [128, 8, 1024], F32, kind=EI).ap()
        for nm in ("bq", "bk", "bv"):
            io[nm] = nc.dram_tensor(nm, [128, 8], F32, kind=EI).ap()
        io["bv_row"] = nc.dram_tensor("bv_row", [1, D], F32, kind=EI).ap()
        k_qkv = EO if "c" not in phases else EO
        io["qf"] = nc.dram_tensor("qf", [D, T], BF16, kind=k_qkv).ap()
        io["kf"] = nc.dram_tensor("kf", [D, T], BF16, kind=k_qkv).ap()
        io["v0"] = nc.dram_tensor("v0", [8, T, 130], BF16, kind=k_qkv).ap()
    else:
        io["qf"] = nc.dram_tensor("qf", [D, T], BF16, kind=EI).ap()
        io["kf"] = nc.dram_tensor("kf", [D, T], BF16, kind=EI).ap()
        io["v0"] = nc.dram_tensor("v0", [8, T, 130], BF16, kind=EI).ap()
    if "c" in phases:
        io["attnf"] = nc.dram_tensor("attnf", [D, T], BF16, kind=EO).ap()
    if "d" in phases:
        io["w1h"] = nc.dram_tensor("w1h", [8, 128, 8, 4096], F32, kind=EI).ap()
        io["w2h"] = nc.dram_tensor("w2h", [8, 128, 8, 4, 1024], F32, kind=EI).ap()
        io["b1c"] = nc.dram_tensor("b1c", [128, 8, 32], F32, kind=EI).ap()
        io["b2r"] = nc.dram_tensor("b2r", [8, 1024], F32, kind=EI).ap()
        if "a" not in phases:
            io["msk"] = nc.dram_tensor("msk", [T, E0], F32, kind=EI).ap()
        io["g32"] = nc.dram_tensor("g32", [T, D], F32, kind=EO).ap()

    if "a" in phases:
        ph_ln0(mk, io, T, layer=0)
        io["xnf_sb"] = mk._xnf_sb
    if "b" in phases:
        if "a" not in phases:
            xnf_sb = io["ab"].alloc(8, T)
            for c in range(8):
                mk.dma("sp", xnf_sb[:, c, :], io["xnf"][128 * c:128 * (c + 1), :],
                       sem_key="xnfld", tag="xnf_load")
            io["xnf_sb"] = xnf_sb
        ph_l0proj(mk, io, T)
    if "c" in phases:
        io["ab"].reset()
        io["af"].reset()
        ph_l0attn(mk, io, T)
    if "d" in phases:
        io["ab"].reset()
        io["af"].reset()
        ph_l0moe(mk, io, T)
    mk.emit_program()
    return nc


def ph_l0post(mk, io, T):
    ab, af = io["ab"], io["af"]
    at = [ab.alloc(8, 256) for _ in range(2)]
    wm = ab.alloc(8, 1024)
    gx = [af.alloc(2, 1024) for _ in range(2)]
    xx = [af.alloc(2, 1024) for _ in range(2)]
    x1s = [af.alloc(2, 1024) for _ in range(2)]
    bmrow = io["rowb_sb"]
    onesb = io["onesb_sb"]
    mk.dma("pool", wm, io["wm"], tag="wm")
    mk.dma("pool", bmrow[:], io["bm_row"], tag="bm")
    PS_Y = (io["PS_A"], io["PS_B"])
    nt = T // 256
    for i in range(nt):
        t0 = i * 256
        b = i % 2
        mk.dma("sp", at[b], io["attnf"].rearrange("(c p) t -> p c t", p=128)[:, :, t0:t0 + 256],
               sem_key=f"at{b}", tag="atload")
        mk.dma("sp", gx[b], io["g32"].rearrange("(i p) d -> p i d", p=128)[:, 2 * i:2 * i + 2, :],
               sem_key=f"gx{b}", tag="gload")
        mk.dma("sp", xx[b], io["x"].rearrange("(i p) d -> p i d", p=128)[:, 2 * i:2 * i + 2, :],
               sem_key=f"xx{b}", tag="xload")
        for tsub in range(2):
            for n in range(2):
                for dc in range(8):
                    mm(mk, PS_Y[tsub][:, n * 512:(n + 1) * 512],
                       at[b][:, dc, tsub * 128:(tsub + 1) * 128],
                       wm[:, dc, n * 512:(n + 1) * 512], start=(dc == 0), stop=False)
                mm(mk, PS_Y[tsub][:, n * 512:(n + 1) * 512], onesb[:],
                   bmrow[:, n * 512:(n + 1) * 512], start=False, stop=True)
            act(mk, gx[b][:, tsub, :], gx[b][:, tsub, :], AF.Silu)
            for n in range(2):
                sl_ = slice(n * 512, (n + 1) * 512)
                tt(mk, "dve", x1s[b][:, tsub, sl_], PS_Y[tsub][:, sl_],
                   gx[b][:, tsub, sl_], ALU.mult)
            tt(mk, "dve", x1s[b][:, tsub, :], x1s[b][:, tsub, :], xx[b][:, tsub, :], ALU.add)
        mk.dma("sp", io["x1"].rearrange("(i p) d -> p i d", p=128)[:, 2 * i:2 * i + 2, :],
               x1s[b], sem_key=f"x1{b}", tag="x1store")


def ph_l1prep(mk, io, T):
    ab, af = io["ab"], io["af"]
    xnf = io["xnf_sb"]
    wbuf = [ab.alloc(8, 1024) for _ in range(2)]
    permT = mk.sbuf("permT_sb", [128, 128], F32)
    mk.dma("sp", permT[:], io["permT"], tag="permT")
    cosx = [af.alloc(512) for _ in range(2)]
    sinx = [af.alloc(512) for _ in range(2)]
    raw32 = [af.alloc(512) for _ in range(2)]
    rot = af.alloc(512)
    rot2 = af.alloc(512)
    phs = [af.alloc(512) for _ in range(2)]
    PS_A, PS_B, PS_E, PS_F = io["PS_A"], io["PS_B"], io["PS_E"], io["PS_F"]

    for j, (wname, outh) in enumerate((("wq1", "phq"), ("wk1", "phk"))):
        wb = wbuf[j % 2]
        mk.dma("pool", wb, io[wname], tag=wname)
        for tj in range(T // 512):
            ts_ = tj * 512
            u2 = tj % 2
            mk.dma("sp", cosx[u2], io["cosx"][:, ts_:ts_ + 512],
                   sem_key=f"cosx{u2}", tag="cosx")
            mk.dma("sp", sinx[u2], io["sinx"][:, ts_:ts_ + 512],
                   sem_key=f"sinx{u2}", tag="sinx")
            for c in range(8):
                u = (c + tj) % 2
                p = PS_A[:, (u % 2) * 512:(u % 2) * 512 + 512]
                for dc in range(8):
                    mm(mk, p, wb[:, dc, c * 128:(c + 1) * 128],
                       xnf[:, dc, tj * 512:(tj + 1) * 512], start=(dc == 0), stop=(dc == 7))
                rb = raw32[u]
                copy(mk, "act", rb, p)
                mm(mk, PS_B[:, 0:512], permT[:], rb)
                tt(mk, "dve", rot, p, cosx[u2], ALU.mult)
                tt(mk, "dve", rot2, PS_B[:, 0:512], sinx[u2], ALU.mult)
                tt(mk, "dve", rot, rot, rot2, ALU.add)
                ts(mk, "dve", rot2, rot, 0.0, ALU.min)
                act(mk, rot2, rot2, AF.Exp)
                ts(mk, "dve", rot, rot, 0.0, ALU.max)
                tt(mk, "dve", phs[u], rot, rot2, ALU.add)
                mk.dma("sp", io[outh][c * 128:(c + 1) * 128, ts_:ts_ + 512], phs[u],
                       sem_key=f"{outh}{u}", tag=outh)

    wb = wbuf[0]
    mk.dma("pool", wb, io["wv1"], tag="wv1")
    vstage = af.alloc(2, 1056)
    for ti in range(T // 128):
        b = ti % 2
        p = PS_A[:, 0:512] if b == 0 else PS_A[:, 512:1024]
        stg = vstage[:, b, :].rearrange("p (h d) -> p h d", d=33)
        for o in range(2):
            for dc in range(8):
                mm(mk, p, xnf[:, dc, ti * 128:(ti + 1) * 128], wb[:, dc, o * 512:(o + 1) * 512],
                   start=(dc == 0), stop=(dc == 7))
            copy(mk, "dve" if b == 0 else "act",
                 stg[:, o * 16:(o + 1) * 16, 0:32],
                 p.rearrange("p (h d) -> p h d", d=32))
        memset(mk, "pool", stg[:, :, 32:33], 1.0)
        mk.dma("sp", io["v1"].rearrange("g (i p) c -> p i g c", p=128)[:, ti, :, :],
               vstage[:, b, :].rearrange("p (g c) -> p g c", c=132),
               sem_key=f"v1{b}", tag="v1store")

    wbm = ab.alloc(2, 8, 64)
    mk.dma("pool", wbm, io["wbm"], tag="wbm")
    bcol = mk.sbuf("l1_bcol", [64, 2], F32)
    mk.dma("sp", bcol[:], io["bbm_col"], tag="bbm")
    e2 = mk.sbuf("e2_sb", [64, 64], F32)
    mk.dma("sp", e2[:], io["e2"], tag="e2")
    beta = [af.alloc(512) for _ in range(2)]
    mixe = [af.alloc(512) for _ in range(2)]
    Pst = [af.alloc(512) for _ in range(2)]
    zer = af.alloc(128)
    memset(mk, "pool", zer, 0.0)
    for tj in range(T // 512):
        sl = slice(tj * 512, (tj + 1) * 512)
        u = tj % 2
        bt, mx, Ps = beta[u][0:64, :], mixe[u][0:64, :], Pst[u][0:64, :]
        for a in range(2):
            p = PS_E[0:64, :]
            for dc in range(8):
                mm(mk, p, wbm[:, a, dc, :], xnf[:, dc, sl],
                   start=(dc == 0), stop=(dc == 7))
            if a == 0:
                act(mk, bt, p, AF.Sigmoid, bias=bcol[:, 0:1])
                ts(mk, "dve", bt, bt, BMIN, ALU.max, BMAX, ALU.min)
            else:
                act(mk, mx, p, AF.Exp, bias=bcol[:, 1:2])
        mm(mk, PS_E[0:64, :], e2[:], mx)
        recip(mk, rot[0:64, :], PS_E[0:64, :])
        tt(mk, "dve", mx, mx, rot[0:64, :], ALU.mult)
        for cc in range(4):
            sc = slice(cc * 128, (cc + 1) * 128)
            scan(mk, Ps[:, sc], bt[:, sc], zer[0:64, 0:128], 1.0, ALU.mult, ALU.add)
        mk.dma("sp", io["mixh"][:, sl], mx, sem_key=f"mixh{u}", tag="mixh")
        mk.dma("sp", io["ph_"][:, sl], Ps, sem_key=f"ph{u}", tag="ph")


def ph_l1gla(mk, io, T):
    ab, af = io["ab"], io["af"]
    C = 128
    ncu = T // C
    onesf = io["onesf_sb"]
    PS_A, PS_B, PS_C, PS_D, PS_E, PS_F = (io["PS_A"], io["PS_B"], io["PS_C"],
                                          io["PS_D"], io["PS_E"], io["PS_F"])
    gmask = io["ub_sb"]
    ub = io["ub_sb"]
    bo32 = mk.sbuf("bo_sb", [128, 128], F32)
    mk.dma("sp", bo32[:], io["bo"], tag="bo")
    Pb = [af.alloc(128) for _ in range(2)]
    mxb = [af.alloc(128) for _ in range(2)]
    if io.get("dbg") is not None:
        io["_dbgs"] = [af.alloc(128) for _ in range(8)]
        io["_dbgs2"] = [af.alloc(128) for _ in range(5)]
    wout = ab.alloc(8, 1024)
    mk.dma("pool", wout, io["wout"], tag="wout")
    onesb = io["onesb_sb"]
    lnrow2 = mk.sbuf("lnrow2_sb", [1, 3, D], BF16)
    lnbc = af.alloc(2, 1024)
    for k in range(2):
        mk.dma("sp", lnbc[:, k, :],
               io["lnrows"][2 * k:2 * k + 1, :].broadcast_to((128, D)),
               sem_key="lnrow", tag="lnbc")
    mk.dma("pool", lnrow2[:, 0, :], io["lnrows"][1:2, :], sem_key="lnrow", tag="lnb")
    mk.dma("pool", lnrow2[:, 1, :], io["lnrows"][3:4, :], sem_key="lnrow", tag="fb")
    mk.dma("pool", lnrow2[:, 2, :], io["bout_row"], sem_key="lnrow", tag="bout")

    state = mk.sbuf("gla_state", [128, 8, 2, 33], F32)
    memset(mk, "dve", state[:], 0.0)
    io.setdefault("_st8_shared", None)

    _pq = af.alloc(8, C)
    _pk = af.alloc(8, C)
    _v1 = af.alloc(8, 132)
    _x1 = af.alloc(1024)
    _xn = af.alloc(1024)
    phq_c = [_pq, _pq]
    phk_c = [_pk, _pk]
    v1_c = [_v1, _v1]
    x1t = [_x1, _x1]
    xnt1 = [_xn, _xn]
    qt32_ = [af.alloc(C) for _ in range(2)]
    t1b32_ = [af.alloc(C) for _ in range(2)]
    zq32_ = [af.alloc(C) for _ in range(2)]
    kt32_ = [af.alloc(C) for _ in range(2)]
    _kT = af.alloc(C)
    ktT32_ = [_kT, _kT]
    A_sb_ = [af.alloc(4, C) for _ in range(2)]
    y_sb = ab.alloc(8, C)
    _wf = af.alloc(C)
    w_f_ = [_wf, _wf]
    ytmp = af.alloc(C)
    pend = af.alloc(2)
    st8 = io.get("_st8_shared")
    if st8 is None:
        st8 = mk.sbuf("gla_st8", [128, 8], F32)
    zst = af.alloc(1024)
    g1st = af.alloc(1024)
    _ou = af.alloc(1024)
    outst = [_ou, _ou]

    G4 = (PS_A[:, 0:128], PS_A[:, 512:640], PS_B[:, 0:128], PS_B[:, 512:640])

    for c in range(ncu):
        b = c % 2
        t0 = c * C
        for g in range(8):
            mk.dma("sp", phq_c[b][:, g, :], io["phq"][128 * g:128 * (g + 1), t0:t0 + C],
                   sem_key=f"phq{b}", tag="phq")
            mk.dma("sp", phk_c[b][:, g, :], io["phk"][128 * g:128 * (g + 1), t0:t0 + C],
                   sem_key=f"phk{b}", tag="phk")
        mk.dma("sp", v1_c[b], io["v1"].rearrange("g (i p) c -> p i g c", p=128)[:, c, :, :],
               sem_key=f"v1c{b}", tag="v1c")
        mk.dma("sp", x1t[b], io["x1"][t0:t0 + C, :], sem_key=f"x1t{b}", tag="x1t")
        mk.dma("sp", xnt1[b], io["xn1t"][t0:t0 + C, :], sem_key=f"xnt1{b}", tag="xnt1")

        for g in range(8):
            for m in range(2):
                ug = (g * 2 + m) % 2
                qt32, t1b32, zq32 = qt32_[ug], t1b32_[ug], zq32_[ug]
                kt32, ktT32, A_sb, w_f = kt32_[ug], ktT32_[ug], A_sb_[ug], w_f_[ug]
                ytmp2 = zq32
                r0 = m * 32 + g * 4
                mk.dma("sp", Pb[ug],
                       io["ph_"][r0:r0 + 4, t0:t0 + C].rearrange("r (u t) -> r u t", u=1)
                       .broadcast_to((4, 32, C)),
                       sem_key=f"pb{ug}", tag="pb")
                mk.dma("sp", mxb[ug],
                       io["mixh"][r0:r0 + 4, t0:t0 + C].rearrange("r (u t) -> r u t", u=1)
                       .broadcast_to((4, 32, C)),
                       sem_key=f"mxb{ug}", tag="mxb")
                copy(mk, "dve", pend[:, m:m + 1], Pb[ug][:, 127:128])
                tt(mk, "dve", qt32, phq_c[b][:, g, :], Pb[ug], ALU.mult)
                recip(mk, kt32, Pb[ug])
                tt(mk, "dve", kt32, phk_c[b][:, g, :], kt32, ALU.mult)
                transpose(mk, PS_F[:, 0:128], kt32, io["identf_sb"][:])
                copy(mk, "act", ktT32, PS_F[:, 0:128])
                for j in range(4):
                    mm(mk, G4[j], kt32[32 * j:32 * (j + 1), :], qt32[32 * j:32 * (j + 1), :],
                       tile_position=(32 * j, 0))
                for jj, ps_ in ((0, PS_A), (2, PS_B)):
                    for u in range(2):
                        tt(mk, "dve", A_sb[:, jj + u, :], ps_[:, u * 512:u * 512 + 128],
                           gmask[:], ALU.mult, tag="amask")
                for j in range(4):
                    mm(mk, PS_C[32 * j:32 * (j + 1), 0:128],
                       state[32 * j:32 * (j + 1), g, m, 0:32],
                       qt32[32 * j:32 * (j + 1), :],
                       start=True, stop=False, tile_position=(32 * j, 32 * j), skip=True)
                for j in range(4):
                    mm(mk, PS_C[32 * j:32 * (j + 1), 0:128],
                       v1_c[b][:, g, 33 * j:33 * j + 32],
                       A_sb[:, j, :], start=False, stop=True, tile_position=(0, 32 * j), skip=True)
                for j in range(4):
                    mm(mk, PS_D[32 * j:32 * (j + 1), 0:128],
                       ktT32[:, 32 * j:32 * (j + 1)], ub[:],
                       start=True, stop=True, tile_position=(0, 32 * j), skip=True)
                act(mk, t1b32, PS_D[:, 0:128], AF.Identity, bias=state[:, g, m, 32:33])
                tt(mk, "dve", zq32, qt32, t1b32, ALU.mult)
                mm(mk, PS_E[:, 128:256], bo32[:], zq32)
                ts(mk, "dve", w_f, PS_E[:, 128:256], GLA_EPS, ALU.add)
                recip(mk, w_f, w_f)
                tt(mk, "dve", w_f, w_f, mxb[ug], ALU.mult)
                if m == 0:
                    tt(mk, "dve", ytmp, PS_C[:, 0:128], w_f, ALU.mult)
                else:
                    tt(mk, "dve", ytmp2, PS_C[:, 0:128], w_f, ALU.mult)
                    tt(mk, "dve", y_sb[:, g, :], ytmp, ytmp2, ALU.add)
                if io.get("dbg") is not None and c == 0 and g == 7 and m == 0:
                    dm = io["dbg"]
                    for k_, t_ in ((0, qt), (1, kt), (2, A_sb[:, 0, :]), (3, A_sb[:, 1, :]),
                                   (4, A_sb[:, 2, :]), (5, A_sb[:, 3, :]), (6, w_f), (7, Pb[ug])):
                        copy(mk, "dve", io["_dbgs"][k_], t_)
                        mk.dma("sp", dm[k_, :, :], io["_dbgs"][k_], sem_key="dbg", tag="dbg")
                    copy(mk, "dve", io["_dbgs"][0], PS_C[:, 0:128])
                    mk.dma("sp", dm[13, :, :], io["_dbgs"][0], sem_key="dbg", tag="dbg")
                for j in range(4):
                    mm(mk, PS_E[32 * j:32 * (j + 1), 384:417],
                       ktT32[:, 32 * j:32 * (j + 1)],
                       v1_c[b][:, g, 33 * j:33 * j + 33],
                       start=True, stop=True, tile_position=(0, 32 * j), skip=True)
                tt(mk, "dve", state[:, g, m, :], state[:, g, m, :], PS_E[:, 384:417], ALU.add)
                ts(mk, "dve", state[:, g, m, :], state[:, g, m, :], pend[:, m:m + 1], ALU.mult)
        for n in range(2):
            for g in range(8):
                mm(mk, PS_A[:, n * 512:(n + 1) * 512], y_sb[:, g, :],
                   wout[:, g, n * 512:(n + 1) * 512], start=(g == 0), stop=False)
            mm(mk, PS_A[:, n * 512:(n + 1) * 512], onesb[0:1, :],
               lnrow2[:, 2, n * 512:(n + 1) * 512], start=False, stop=True)
        for n in range(2):
            tt(mk, "dve", zst[:, n * 512:(n + 1) * 512], PS_A[:, n * 512:(n + 1) * 512],
               xnt1[b][:, n * 512:(n + 1) * 512], ALU.add)
        reduce(mk, st8[:, 0:1], zst, ALU.add)
        ts(mk, "dve", st8[:, 1:2], st8[:, 0:1], 1.0 / D, ALU.mult)
        ts(mk, "dve", zst, zst, st8[:, 1:2], ALU.subtract)
        act(mk, g1st, zst, AF.Square, accum_out=st8[:, 2:3])
        act(mk, st8[:, 3:4], st8[:, 2:3], AF.Ln, scale=1.0 / D, bias=LN_EPS)
        act(mk, st8[:, 4:5], st8[:, 3:4], AF.Exp, scale=-0.5)
        ts(mk, "dve", zst, zst, st8[:, 4:5], ALU.mult)
        tt(mk, "dve", zst, zst, lnbc[:, 0, :], ALU.mult)
        for n in range(2):
            mm(mk, PS_B[:, n * 512:(n + 1) * 512], onesb[0:1, :],
               lnrow2[:, 0, n * 512:(n + 1) * 512])
            tt(mk, "dve", zst[:, n * 512:(n + 1) * 512], zst[:, n * 512:(n + 1) * 512],
               PS_B[:, n * 512:(n + 1) * 512], ALU.add)
        reduce(mk, st8[:, 5:6], xnt1[b], ALU.add)
        ts(mk, "dve", st8[:, 5:6], st8[:, 5:6], 1.0 / D, ALU.mult)
        ts(mk, "dve", g1st, xnt1[b], st8[:, 5:6], ALU.subtract)
        act(mk, outst[b], g1st, AF.Square, accum_out=st8[:, 6:7])
        act(mk, st8[:, 6:7], st8[:, 6:7], AF.Ln, scale=1.0 / D, bias=LN_EPS)
        act(mk, st8[:, 7:8], st8[:, 6:7], AF.Exp, scale=-0.5)
        ts(mk, "dve", g1st, g1st, st8[:, 7:8], ALU.mult)
        tt(mk, "dve", g1st, g1st, lnbc[:, 1, :], ALU.mult)
        for n in range(2):
            mm(mk, PS_B[:, n * 512:(n + 1) * 512], onesb[0:1, :],
               lnrow2[:, 1, n * 512:(n + 1) * 512])
            tt(mk, "dve", g1st[:, n * 512:(n + 1) * 512], g1st[:, n * 512:(n + 1) * 512],
               PS_B[:, n * 512:(n + 1) * 512], ALU.add)
        act(mk, g1st, g1st, AF.Silu)
        tt(mk, "dve", outst[b], zst, g1st, ALU.mult)
        tt(mk, "dve", outst[b], outst[b], x1t[b], ALU.add)
        mk.dma("sp", io["out"][t0:t0 + C, :], outst[b], sem_key=f"out{b}", tag="outstore")
        if io.get("dbg") is not None and c == 0:
            for k_, t_ in ((8, y_sb[:, 0, :]), (9, zst[:, 0:128]), (10, g1st[:, 0:128]),
                           (11, outst[b][:, 0:128]), (12, y_sb[:, 7, :])):
                copy(mk, "dve", io["_dbgs"][k_ - 8] if False else io["_dbgs2"][k_ - 8], t_)
                mk.dma("sp", io["dbg"][k_, :, :], io["_dbgs2"][k_ - 8], sem_key="dbg", tag="dbg")


def build_l1(T=4096, phases=("e", "e2", "f", "g"), ab_elems=50176, af_elems=21632, dbg=False):
    nc = bass.Bass("TRN2", target_bir_lowering=False)
    io = {}
    EI, EO = "ExternalInput", "ExternalOutput"
    mk = Mk(nc)
    _mk_shared(mk, nc, io, T, ab_elems=ab_elems, af_elems=af_elems)
    dt_ = nc.dram_tensor
    if "e" in phases:
        io["attnf"] = dt_("attnf", [D, T], BF16, kind=EI).ap()
        io["wm"] = dt_("wm", [128, 8, 1024], F32, kind=EI).ap()
        io["bm_row"] = dt_("bm_row", [1, D], F32, kind=EI).ap()
        io["g32"] = dt_("g32", [T, D], F32, kind=EI).ap()
        io["x"] = dt_("x", [T, D], F32, kind=EI).ap()
        io["x1"] = dt_("x1", [T, D], F32, kind=EO).ap()
    else:
        io["x1"] = dt_("x1", [T, D], F32, kind=EI).ap()
    if "e2" in phases:
        io["xnf"] = dt_("xnf1", [D, T], BF16, kind=EO).ap()
        io["xn1t"] = dt_("xn1t", [T, D], F32, kind=EO).ap()
    else:
        io["xnf"] = dt_("xnf1", [D, T], BF16, kind=EI).ap()
        io["xn1t"] = dt_("xn1t", [T, D], F32, kind=EI).ap()
    if "f" in phases:
        for nm in ("wq1", "wk1", "wv1"):
            io[nm] = dt_(nm, [128, 8, 1024], F32, kind=EI).ap()
        io["wbm"] = dt_("wbm", [128, 2, 8, 64], F32, kind=EI).ap()
        io["bbm_col"] = dt_("bbm_col", [64, 2], F32, kind=EI).ap()
        io["permT"] = dt_("permT", [128, 128], F32, kind=EI).ap()
        io["cosx"] = dt_("cosx", [128, T], F32, kind=EI).ap()
        io["sinx"] = dt_("sinx", [128, T], F32, kind=EI).ap()
        io["e2"] = dt_("e2c", [64, 64], F32, kind=EI).ap()
        io["phq"] = dt_("phq", [D, T], F32, kind=EO).ap()
        io["phk"] = dt_("phk", [D, T], F32, kind=EO).ap()
        io["v1"] = dt_("v1", [8, T, 132], F32, kind=EO).ap()
        io["mixh"] = dt_("mixh", [64, T], F32, kind=EO).ap()
        io["ph_"] = dt_("ph_", [64, T], F32, kind=EO).ap()
    else:
        io["phq"] = dt_("phq", [D, T], F32, kind=EI).ap()
        io["phk"] = dt_("phk", [D, T], F32, kind=EI).ap()
        io["v1"] = dt_("v1", [8, T, 132], F32, kind=EI).ap()
        io["mixh"] = dt_("mixh", [64, T], F32, kind=EI).ap()
        io["ph_"] = dt_("ph_", [64, T], F32, kind=EI).ap()
    if "g" in phases:
        io["bo"] = dt_("boc", [128, 128], F32, kind=EI).ap()
        io["wout"] = dt_("wout", [128, 8, 1024], F32, kind=EI).ap()
        io["bout_row"] = dt_("bout_row", [1, D], F32, kind=EI).ap()
        io["lnrows"] = dt_("lnrows", [4, D], F32, kind=EI).ap()
        io["out"] = dt_("outp", [T, D], F32, kind=EO).ap()
        if dbg:
            io["dbg"] = dt_("dbg", [16, 128, 128], F32, kind=EO).ap()

    if "e" in phases:
        ph_l0post(mk, io, T)
        io["ab"].reset(); io["af"].reset()
    if "e2" in phases:
        ph_ln0(mk, io, T, layer=1)
        io["xnf_sb"] = mk._xnf_sb
    if "f" in phases:
        if "e2" not in phases:
            xnf_sb = io["ab"].alloc(8, T)
            for cc in range(8):
                mk.dma("sp", xnf_sb[:, cc, :], io["xnf"][128 * cc:128 * (cc + 1), :],
                       sem_key="xnf1ld", tag="xnf1_load")
            io["xnf_sb"] = xnf_sb
        ph_l1prep(mk, io, T)
    if "g" in phases:
        io["ab"].reset(); io["af"].reset()
        ph_l1gla(mk, io, T)
    mk.emit_program()
    return nc



def _host_pack(inputs, T):
    f32 = np.float32
    pk = {}
    pack8 = lambda W: np.ascontiguousarray(W.reshape(8, 128, 1024).transpose(1, 0, 2)).astype(f32)
    packcol = lambda b: np.ascontiguousarray(b.reshape(8, 128).T).astype(f32)
    pk["wq"] = pack8(inputs["L0_attn_Wq"])
    pk["wk"] = pack8(inputs["L0_attn_Wk"])
    pk["wv"] = pack8(inputs["L0_attn_Wv"])
    pk["wm"] = pack8(inputs["L0_attn_Wm"])
    pk["bq"] = packcol(inputs["L0_attn_bq"])
    pk["bk"] = packcol(inputs["L0_attn_bk"])
    pk["bv"] = packcol(inputs["L0_attn_bv"])
    pk["bv_row"] = inputs["L0_attn_bv"].reshape(1, -1).astype(f32)
    pk["bm_row"] = inputs["L0_attn_bm"].reshape(1, -1).astype(f32)
    pk["w_router"] = inputs["L0_ffn_w_router"].astype(f32)
    w1, b1 = inputs["L0_ffn_w1"], inputs["L0_ffn_b1"]
    w2, b2 = inputs["L0_ffn_w2"], inputs["L0_ffn_b2"]
    pk["w1h"] = np.ascontiguousarray(w1.reshape(8, 8, 128, 4096).transpose(0, 2, 1, 3)).astype(f32)
    pk["w2h"] = np.ascontiguousarray(w2.reshape(8, 8, 4, 128, 1024).transpose(0, 3, 1, 2, 4)).astype(f32)
    pk["b1c"] = np.ascontiguousarray(b1.reshape(8, 32, 128).transpose(2, 0, 1)).astype(f32)
    pk["b2r"] = b2.astype(f32)
    packT = lambda W: np.ascontiguousarray(W.T.reshape(8, 128, 1024).transpose(1, 0, 2)).astype(f32)
    pk["wq1"] = packT(inputs["L1_attn_Wq"])
    pk["wk1"] = packT(inputs["L1_attn_Wk"])
    pk["wv1"] = packT(inputs["L1_attn_Wv"])
    pk["wout"] = packT(inputs["L1_attn_Wout"])
    perm = np.zeros(64, np.int64)
    for m in range(2):
        for g in range(8):
            for j in range(4):
                perm[m * 32 + g * 4 + j] = (4 * g + j) * 2 + m
    Wb_p = inputs["L1_attn_Wbeta"][perm]
    Wm_p = inputs["L1_attn_Wmix"][perm]
    pk["wbm"] = np.ascontiguousarray(
        np.stack([Wb_p.T.reshape(8, 128, 64).transpose(1, 0, 2),
                  Wm_p.T.reshape(8, 128, 64).transpose(1, 0, 2)], axis=1)).astype(f32)
    pk["bbm_col"] = np.stack([inputs["L1_attn_bbeta"][perm],
                              inputs["L1_attn_bmix"][perm]], 1).astype(f32)
    pk["bout_row"] = inputs["L1_attn_bout"].reshape(1, -1).astype(f32)
    pk["lnrows"] = np.stack([inputs["L1_attn_ln_g"], inputs["L1_attn_ln_b"],
                             inputs["L1_ffn_ln_g"], inputs["L1_ffn_ln_b"]]).astype(f32)
    pk["cident"] = np.eye(128, dtype=f32)
    kk = np.arange(128)[:, None]
    qq = np.arange(512)[None, :]
    dmask = np.stack([(kk + 128 * j <= qq).astype(f32) for j in range(4)])
    pk["dmask"] = np.ascontiguousarray(np.transpose(dmask, (1, 0, 2)))
    Pm = np.zeros((128, 128), f32)
    for p in range(128):
        blk, d_ = (p // 32) * 32, p % 32
        if d_ < 16:
            Pm[p, blk + d_ + 16] = -1.0
        else:
            Pm[p, blk + d_ - 16] = 1.0
    pk["permT"] = np.ascontiguousarray(Pm.T)
    invf = 1.0 / 10000.0 ** (np.arange(16, dtype=f32) / 16.0)
    t = np.arange(T, dtype=f32)
    lane = np.arange(128) % 16
    pk["cosx"] = np.cos(t[None, :] * invf[lane][:, None]).astype(f32)
    pk["sinx"] = np.sin(t[None, :] * invf[lane][:, None]).astype(f32)
    e2 = np.zeros((64, 64), f32)
    for r in range(64):
        e2[r, r % 32] = 1.0
        e2[r, r % 32 + 32] = 1.0
    pk["e2c"] = np.ascontiguousarray(e2)
    s_, t_ = np.arange(128)[:, None], np.arange(128)[None, :]
    pk["gmaskc"] = (s_ <= t_).astype(f32)
    pk["ubc"] = (s_ <= t_).astype(f32)
    bo = np.zeros((128, 128), f32)
    for p in range(128):
        bo[p, (p // 32) * 32:(p // 32) * 32 + 32] = 1.0
    pk["boc"] = bo
    return pk


_L0_KEYS = ("x", "w_router", "wq", "wk", "wv", "bq", "bk", "bv", "bv_row", "ubc",
            "w1h", "w2h", "b1c", "b2r", "cident")
_L1_KEYS = ("attnf", "wm", "bm_row", "g32", "x", "x1", "wq1", "wk1", "wv1", "wbm",
            "bbm_col", "permT", "cosx", "sinx", "e2c", "wout", "bout_row", "lnrows",
            "ubc", "boc", "cident")


def kernel_unfused(**inputs):
    T = 4096
    n = inputs["L0_attn_x"].shape[0]
    pk = _host_pack(inputs, T)
    xs = [np.ascontiguousarray(inputs["L0_attn_x"][b]).astype(np.float32) for b in range(n)]

    nc0 = build_l0(T, phases=("a", "b", "c", "d"))
    maps0 = []
    for b in range(n):
        m = {k: pk[k] for k in _L0_KEYS if k in pk}
        m["x"] = xs[b]
        maps0.append(m)
    res0 = run_bass_kernel_spmd(nc0, maps0, core_ids=list(range(n)))
    global _dbg_res0
    _dbg_res0 = res0.results

    nc1 = build_l1(T, phases=("e", "e2", "f", "g"))
    maps1 = []
    for b in range(n):
        m = {k: pk[k] for k in _L1_KEYS if k in pk}
        m["x"] = xs[b]
        m["attnf"] = res0.results[b]["attnf"]
        m["g32"] = res0.results[b]["g32"]
        maps1.append(m)
    res1 = run_bass_kernel_spmd(nc1, maps1, core_ids=list(range(n)))
    global _dbg_res1
    _dbg_res1 = res1.results
    out = np.stack([res1.results[b]["outp"] for b in range(n)], axis=0)
    return out


def build_full(T=4096, ab_elems=71680, af_elems=13312):
    nc = bass.Bass("TRN2", target_bir_lowering=False)
    io = {}
    EI = "ExternalInput"
    mk = Mk(nc)
    _mk_shared(mk, nc, io, T, ab_elems=ab_elems, af_elems=af_elems)
    dt_ = nc.dram_tensor
    io["x"] = dt_("x", [T, D], F32, kind=EI).ap()
    io["w_router"] = dt_("w_router", [D, E0], F32, kind=EI).ap()
    for nm in ("wq", "wk", "wv", "wm", "wq1", "wk1", "wv1", "wout"):
        io[nm] = dt_(nm, [128, 8, 1024], F32, kind=EI).ap()
    for nm in ("bq", "bk", "bv"):
        io[nm] = dt_(nm, [128, 8], F32, kind=EI).ap()
    io["bv_row"] = dt_("bv_row", [1, D], F32, kind=EI).ap()
    io["bm_row"] = dt_("bm_row", [1, D], F32, kind=EI).ap()
    io["w1h"] = dt_("w1h", [8, 128, 8, 4096], F32, kind=EI).ap()
    io["w2h"] = dt_("w2h", [8, 128, 8, 4, 1024], F32, kind=EI).ap()
    io["b1c"] = dt_("b1c", [128, 8, 32], F32, kind=EI).ap()
    io["b2r"] = dt_("b2r", [8, 1024], F32, kind=EI).ap()
    io["wbm"] = dt_("wbm", [128, 2, 8, 64], F32, kind=EI).ap()
    io["bbm_col"] = dt_("bbm_col", [64, 2], F32, kind=EI).ap()
    io["permT"] = dt_("permT", [128, 128], F32, kind=EI).ap()
    io["cosx"] = dt_("cosx", [128, T], F32, kind=EI).ap()
    io["sinx"] = dt_("sinx", [128, T], F32, kind=EI).ap()
    io["e2"] = dt_("e2c", [64, 64], F32, kind=EI).ap()
    io["bo"] = dt_("boc", [128, 128], F32, kind=EI).ap()
    io["bout_row"] = dt_("bout_row", [1, D], F32, kind=EI).ap()
    io["lnrows"] = dt_("lnrows", [4, D], F32, kind=EI).ap()
    IN = "Internal"
    xnf0 = dt_("i_xnf0", [D, T], BF16, kind=IN).ap()
    xnf1 = dt_("i_xnf1", [D, T], BF16, kind=IN).ap()
    io["msk"] = dt_("i_msk", [T, E0], F32, kind=IN).ap()
    io["qf"] = dt_("i_qf", [D, T], BF16, kind=IN).ap()
    io["kf"] = dt_("i_kf", [D, T], BF16, kind=IN).ap()
    io["v0"] = dt_("i_v0", [8, T, 130], BF16, kind=IN).ap()
    io["attnf"] = dt_("i_attnf", [D, T], BF16, kind=IN).ap()
    io["g32"] = dt_("i_g32", [T, D], F32, kind=IN).ap()
    io["x1"] = dt_("i_x1", [T, D], F32, kind=IN).ap()
    io["xn1t"] = dt_("i_xn1t", [T, D], F32, kind=IN).ap()
    io["phq"] = dt_("i_phq", [D, T], F32, kind=IN).ap()
    io["phk"] = dt_("i_phk", [D, T], F32, kind=IN).ap()
    io["v1"] = dt_("i_v1", [8, T, 132], F32, kind=IN).ap()
    io["mixh"] = dt_("i_mixh", [64, T], F32, kind=IN).ap()
    io["ph_"] = dt_("i_ph", [64, T], F32, kind=IN).ap()
    io["out"] = dt_("outp", [T, D], F32, kind="ExternalOutput").ap()

    ab, af = io["ab"], io["af"]
    io["xnf"] = xnf0
    ph_ln0(mk, io, T, layer=0)
    io["xnf_sb"] = mk._xnf_sb
    ph_l0proj(mk, io, T)
    ab.reset(); af.reset()
    ph_l0attn(mk, io, T)
    ab.reset(); af.reset()
    ph_l0moe(mk, io, T)
    ab.reset(); af.reset()
    ph_l0post(mk, io, T)
    ab.reset(); af.reset()
    io["xnf"] = xnf1
    ph_ln0(mk, io, T, layer=1)
    io["xnf_sb"] = mk._xnf_sb
    af.reset()
    ph_l1prep(mk, io, T)
    ab.reset(); af.reset()
    ph_l1gla(mk, io, T)
    mk.emit_program()
    return nc


_FULL_KEYS = ("x", "w_router", "wq", "wk", "wv", "wm", "bq", "bk", "bv", "bv_row",
              "bm_row", "w1h", "w2h", "b1c", "b2r", "wq1", "wk1", "wv1",
              "wout", "wbm", "bbm_col", "permT", "cosx", "sinx", "e2c",
              "ubc", "boc", "bout_row", "lnrows", "cident")


def kernel(**inputs):
    T = 4096
    n = inputs["L0_attn_x"].shape[0]
    pk = _host_pack(inputs, T)
    nc = build_full(T)
    maps = []
    for b in range(n):
        m = {k: pk[k] for k in _FULL_KEYS if k in pk}
        m["x"] = np.ascontiguousarray(inputs["L0_attn_x"][b]).astype(np.float32)
        maps.append(m)
    res = run_bass_kernel_spmd(nc, maps, core_ids=list(range(n)))
    return np.stack([res.results[b]["outp"] for b in range(n)], axis=0)


# version 07
# speedup vs baseline: 1.0283x; 1.0283x over previous
import math
from contextlib import ExitStack

import numpy as np
import concourse.bass as bass
import concourse.mybir as mybir
from concourse.bass_utils import run_bass_kernel_spmd

F32 = mybir.dt.float32
BF16 = mybir.dt.bfloat16
AF = mybir.ActivationFunctionType
ALU = mybir.AluOpType
AX = mybir.AxisListType


_ENG_ATTR = {"pe": "tensor", "dve": "vector", "act": "scalar", "pool": "gpsimd", "sp": "sync"}
_ENGINES = list(_ENG_ATTR)

_DT_SIZE = {"dt.float32": 4, "dt.bfloat16": 2, "dt.float16": 2, "dt.int32": 4,
            "dt.uint32": 4, "dt.int8": 1, "dt.uint8": 1, "dt.float32r": 4,
            "dt.int16": 2, "dt.uint16": 2, "dt.int64": 8, "dt.uint64": 8}


def _ap_region(ap):
    if not hasattr(ap, "tensor"):
        ap = ap[:]
    t = ap.tensor
    name = t.name
    tn = type(t).__name__
    if "PSum" in tn:
        space = "psum"
    elif "SB" in tn:
        space = "sbuf"
    else:
        return (name, "dram", 0, 1, 0, 1)
    shape = list(t.shape)
    esz = _DT_SIZE[str(t.dtype)]
    rowpitch = 1
    for s in shape[1:]:
        rowpitch *= s
    pairs = [list(p) for p in ap.ap]
    off = ap.offset
    pidx = None
    for i, (st, ct) in enumerate(pairs):
        if st == rowpitch and rowpitch > 1:
            pidx = i
            break
    if pidx is None:
        if shape[0] == 1 or rowpitch == 1:
            pidx = -1
        else:
            maxspan = off + sum(abs(st) * (ct - 1) for st, ct in pairs)
            if maxspan < rowpitch * (off // rowpitch + 1):
                pidx = -1
            else:
                return (name, space, 0, shape[0], 0, rowpitch * esz)
    plo = off // rowpitch
    f0 = off % rowpitch
    fspan = 0
    pcnt = 1
    for i, (st, ct) in enumerate(pairs):
        if i == pidx:
            pcnt = ct
        else:
            fspan += abs(st) * (ct - 1)
    if f0 + fspan >= rowpitch:
        return (name, space, 0, shape[0], 0, rowpitch * esz)
    blo, bhi = f0 * esz, (f0 + fspan + 1) * esz
    if space == "psum":
        blo = (blo // 2048) * 2048
        bhi = ((bhi + 2047) // 2048) * 2048
    return (name, space, plo, plo + pcnt, blo, bhi)


def _overlap(r1, r2):
    return (r1[0] == r2[0] and r1[2] < r2[3] and r2[2] < r1[3]
            and r1[4] < r2[5] and r2[4] < r1[5])


def _covers(w, r):
    return (w[0] == r[0] and w[2] <= r[2] and w[3] >= r[3]
            and w[4] <= r[4] and w[5] >= r[5])


class _Op:
    __slots__ = ("idx", "eng", "emit", "reads", "writes", "deps", "need_inc",
                 "is_dma", "sem_key", "seq", "tag")

    def __init__(self, idx, eng, emit, reads, writes, is_dma, sem_key, tag):
        self.idx = idx
        self.eng = eng
        self.emit = emit
        self.reads = reads
        self.writes = writes
        self.deps = []
        self.need_inc = is_dma
        self.is_dma = is_dma
        self.sem_key = sem_key
        self.seq = None
        self.tag = tag


class Mk:
    SERIALIZE = False

    def __init__(self, nc):
        self.nc = nc
        self.ops = []
        self.w_recs = {}
        self.r_recs = {}
        self._es = ExitStack()
        self.dma_counts = {}
        self._const_deps = []
        self._const_cols = {}

    def sbuf(self, name, shape, dtype):
        return self._es.enter_context(self.nc.sbuf_tensor(name, shape, dtype))

    def psum(self, name, shape, dtype=F32):
        return self._es.enter_context(self.nc.psum_tensor(name, shape, dtype))

    def dram(self, name, shape, dtype, kind="Internal"):
        return self.nc.dram_tensor(name, shape, dtype, kind=kind).ap()

    def init_consts(self, vals):
        vals = sorted(set(float(v) for v in vals) - {0.0, 1.0})
        if not vals:
            return
        n = len(vals)
        cf = self.sbuf("mk_consts_f32", [128, n], F32)
        cb = self.sbuf("mk_consts_bf16", [128, n], BF16)
        for i, v in enumerate(vals):
            self.nc.const_aps.register(F32, v, cf[:, i:i + 1])
            self.nc.const_aps.register(BF16, v, cb[:, i:i + 1])
            o1 = self.op("pool", lambda g, _cf=cf, _i=i, _v=v: g.memset(_cf[:, _i:_i + 1], _v),
                         writes=[cf[:, i:i + 1]], tag="const")
            o2 = self.op("pool", lambda g, _cb=cb, _i=i, _v=v: g.memset(_cb[:, _i:_i + 1], _v),
                         writes=[cb[:, i:i + 1]], tag="const")
            self._const_deps += [o1.idx, o2.idx]

    def _record(self, eng, emit, reads, writes, is_dma=False, sem_key=None, tag=""):
        idx = len(self.ops)
        rr = [_ap_region(a) for a in reads if a is not None]
        ww = [_ap_region(a) for a in writes if a is not None]
        op = _Op(idx, eng, emit, rr, ww, is_dma, sem_key, tag)
        deps = set()
        if not is_dma and tag != "const":
            deps.update(self._const_deps)
        for r in rr:
            for rec in self.w_recs.get(r[0], ()):
                if _overlap(r, rec[0]):
                    deps.add(rec[1])
            if r[1] == "psum" and eng in ("dve", "act"):
                other = "act" if eng == "dve" else "dve"
                rec = self.r_recs.get(r[0], {}).get(other)
                if rec is not None and _overlap(r, rec[0]):
                    deps.add(rec[1])
        for w in ww:
            for rec in self.w_recs.get(w[0], ()):
                if _overlap(w, rec[0]):
                    deps.add(rec[1])
            for rec in self.r_recs.get(w[0], {}).values():
                if _overlap(w, rec[0]):
                    deps.add(rec[1])
        if Mk.SERIALIZE and idx > 0:
            deps.add(idx - 1)
        op.deps = sorted(deps)
        key = sem_key if is_dma else eng
        for r in rr:
            d = self.r_recs.setdefault(r[0], {})
            if key in d:
                old = d[key][0]
                d[key] = [(r[0], r[1], min(old[2], r[2]), max(old[3], r[3]),
                           min(old[4], r[4]), max(old[5], r[5])), idx]
            else:
                d[key] = [r, idx]
        for w in ww:
            lst = self.w_recs.setdefault(w[0], [])
            lst[:] = [rec for rec in lst if not _covers(w, rec[0])]
            lst.append([w, idx])
            if len(lst) > 96:
                self._consolidate(lst)
            d = self.r_recs.get(w[0])
            if d:
                for k in list(d):
                    if _covers(w, d[k][0]):
                        del d[k]
        self.ops.append(op)
        return op

    def _consolidate(self, lst):
        byk = {}
        out = []
        half = lst[: len(lst) // 2]
        for rec in half:
            op = self.ops[rec[1]]
            k = op.sem_key if op.is_dma else op.eng
            if k in byk:
                o = byk[k]
                r0, r1 = o[0], rec[0]
                o[0] = (r0[0], r0[1], min(r0[2], r1[2]), max(r0[3], r1[3]),
                        min(r0[4], r1[4]), max(r0[5], r1[5]))
                o[1] = max(o[1], rec[1])
            else:
                byk[k] = rec
                out.append(rec)
        lst[:] = out + lst[len(lst) // 2:]

    def op(self, eng, emit, reads=(), writes=(), tag=""):
        return self._record(eng, emit, list(reads), list(writes), tag=tag)

    def dma(self, queue, out, in_, sem_key=None, tag="", **kw):
        if sem_key is None:
            sem_key = (queue, _ap_region(out)[0])
        else:
            sem_key = (queue, sem_key)
        emit = lambda e, _o=out, _i=in_, _k=dict(kw): e.dma_start(out=_o, in_=_i, **_k)
        return self._record(queue, emit, [in_], [out], is_dma=True, sem_key=sem_key, tag=tag)

    def emit_program(self):
        nc = self.nc
        ops = self.ops
        for op in ops:
            for d in op.deps:
                ops[d].need_inc = True
        eng_seq = {e: 0 for e in _ENGINES}
        cnt = {}
        for op in ops:
            if op.is_dma:
                cnt[op.sem_key] = cnt.get(op.sem_key, 0) + 1
                op.seq = cnt[op.sem_key]
            elif op.need_inc:
                eng_seq[op.eng] += 1
                op.seq = eng_seq[op.eng]
        self.dma_counts = cnt
        es = self._es
        eng_sems = {e: es.enter_context(nc.semaphore(f"se_{e}")) for e in _ENGINES}
        sem_keys = sorted(cnt, key=str)
        dma_sems = {k: es.enter_context(nc.semaphore(f"sd_{i}")) for i, k in enumerate(sem_keys)}

        per_eng = {e: [] for e in _ENGINES}
        for op in ops:
            per_eng[op.eng].append(op)

        block = es.enter_context(nc.Block())
        mkops = self.ops

        def make_body(ename, elist, final=False):
            def body(eng):
                clock = {}
                for op in elist:
                    if op.is_dma and op.seq >= 2:
                        kk = ("d", op.sem_key)
                        v = 16 * (op.seq - 1)
                        if clock.get(kk, -1) < v:
                            eng.wait_ge(dma_sems[op.sem_key], v)
                            clock[kk] = v
                    for d in op.deps:
                        p = mkops[d]
                        if (not p.is_dma) and p.eng == "pe" and ename == "pe":
                            continue
                        if p.is_dma:
                            kk, v, sem = ("d", p.sem_key), 16 * p.seq, dma_sems[p.sem_key]
                        else:
                            kk, v, sem = ("e", p.eng), p.seq, eng_sems[p.eng]
                        if clock.get(kk, -1) < v:
                            eng.wait_ge(sem, v)
                            clock[kk] = v
                    inst = op.emit(eng)
                    if op.is_dma:
                        inst.then_inc(dma_sems[op.sem_key], 16)
                    elif op.need_inc:
                        inst.then_inc(eng_sems[op.eng], 1)
                if final:
                    for k in sem_keys:
                        eng.wait_ge(dma_sems[k], 16 * cnt[k])
            return body

        for ename in _ENGINES:
            if per_eng[ename] or ename == "sp":
                body = make_body(ename, per_eng[ename], final=(ename == "sp"))
                getattr(block, _ENG_ATTR[ename])(body)
        es.close()
        return nc


def mm(mk, out, lhsT, rhs, start=True, stop=True, tile_position=None, tag="mm", skip=False):
    return mk.op("pe", lambda t, _o=out, _l=lhsT, _r=rhs, _s=start, _e=stop, _p=tile_position,
                 _k=skip:
                 t.matmul(_o, _l, _r, start=_s, stop=_e, tile_position=_p, skip_group_check=_k),
                 reads=[lhsT, rhs], writes=[out], tag=tag)


def transpose(mk, out, in_, ident, tag="tr"):
    return mk.op("pe", lambda t, _o=out, _i=in_, _d=ident: t.transpose(_o, _i, _d),
                 reads=[in_, ident], writes=[out], tag=tag)


def act(mk, out, in_, func, bias=None, scale=None, accum_out=None, tag="act"):
    kw = {}
    if bias is not None:
        kw["bias"] = bias
    if scale is not None:
        kw["scale"] = scale
    if accum_out is not None:
        kw["accum_out"] = accum_out
    reads = [in_] + ([bias] if hasattr(bias, "tensor") else [])
    writes = [out] + ([accum_out] if accum_out is not None else [])
    return mk.op("act", lambda s, _o=out, _i=in_, _f=func, _k=dict(kw):
                 s.activation(out=_o, in_=_i, func=_f, **_k), reads=reads, writes=writes, tag=tag)


def tt(mk, eng, out, in0, in1, op, tag="tt"):
    return mk.op(eng, lambda v, _o=out, _a=in0, _b=in1, _p=op: v.tensor_tensor(_o, _a, _b, _p),
                 reads=[in0, in1], writes=[out], tag=tag)


def ts(mk, eng, out, in0, s1, op0, s2=None, op1=None, tag="ts"):
    def emitfn(v, _o=out, _i=in0, _s1=s1, _s2=s2, _o0=op0, _o1=op1):
        if _s2 is None:
            return v.tensor_scalar(_o, _i, _s1, None, _o0)
        return v.tensor_scalar(_o, _i, _s1, _s2, _o0, _o1)
    reads = [in0] + [s for s in (s1, s2) if hasattr(s, "tensor")]
    return mk.op(eng, emitfn, reads=reads, writes=[out], tag=tag)


def copy(mk, eng, out, in_, tag="copy"):
    if eng == "act":
        return mk.op(eng, lambda s, _o=out, _i=in_: s.activation(out=_o, in_=_i, func=AF.Identity),
                     reads=[in_], writes=[out], tag=tag)
    return mk.op(eng, lambda v, _o=out, _i=in_: v.tensor_copy(_o, _i), reads=[in_], writes=[out], tag=tag)


def reduce(mk, out, in_, op, tag="red"):
    return mk.op("dve", lambda v, _o=out, _i=in_, _p=op:
                 v.tensor_reduce(_o, _i, AX.X, _p), reads=[in_], writes=[out], tag=tag)


def recip(mk, out, in_, tag="recip"):
    return mk.op("dve", lambda v, _o=out, _i=in_: v.reciprocal(_o, _i),
                 reads=[in_], writes=[out], tag=tag)


def scan(mk, out, d0, d1, init, op0, op1, tag="scan"):
    return mk.op("dve", lambda v, _o=out, _a=d0, _b=d1, _n=init, _p=op0, _q=op1:
                 v.tensor_tensor_scan(_o, _a, _b, _n, _p, _q),
                 reads=[d0, d1], writes=[out], tag=tag)


def stt(mk, out, in0, scalar, in1, op0, op1, tag="stt"):
    reads = [in0, in1] + ([scalar] if hasattr(scalar, "tensor") else [])
    return mk.op("dve", lambda v, _o=out, _a=in0, _s=scalar, _b=in1, _p=op0, _q=op1:
                 v.scalar_tensor_tensor(_o, _a, _s, _b, _p, _q),
                 reads=reads, writes=[out], tag=tag)


def memset(mk, eng, ap_, val, tag="memset"):
    return mk.op(eng, lambda v, _o=ap_, _v=val: v.memset(_o, _v), writes=[ap_], tag=tag)



D = 1024
NH0, HS0 = 16, 64
NH1, HS1, M1 = 32, 32, 2
E0, F0 = 8, 4096
LN_EPS = 1e-5
SOFT_M = 40.0
GLA_EPS = 1e-6
BMIN, BMAX = 0.85, 0.9995


ALL_CONSTS = [1.0 / D, LN_EPS, -1.0, -0.5, -SOFT_M, 0.125, GLA_EPS, BMIN, BMAX,
              2.0, 1.0 / F0, 0.5]


def ph_ln0(mk, io, T, layer=0):
    nt = T // 128
    sfx = f"_l{layer}"
    xsrc = io["x"] if layer == 0 else io["x1"]
    x_t = io["af"].alloc(2, D)
    xc = io["af"].alloc(2, D)
    sq = io["af"].alloc(D)
    st = mk.sbuf(f"ln_st{sfx}", [128, 8], F32)
    io["_st8_shared"] = st
    xnt32 = io["af"].alloc(D)
    xnf = io["ab"].alloc(D // 128, T)
    ps_tr = io["PS_F"]
    if layer == 0:
        wr = mk.sbuf("ln_wr", [128, 8, E0], BF16)
        mk.dma("pool", wr[:], io["w_router"].rearrange("(c p) e -> p c e", p=128), tag="wr")
        ps_r = io["PS_E"][:, 0:E0]
        lg = mk.sbuf("ln_lg", [128, E0], F32)
        eq = mk.sbuf("ln_eq", [128, E0], F32)
        cum = mk.sbuf("ln_cum", [128, E0], F32)
        zer8 = mk.sbuf("ln_z8", [128, E0], F32)
        msk = mk.sbuf("ln_msk", [128, nt, E0], F32)
        io["_msk_sb"] = msk
        memset(mk, "pool", zer8[:], 0.0)

    for i in range(nt):
        xi = x_t[:, i % 2, :]
        xci = xc[:, i % 2, :]
        mk.dma("sp", xi, xsrc[i * 128:(i + 1) * 128, :], sem_key=f"x{sfx}{i % 2}", tag="xload")
        reduce(mk, st[:, 0:1], xi, ALU.add)
        ts(mk, "dve", st[:, 1:2], st[:, 0:1], 1.0 / D, ALU.mult)
        ts(mk, "dve", xci, xi, st[:, 1:2], ALU.subtract)
        act(mk, sq[:], xci, AF.Square, accum_out=st[:, 2:3])
        act(mk, st[:, 3:4], st[:, 2:3], AF.Ln, scale=1.0 / D, bias=LN_EPS)
        act(mk, st[:, 4:5], st[:, 3:4], AF.Exp, scale=-0.5)
        ts(mk, "dve", xnt32[:], xci, st[:, 4:5], ALU.mult)
        if layer == 1 and io.get("xn1t") is not None:
            mk.dma("sp", io["xn1t"][i * 128:(i + 1) * 128, :], xnt32[:],
                   sem_key=f"xn1t{i % 2}", tag="xn1t")
        for hb in range(2):
            for b2 in range(4):
                b = hb * 4 + b2
                transpose(mk, ps_tr[:, 128 * b2:128 * (b2 + 1)],
                          xnt32[:, 128 * b:128 * (b + 1)], io["identf_sb"][:])
            for b2 in range(4):
                b = hb * 4 + b2
                copy(mk, "dve" if b % 2 == 0 else "act", xnf[:, b, i * 128:(i + 1) * 128],
                     ps_tr[:, 128 * b2:128 * (b2 + 1)])
        if layer == 0:
            for c in range(8):
                mm(mk, ps_r, xnf[:, c, i * 128:(i + 1) * 128], wr[:, c, :],
                   start=(c == 0), stop=(c == 7))
            reduce(mk, st[:, 5:6], ps_r, ALU.max)
            ts(mk, "dve", st[:, 6:7], st[:, 5:6], -1.0, ALU.mult)
            act(mk, lg[:], ps_r, AF.Exp, bias=st[:, 6:7])
            reduce(mk, st[:, 7:8], lg[:], ALU.add)
            recip(mk, st[:, 7:8], st[:, 7:8])
            tt(mk, "dve", eq[:], ps_r, st[:, 5:6].broadcast_to((128, E0)), ALU.is_equal)
            scan(mk, cum[:], eq[:], zer8[:], 0.0, ALU.add, ALU.add)
            ts(mk, "dve", cum[:], cum[:], 1.0, ALU.is_equal)
            tt(mk, "dve", eq[:], eq[:], cum[:], ALU.mult)
            ts(mk, "dve", msk[:, i, :], eq[:], st[:, 7:8], ALU.mult)

    mk._xnf_sb = xnf
    if "xnf" in io and io["xnf"] is not None:
        for b in range(8):
            mk.dma("sp", io["xnf"][128 * b:128 * (b + 1), :], xnf[:, b, :],
                   sem_key=f"xnf{sfx}", tag="xnf_store")
    if layer == 0:
        mk.dma("sp", io["msk"].rearrange("(i p) e -> p i e", p=128), msk[:], tag="msk_store")


def build_phase_a(T=4096):
    nc = bass.Bass("TRN2", target_bir_lowering=False)
    io = {}
    mk = Mk(nc)
    io["x"] = nc.dram_tensor("x", [T, D], F32, kind="ExternalInput").ap()
    io["w_router"] = nc.dram_tensor("w_router", [D, E0], F32, kind="ExternalInput").ap()
    io["xnf"] = nc.dram_tensor("xnf", [D, T], BF16, kind="ExternalOutput").ap()
    io["msk"] = nc.dram_tensor("msk", [T, E0], F32, kind="ExternalOutput").ap()
    _mk_shared(mk, nc, io, T)
    ph_ln0(mk, io, T, layer=0)
    mk.emit_program()
    return nc



class Arena:
    def __init__(self, mk, name, n_elems, dtype):
        self.t = mk.sbuf(name, [128, n_elems], dtype)
        self.n = n_elems
        self.off = 0

    def reset(self):
        self.off = 0

    def alloc(self, *shape):
        n = 1
        for s in shape:
            n *= s
        assert self.off + n <= self.n, f"arena overflow: {self.off}+{n} > {self.n}"
        v = self.t[:, self.off:self.off + n]
        self.off += n
        if len(shape) > 1:
            dims = " ".join(f"d{i}" for i in range(len(shape)))
            v = v.rearrange(f"p ({dims}) -> p {dims}",
                            **{f"d{i}": s for i, s in enumerate(shape)})
        return v


def ph_l0proj(mk, io, T):
    ab, af = io["ab"], io["af"]
    xnf = io["xnf_sb"]
    wbuf = [ab.alloc(8, 1024) for _ in range(2)]
    bq = mk.sbuf("b_qkv", [128, 3, 8], F32)
    for j, nm in enumerate(("bq", "bk", "bv")):
        mk.dma("sp", bq[:, j, :], io[nm], tag="bias")
    onesb = io["onesb_sb"]
    bvrow = io["rowb_sb"]
    mk.dma("pool", bvrow[:], io["bv_row"], tag="bvrow")
    ps = io["PS_A"]
    ps2 = io["PS_B"]
    qstage = ab.alloc(2, 512)

    for j, (wname, out_h) in enumerate((("wq", "qf"), ("wk", "kf"))):
        wb = wbuf[j % 2]
        mk.dma("pool", wb, io[wname], tag=f"w_{wname}")
        for c in range(8):
            for tj in range(T // 512):
                p = ps[:, 0:512] if (c * (T // 512) + tj) % 2 == 0 else ps2[:, 0:512]
                for dc in range(8):
                    mm(mk, p, wb[:, dc, c * 128:(c + 1) * 128],
                       xnf[:, dc, tj * 512:(tj + 1) * 512], start=(dc == 0), stop=(dc == 7))
                stg = qstage[:, tj % 2, :]
                if tj % 2 == 0:
                    ts(mk, "dve", stg, p, bq[:, j, c:c + 1], ALU.add)
                else:
                    act(mk, stg, p, AF.Identity, bias=bq[:, j, c:c + 1])
                mk.dma("sp", io[out_h][c * 128:(c + 1) * 128, tj * 512:(tj + 1) * 512], stg,
                       sem_key=f"{out_h}{tj % 2}", tag=f"{out_h}_store")

    wb = wbuf[0]
    mk.dma("pool", wb, io["wv"], tag="w_wv")
    vstage = ab.alloc(2, 1040)
    for ti in range(T // 128):
        p = ps[:, 0:512] if ti % 2 == 0 else ps2[:, 0:512]
        stg = vstage[:, ti % 2, :].rearrange("p (h d) -> p h d", d=65)
        for o in range(2):
            for dc in range(8):
                mm(mk, p, xnf[:, dc, ti * 128:(ti + 1) * 128], wb[:, dc, o * 512:(o + 1) * 512],
                   start=(dc == 0), stop=False)
            mm(mk, p, onesb[:], bvrow[:, o * 512:(o + 1) * 512], start=False, stop=True)
            copy(mk, "dve" if ti % 2 == 0 else "act",
                 stg[:, o * 8:(o + 1) * 8, 0:64],
                 p.rearrange("p (h d) -> p h d", d=64))
        memset(mk, "pool", stg[:, :, 64:65], 1.0)
        mk.dma("sp", io["v0"].rearrange("h (i p) c -> p i h c", p=128)[:, ti, :, :],
               vstage[:, ti % 2, :].rearrange("p (h c) -> p h c", c=130),
               sem_key=f"v0{ti % 2}", tag="v0_store")


def ph_l0attn(mk, io, T):
    ab, af = io["ab"], io["af"]
    qp = [ab.alloc(T) for _ in range(2)]
    kp = [ab.alloc(T) for _ in range(2)]
    vp = [ab.alloc(T // 128, 130) for _ in range(2)]
    pt = [ab.alloc(4, 512) for _ in range(2)]
    stage = [ab.alloc(512) for _ in range(2)]
    rden = af.alloc(512)
    rdb = af.alloc(2, 512)
    onesf = io["onesf_sb"]
    PS_A, PS_B = io["PS_A"], io["PS_B"]
    PS_O = (io["PS_C"][0:65, :], io["PS_D"][0:65, :])
    PS_E = io["PS_E"]
    nq = T // 512
    ubb = io["ubb_sb"]

    for hp in range(8):
        b = hp % 2
        mk.dma("sp", qp[b], io["qf"][128 * hp:128 * (hp + 1), :], sem_key=f"qp{b}", tag="qload")
        mk.dma("sp", kp[b], io["kf"][128 * hp:128 * (hp + 1), :], sem_key=f"kp{b}", tag="kload")
        mk.dma("pool", vp[b], io["v0"].rearrange("h (i p) c -> p i h c", p=128)[:, :, hp, :],
               sem_key=f"vp{b}", tag="vload")
        for qb in range(nq):
            nk = 4 * (qb + 1)
            qA = qp[b][0:64, qb * 512:(qb + 1) * 512]
            qB = qp[b][64:128, qb * 512:(qb + 1) * 512]
            for kb in range(nk):
                pA = PS_A[:, (kb % 2) * 512:(kb % 2 + 1) * 512]
                pB = PS_B[:, (kb % 2) * 512:(kb % 2 + 1) * 512]
                mm(mk, pA, kp[b][0:64, kb * 128:(kb + 1) * 128], qA, tile_position=(0, 0))
                mm(mk, pB, kp[b][64:128, kb * 128:(kb + 1) * 128], qB, tile_position=(64, 0))
                if kb % 2 == 1:
                    for h, ps_ in ((0, PS_A), (1, PS_B)):
                        r = (kb - 1) % 4
                        act(mk, pt[h][:, r:r + 2, :].rearrange("p a t -> p (a t)"),
                            ps_[:], AF.Exp, scale=0.125, bias=-SOFT_M)
                        for kk in (kb - 1, kb):
                            j = kk - 4 * qb
                            if 0 <= j < 4:
                                sl_ = pt[h][:, kk % 4, :]
                                if j > 0:
                                    memset(mk, "dve", sl_[:, 0:128 * j], 0.0, tag="diagz")
                                tt(mk, "dve", sl_[:, 128 * j:128 * (j + 1)],
                                   sl_[:, 128 * j:128 * (j + 1)], ubb[:],
                                   ALU.mult, tag="diagmask")
                        for kk in (kb - 1, kb):
                            mm(mk, PS_O[h], vp[b][:, kk, 65 * h:65 * h + 65],
                               pt[h][:, kk % 4, :], start=(kk == 0), stop=(kk == nk - 1))
            for h in range(2):
                psO = PS_O[h]
                recip(mk, rden[64:65, :], psO[64:65, :])
                mm(mk, PS_E[0:65, :], onesf[64:65, 0:65], rden[64:65, :],
                   tile_position=(64, 0))
                copy(mk, "act", rdb[0:65, h, :], PS_E[0:65, :])
                tt(mk, "dve", stage[h][0:65, :], psO[0:65, :], rdb[0:65, h, :], ALU.mult)
                mk.dma("sp", io["attnf"][64 * (2 * hp + h):64 * (2 * hp + h + 1),
                                         qb * 512:(qb + 1) * 512],
                       stage[h][0:64, :], sem_key=f"at{h}", tag="attn_store")


def ph_l0moe(mk, io, T):
    ab, af = io["ab"], io["af"]
    nt = T // 256
    w1a = ab.alloc(8, 2048)
    w1b = ab.alloc(8, 2048)
    w2e = ab.alloc(8, 4, 1024)
    _x2 = ab.alloc(8, 256)
    xnt2 = [_x2, _x2]
    h_sb = [ab.alloc(4, 256) for _ in range(2)]
    ystage = [af.alloc(2, 1024) for _ in range(2)]
    b1c = mk.sbuf("moe_b1", [128, 8, 32], F32)
    b2r = io["rowb_sb"].rearrange("a (u d) -> a u d", u=1)
    onesb = io["onesb_sb"]
    if "_msk_sb" in io:
        mskb = io["_msk_sb"]
    else:
        mskb = mk.sbuf("moe_msk", [128, T // 128, 8], F32)
        mk.dma("sp", mskb[:], io["msk"].rearrange("(i p) e -> p i e", p=128), tag="mskload")
    mk.dma("sp", b1c[:], io["b1c"], tag="b1load")
    PS_Y = (io["PS_A"], io["PS_B"])
    PS_H4 = (io["PS_C"][:, 0:256], io["PS_E"][:, 0:256],
             io["PS_D"][:, 0:256], io["PS_F"][:, 0:256])
    h4 = tuple(h_sb[0][:, k, :] for k in range(4)) + tuple(h_sb[1][:, k, :] for k in range(4))

    def emit_h(e, xnf, w1a, w1b, fc):
        w1h = w1a if fc < 4 else w1b
        fo = (fc % 4) * 512
        for fs in range(4):
            ph = PS_H4[(fc % 2) * 2 + (fs % 2)]
            for dc in range(8):
                mm(mk, ph, w1h[:, dc, fo + fs * 128:fo + (fs + 1) * 128],
                   xnf[:, dc, :], start=(dc == 0), stop=(dc == 7))
            col = b1c[:, e, fc * 4 + fs:fc * 4 + fs + 1]
            hv = h4[(fc % 2) * 4 + fs]
            if fs % 2 == 0:
                ts(mk, "dve", hv, ph, col, ALU.add, 0.0, ALU.max, tag="relu")
            else:
                act(mk, hv, ph, AF.Relu, bias=col, tag="relu")

    def emit_y(e, fc):
        for fs in range(4):
            hv = h4[(fc % 2) * 4 + fs]
            for tsub in range(2):
                for n in range(2):
                    mm(mk, PS_Y[tsub][:, n * 512:(n + 1) * 512],
                       hv[:, tsub * 128:(tsub + 1) * 128],
                       w2e[:, fc, fs, n * 512:(n + 1) * 512],
                       start=(fc == 0 and fs == 0), stop=False)

    for e in range(8):
        mk.dma("pool", b2r[:, 0, :], io["b2r"][e:e + 1, :], sem_key="b2r", tag="b2load")
        mk.dma("pool", w1a, io["w1h"][e, :, :, 0:2048], sem_key="w1a", tag="w1a")
        mk.dma("pool", w1b, io["w1h"][e, :, :, 2048:4096], sem_key="w1b", tag="w1b")
        mk.dma("pool", w2e, io["w2h"][e, :, :, :, :], sem_key="w2e", tag="w2e")
        for i in range(nt):
            t0 = i * 256
            xnf = xnt2[i % 2]
            mk.dma("sp", xnf,
                   io["xnf"].rearrange("(c p) t -> p c t", p=128)[:, :, t0:t0 + 256],
                   sem_key=f"xnt2{i % 2}", tag="xnf_tile")
            emit_h(e, xnf, w1a, w1b, 0)
            for fc in range(8):
                if fc < 7:
                    emit_h(e, xnf, w1a, w1b, fc + 1)
                emit_y(e, fc)
            for tsub in range(2):
                for n in range(2):
                    mm(mk, PS_Y[tsub][:, n * 512:(n + 1) * 512], onesb[:],
                       b2r[:, 0, n * 512:(n + 1) * 512], start=False, stop=True)
                ts(mk, "dve", ystage[i % 2][:, tsub, :], PS_Y[tsub][:],
                   mskb[:, 2 * i + tsub, e:e + 1], ALU.mult, tag="ymask")
            kw = {} if e == 0 else {"accum_op": ALU.add}
            mk.dma("pool", io["g32"].rearrange("(i p) d -> p i d", p=128)[:, 2 * i:2 * i + 2, :],
                   ystage[i % 2][:], sem_key=f"g{i % 3}", tag="gaccum", **kw)


def _mk_shared(mk, nc, io, T, ab_elems=71680, af_elems=14336):
    mk.init_consts(ALL_CONSTS)
    io["ab"] = Arena(mk, "pool_bf16", ab_elems, BF16)
    io["af"] = Arena(mk, "pool_f32", af_elems, F32)
    io["cident"] = nc.dram_tensor("cident", [128, 128], F32, kind="ExternalInput").ap()
    identf = mk.sbuf("identf_sb", [128, 128], F32)
    mk.dma("sp", identf[:], io["cident"], tag="identf")
    io["identf_sb"] = identf
    onesf = mk.sbuf("onesf_sb", [128, 128], F32)
    memset(mk, "pool", onesf[:], 1.0)
    io["onesf_sb"] = onesf
    onesb = mk.sbuf("onesb_sb", [1, 128], BF16)
    memset(mk, "pool", onesb[:], 1.0)
    io["onesb_sb"] = onesb
    io["ubc"] = nc.dram_tensor("ubc", [128, 128], F32, kind="ExternalInput").ap()
    ubf = mk.sbuf("ub_sb", [128, 128], F32)
    mk.dma("sp", ubf[:], io["ubc"], tag="ubf")
    io["ub_sb"] = ubf
    ubb = mk.sbuf("ubb_sb", [128, 128], BF16)
    mk.dma("pool", ubb[:], io["ubc"], tag="ubb")
    io["ubb_sb"] = ubb
    io["rowb_sb"] = mk.sbuf("rowb_sb", [1, D], BF16)
    io["PS_A"] = mk.psum("PS_A", [128, 1024], F32)
    io["PS_B"] = mk.psum("PS_B", [128, 1024], F32)
    io["PS_C"] = mk.psum("PS_C", [128, 512], F32)
    io["PS_D"] = mk.psum("PS_D", [128, 512], F32)
    io["PS_E"] = mk.psum("PS_E", [128, 512], F32)
    io["PS_F"] = mk.psum("PS_F", [128, 512], F32)


def build_l0(T=4096, phases=("a", "b", "c"), ab_elems=71680, af_elems=6144):
    nc = bass.Bass("TRN2", target_bir_lowering=False)
    io = {}
    EI, EO = "ExternalInput", "ExternalOutput"
    mk = Mk(nc)
    _mk_shared(mk, nc, io, T, ab_elems=ab_elems, af_elems=af_elems)
    if "a" in phases:
        io["x"] = nc.dram_tensor("x", [T, D], F32, kind=EI).ap()
        io["w_router"] = nc.dram_tensor("w_router", [D, E0], F32, kind=EI).ap()
        io["xnf"] = nc.dram_tensor("xnf", [D, T], BF16, kind=EO).ap()
        io["msk"] = nc.dram_tensor("msk", [T, E0], F32, kind=EO).ap()
    else:
        io["xnf"] = nc.dram_tensor("xnf", [D, T], BF16, kind=EI).ap()
    if "b" in phases:
        for nm in ("wq", "wk", "wv"):
            io[nm] = nc.dram_tensor(nm, [128, 8, 1024], F32, kind=EI).ap()
        for nm in ("bq", "bk", "bv"):
            io[nm] = nc.dram_tensor(nm, [128, 8], F32, kind=EI).ap()
        io["bv_row"] = nc.dram_tensor("bv_row", [1, D], F32, kind=EI).ap()
        k_qkv = EO if "c" not in phases else EO
        io["qf"] = nc.dram_tensor("qf", [D, T], BF16, kind=k_qkv).ap()
        io["kf"] = nc.dram_tensor("kf", [D, T], BF16, kind=k_qkv).ap()
        io["v0"] = nc.dram_tensor("v0", [8, T, 130], BF16, kind=k_qkv).ap()
    else:
        io["qf"] = nc.dram_tensor("qf", [D, T], BF16, kind=EI).ap()
        io["kf"] = nc.dram_tensor("kf", [D, T], BF16, kind=EI).ap()
        io["v0"] = nc.dram_tensor("v0", [8, T, 130], BF16, kind=EI).ap()
    if "c" in phases:
        io["attnf"] = nc.dram_tensor("attnf", [D, T], BF16, kind=EO).ap()
    if "d" in phases:
        io["w1h"] = nc.dram_tensor("w1h", [8, 128, 8, 4096], F32, kind=EI).ap()
        io["w2h"] = nc.dram_tensor("w2h", [8, 128, 8, 4, 1024], F32, kind=EI).ap()
        io["b1c"] = nc.dram_tensor("b1c", [128, 8, 32], F32, kind=EI).ap()
        io["b2r"] = nc.dram_tensor("b2r", [8, 1024], F32, kind=EI).ap()
        if "a" not in phases:
            io["msk"] = nc.dram_tensor("msk", [T, E0], F32, kind=EI).ap()
        io["g32"] = nc.dram_tensor("g32", [T, D], F32, kind=EO).ap()

    if "a" in phases:
        ph_ln0(mk, io, T, layer=0)
        io["xnf_sb"] = mk._xnf_sb
    if "b" in phases:
        if "a" not in phases:
            xnf_sb = io["ab"].alloc(8, T)
            for c in range(8):
                mk.dma("sp", xnf_sb[:, c, :], io["xnf"][128 * c:128 * (c + 1), :],
                       sem_key="xnfld", tag="xnf_load")
            io["xnf_sb"] = xnf_sb
        ph_l0proj(mk, io, T)
    if "c" in phases:
        io["ab"].reset()
        io["af"].reset()
        ph_l0attn(mk, io, T)
    if "d" in phases:
        io["ab"].reset()
        io["af"].reset()
        ph_l0moe(mk, io, T)
    mk.emit_program()
    return nc


def ph_l0post(mk, io, T):
    ab, af = io["ab"], io["af"]
    at = [ab.alloc(8, 256) for _ in range(2)]
    wm = ab.alloc(8, 1024)
    gx = [af.alloc(2, 1024) for _ in range(2)]
    xx = [af.alloc(2, 1024) for _ in range(2)]
    x1s = [af.alloc(2, 1024) for _ in range(2)]
    bmrow = io["rowb_sb"]
    onesb = io["onesb_sb"]
    mk.dma("pool", wm, io["wm"], tag="wm")
    mk.dma("pool", bmrow[:], io["bm_row"], tag="bm")
    PS_Y = (io["PS_A"], io["PS_B"])
    nt = T // 256
    for i in range(nt):
        t0 = i * 256
        b = i % 2
        mk.dma("sp", at[b], io["attnf"].rearrange("(c p) t -> p c t", p=128)[:, :, t0:t0 + 256],
               sem_key=f"at{b}", tag="atload")
        mk.dma("sp", gx[b], io["g32"].rearrange("(i p) d -> p i d", p=128)[:, 2 * i:2 * i + 2, :],
               sem_key=f"gx{b}", tag="gload")
        mk.dma("sp", xx[b], io["x"].rearrange("(i p) d -> p i d", p=128)[:, 2 * i:2 * i + 2, :],
               sem_key=f"xx{b}", tag="xload")
        for tsub in range(2):
            for n in range(2):
                for dc in range(8):
                    mm(mk, PS_Y[tsub][:, n * 512:(n + 1) * 512],
                       at[b][:, dc, tsub * 128:(tsub + 1) * 128],
                       wm[:, dc, n * 512:(n + 1) * 512], start=(dc == 0), stop=False)
                mm(mk, PS_Y[tsub][:, n * 512:(n + 1) * 512], onesb[:],
                   bmrow[:, n * 512:(n + 1) * 512], start=False, stop=True)
            act(mk, gx[b][:, tsub, :], gx[b][:, tsub, :], AF.Silu)
            for n in range(2):
                sl_ = slice(n * 512, (n + 1) * 512)
                tt(mk, "dve", x1s[b][:, tsub, sl_], PS_Y[tsub][:, sl_],
                   gx[b][:, tsub, sl_], ALU.mult)
            tt(mk, "dve", x1s[b][:, tsub, :], x1s[b][:, tsub, :], xx[b][:, tsub, :], ALU.add)
        mk.dma("sp", io["x1"].rearrange("(i p) d -> p i d", p=128)[:, 2 * i:2 * i + 2, :],
               x1s[b], sem_key=f"x1{b}", tag="x1store")


def ph_l1prep(mk, io, T):
    ab, af = io["ab"], io["af"]
    xnf = io["xnf_sb"]
    wbuf = [ab.alloc(8, 1024) for _ in range(2)]
    permT = mk.sbuf("permT_sb", [128, 128], F32)
    mk.dma("sp", permT[:], io["permT"], tag="permT")
    cosx = [af.alloc(512) for _ in range(2)]
    sinx = [af.alloc(512) for _ in range(2)]
    raw32 = [af.alloc(512) for _ in range(2)]
    rot = af.alloc(512)
    rot2 = af.alloc(512)
    phs = [af.alloc(512) for _ in range(2)]
    PS_A, PS_B, PS_E, PS_F = io["PS_A"], io["PS_B"], io["PS_E"], io["PS_F"]

    for j, (wname, outh) in enumerate((("wq1", "phq"), ("wk1", "phk"))):
        wb = wbuf[j % 2]
        mk.dma("pool", wb, io[wname], tag=wname)
        for tj in range(T // 512):
            ts_ = tj * 512
            u2 = tj % 2
            mk.dma("sp", cosx[u2], io["cosx"][:, ts_:ts_ + 512],
                   sem_key=f"cosx{u2}", tag="cosx")
            mk.dma("sp", sinx[u2], io["sinx"][:, ts_:ts_ + 512],
                   sem_key=f"sinx{u2}", tag="sinx")
            for c in range(8):
                u = (c + tj) % 2
                p = PS_A[:, (u % 2) * 512:(u % 2) * 512 + 512]
                for dc in range(8):
                    mm(mk, p, wb[:, dc, c * 128:(c + 1) * 128],
                       xnf[:, dc, tj * 512:(tj + 1) * 512], start=(dc == 0), stop=(dc == 7))
                rb = raw32[u]
                copy(mk, "act", rb, p)
                mm(mk, PS_B[:, 0:512], permT[:], rb)
                tt(mk, "dve", rot, p, cosx[u2], ALU.mult)
                tt(mk, "dve", rot2, PS_B[:, 0:512], sinx[u2], ALU.mult)
                tt(mk, "dve", rot, rot, rot2, ALU.add)
                ts(mk, "dve", rot2, rot, 0.0, ALU.min)
                act(mk, rot2, rot2, AF.Exp)
                ts(mk, "dve", rot, rot, 0.0, ALU.max)
                tt(mk, "dve", phs[u], rot, rot2, ALU.add)
                mk.dma("sp", io[outh][c * 128:(c + 1) * 128, ts_:ts_ + 512], phs[u],
                       sem_key=f"{outh}{u}", tag=outh)

    wb = wbuf[0]
    mk.dma("pool", wb, io["wv1"], tag="wv1")
    vstage = af.alloc(2, 1056)
    for ti in range(T // 128):
        b = ti % 2
        p = PS_A[:, 0:512] if b == 0 else PS_A[:, 512:1024]
        stg = vstage[:, b, :].rearrange("p (h d) -> p h d", d=33)
        for o in range(2):
            for dc in range(8):
                mm(mk, p, xnf[:, dc, ti * 128:(ti + 1) * 128], wb[:, dc, o * 512:(o + 1) * 512],
                   start=(dc == 0), stop=(dc == 7))
            copy(mk, "dve" if b == 0 else "act",
                 stg[:, o * 16:(o + 1) * 16, 0:32],
                 p.rearrange("p (h d) -> p h d", d=32))
        memset(mk, "pool", stg[:, :, 32:33], 1.0)
        mk.dma("sp", io["v1"].rearrange("g (i p) c -> p i g c", p=128)[:, ti, :, :],
               vstage[:, b, :].rearrange("p (g c) -> p g c", c=132),
               sem_key=f"v1{b}", tag="v1store")

    wbm = ab.alloc(2, 8, 64)
    mk.dma("pool", wbm, io["wbm"], tag="wbm")
    bcol = mk.sbuf("l1_bcol", [64, 2], F32)
    mk.dma("sp", bcol[:], io["bbm_col"], tag="bbm")
    e2 = mk.sbuf("e2_sb", [64, 64], F32)
    mk.dma("sp", e2[:], io["e2"], tag="e2")
    beta = [af.alloc(512) for _ in range(2)]
    mixe = [af.alloc(512) for _ in range(2)]
    Pst = [af.alloc(512) for _ in range(2)]
    zer = af.alloc(128)
    memset(mk, "pool", zer, 0.0)
    for tj in range(T // 512):
        sl = slice(tj * 512, (tj + 1) * 512)
        u = tj % 2
        bt, mx, Ps = beta[u][0:64, :], mixe[u][0:64, :], Pst[u][0:64, :]
        for a in range(2):
            p = PS_E[0:64, :]
            for dc in range(8):
                mm(mk, p, wbm[:, a, dc, :], xnf[:, dc, sl],
                   start=(dc == 0), stop=(dc == 7))
            if a == 0:
                act(mk, bt, p, AF.Sigmoid, bias=bcol[:, 0:1])
                ts(mk, "dve", bt, bt, BMIN, ALU.max, BMAX, ALU.min)
            else:
                act(mk, mx, p, AF.Exp, bias=bcol[:, 1:2])
        mm(mk, PS_E[0:64, :], e2[:], mx)
        recip(mk, rot[0:64, :], PS_E[0:64, :])
        tt(mk, "dve", mx, mx, rot[0:64, :], ALU.mult)
        for cc in range(4):
            sc = slice(cc * 128, (cc + 1) * 128)
            scan(mk, Ps[:, sc], bt[:, sc], zer[0:64, 0:128], 1.0, ALU.mult, ALU.add)
        mk.dma("sp", io["mixh"][:, sl], mx, sem_key=f"mixh{u}", tag="mixh")
        mk.dma("sp", io["ph_"][:, sl], Ps, sem_key=f"ph{u}", tag="ph")


def ph_l1gla(mk, io, T):
    ab, af = io["ab"], io["af"]
    C = 128
    ncu = T // C
    onesf = io["onesf_sb"]
    PS_A, PS_B, PS_C, PS_D, PS_E, PS_F = (io["PS_A"], io["PS_B"], io["PS_C"],
                                          io["PS_D"], io["PS_E"], io["PS_F"])
    gmask = io["ub_sb"]
    ub = io["ub_sb"]
    bo32 = mk.sbuf("bo_sb", [128, 128], F32)
    mk.dma("sp", bo32[:], io["bo"], tag="bo")
    Pb = [af.alloc(128) for _ in range(2)]
    mxb = [af.alloc(128) for _ in range(2)]
    if io.get("dbg") is not None:
        io["_dbgs"] = [af.alloc(128) for _ in range(8)]
        io["_dbgs2"] = [af.alloc(128) for _ in range(5)]
    wout = ab.alloc(8, 1024)
    mk.dma("pool", wout, io["wout"], tag="wout")
    onesb = io["onesb_sb"]
    lnrow2 = mk.sbuf("lnrow2_sb", [1, 3, D], BF16)
    lnbc = af.alloc(2, 1024)
    for k in range(2):
        mk.dma("sp", lnbc[:, k, :],
               io["lnrows"][2 * k:2 * k + 1, :].broadcast_to((128, D)),
               sem_key="lnrow", tag="lnbc")
    mk.dma("pool", lnrow2[:, 0, :], io["lnrows"][1:2, :], sem_key="lnrow", tag="lnb")
    mk.dma("pool", lnrow2[:, 1, :], io["lnrows"][3:4, :], sem_key="lnrow", tag="fb")
    mk.dma("pool", lnrow2[:, 2, :], io["bout_row"], sem_key="lnrow", tag="bout")

    state = mk.sbuf("gla_state", [128, 8, 2, 33], F32)
    memset(mk, "dve", state[:], 0.0)
    io.setdefault("_st8_shared", None)

    _pq = af.alloc(8, C)
    _pk = af.alloc(8, C)
    _v1 = af.alloc(8, 132)
    _x1 = af.alloc(1024)
    _xn = af.alloc(1024)
    phq_c = [_pq, _pq]
    phk_c = [_pk, _pk]
    v1_c = [_v1, _v1]
    x1t = [_x1, _x1]
    xnt1 = [_xn, _xn]
    qt32_ = [af.alloc(C) for _ in range(2)]
    t1b32_ = [af.alloc(C) for _ in range(2)]
    zq32_ = [af.alloc(C) for _ in range(2)]
    kt32_ = [af.alloc(C) for _ in range(2)]
    _kT = af.alloc(C)
    ktT32_ = [_kT, _kT]
    A_sb_ = [af.alloc(4, C) for _ in range(2)]
    y_sb = ab.alloc(8, C)
    _wf = af.alloc(C)
    w_f_ = [_wf, _wf]
    ytmp = af.alloc(C)
    pend = af.alloc(2)
    st8 = io.get("_st8_shared")
    if st8 is None:
        st8 = mk.sbuf("gla_st8", [128, 8], F32)
    zst = af.alloc(1024)
    g1st = af.alloc(1024)
    _ou = af.alloc(1024)
    outst = [_ou, _ou]

    G4 = (PS_A[:, 0:128], PS_A[:, 512:640], PS_B[:, 0:128], PS_B[:, 512:640])

    for c in range(ncu):
        b = c % 2
        t0 = c * C
        for g in range(8):
            mk.dma("sp", phq_c[b][:, g, :], io["phq"][128 * g:128 * (g + 1), t0:t0 + C],
                   sem_key=f"phq{b}", tag="phq")
            mk.dma("sp", phk_c[b][:, g, :], io["phk"][128 * g:128 * (g + 1), t0:t0 + C],
                   sem_key=f"phk{b}", tag="phk")
        mk.dma("sp", v1_c[b], io["v1"].rearrange("g (i p) c -> p i g c", p=128)[:, c, :, :],
               sem_key=f"v1c{b}", tag="v1c")
        mk.dma("sp", x1t[b], io["x1"][t0:t0 + C, :], sem_key=f"x1t{b}", tag="x1t")
        mk.dma("sp", xnt1[b], io["xn1t"][t0:t0 + C, :], sem_key=f"xnt1{b}", tag="xnt1")

        for g in range(8):
            for m in range(2):
                ug = (g * 2 + m) % 2
                qt32, t1b32, zq32 = qt32_[ug], t1b32_[ug], zq32_[ug]
                kt32, ktT32, A_sb, w_f = kt32_[ug], ktT32_[ug], A_sb_[ug], w_f_[ug]
                ytmp2 = zq32
                r0 = m * 32 + g * 4
                mk.dma("sp", Pb[ug],
                       io["ph_"][r0:r0 + 4, t0:t0 + C].rearrange("r (u t) -> r u t", u=1)
                       .broadcast_to((4, 32, C)),
                       sem_key=f"pb{ug}", tag="pb")
                mk.dma("sp", mxb[ug],
                       io["mixh"][r0:r0 + 4, t0:t0 + C].rearrange("r (u t) -> r u t", u=1)
                       .broadcast_to((4, 32, C)),
                       sem_key=f"mxb{ug}", tag="mxb")
                copy(mk, "dve", pend[:, m:m + 1], Pb[ug][:, 127:128])
                tt(mk, "dve", qt32, phq_c[b][:, g, :], Pb[ug], ALU.mult)
                recip(mk, kt32, Pb[ug])
                tt(mk, "dve", kt32, phk_c[b][:, g, :], kt32, ALU.mult)
                transpose(mk, PS_F[:, 0:128], kt32, io["identf_sb"][:])
                copy(mk, "act", ktT32, PS_F[:, 0:128])
                for j in range(4):
                    mm(mk, G4[j], kt32[32 * j:32 * (j + 1), :], qt32[32 * j:32 * (j + 1), :],
                       tile_position=(32 * j, 0))
                for jj, ps_ in ((0, PS_A), (2, PS_B)):
                    for u in range(2):
                        tt(mk, "dve", A_sb[:, jj + u, :], ps_[:, u * 512:u * 512 + 128],
                           gmask[:], ALU.mult, tag="amask")
                for j in range(4):
                    mm(mk, PS_C[32 * j:32 * (j + 1), 0:128],
                       state[32 * j:32 * (j + 1), g, m, 0:32],
                       qt32[32 * j:32 * (j + 1), :],
                       start=True, stop=False, tile_position=(32 * j, 32 * j), skip=True)
                for j in range(4):
                    mm(mk, PS_C[32 * j:32 * (j + 1), 0:128],
                       v1_c[b][:, g, 33 * j:33 * j + 32],
                       A_sb[:, j, :], start=False, stop=True, tile_position=(0, 32 * j), skip=True)
                for j in range(4):
                    mm(mk, PS_D[32 * j:32 * (j + 1), 0:128],
                       ktT32[:, 32 * j:32 * (j + 1)], ub[:],
                       start=True, stop=True, tile_position=(0, 32 * j), skip=True)
                act(mk, t1b32, PS_D[:, 0:128], AF.Identity, bias=state[:, g, m, 32:33])
                tt(mk, "dve", zq32, qt32, t1b32, ALU.mult)
                mm(mk, PS_E[:, 128:256], bo32[:], zq32)
                ts(mk, "dve", w_f, PS_E[:, 128:256], GLA_EPS, ALU.add)
                recip(mk, w_f, w_f)
                tt(mk, "dve", w_f, w_f, mxb[ug], ALU.mult)
                if m == 0:
                    tt(mk, "dve", ytmp, PS_C[:, 0:128], w_f, ALU.mult)
                else:
                    tt(mk, "dve", ytmp2, PS_C[:, 0:128], w_f, ALU.mult)
                    tt(mk, "dve", y_sb[:, g, :], ytmp, ytmp2, ALU.add)
                if io.get("dbg") is not None and c == 0 and g == 7 and m == 0:
                    dm = io["dbg"]
                    for k_, t_ in ((0, qt), (1, kt), (2, A_sb[:, 0, :]), (3, A_sb[:, 1, :]),
                                   (4, A_sb[:, 2, :]), (5, A_sb[:, 3, :]), (6, w_f), (7, Pb[ug])):
                        copy(mk, "dve", io["_dbgs"][k_], t_)
                        mk.dma("sp", dm[k_, :, :], io["_dbgs"][k_], sem_key="dbg", tag="dbg")
                    copy(mk, "dve", io["_dbgs"][0], PS_C[:, 0:128])
                    mk.dma("sp", dm[13, :, :], io["_dbgs"][0], sem_key="dbg", tag="dbg")
                for j in range(4):
                    mm(mk, PS_E[32 * j:32 * (j + 1), 384:417],
                       ktT32[:, 32 * j:32 * (j + 1)],
                       v1_c[b][:, g, 33 * j:33 * j + 33],
                       start=True, stop=True, tile_position=(0, 32 * j), skip=True)
                tt(mk, "dve", state[:, g, m, :], state[:, g, m, :], PS_E[:, 384:417], ALU.add)
                ts(mk, "dve", state[:, g, m, :], state[:, g, m, :], pend[:, m:m + 1], ALU.mult)
        for n in range(2):
            for g in range(8):
                mm(mk, PS_A[:, n * 512:(n + 1) * 512], y_sb[:, g, :],
                   wout[:, g, n * 512:(n + 1) * 512], start=(g == 0), stop=False)
            mm(mk, PS_A[:, n * 512:(n + 1) * 512], onesb[0:1, :],
               lnrow2[:, 2, n * 512:(n + 1) * 512], start=False, stop=True)
        for n in range(2):
            tt(mk, "dve", zst[:, n * 512:(n + 1) * 512], PS_A[:, n * 512:(n + 1) * 512],
               xnt1[b][:, n * 512:(n + 1) * 512], ALU.add)
        reduce(mk, st8[:, 0:1], zst, ALU.add)
        ts(mk, "dve", st8[:, 1:2], st8[:, 0:1], 1.0 / D, ALU.mult)
        ts(mk, "dve", zst, zst, st8[:, 1:2], ALU.subtract)
        act(mk, g1st, zst, AF.Square, accum_out=st8[:, 2:3])
        act(mk, st8[:, 3:4], st8[:, 2:3], AF.Ln, scale=1.0 / D, bias=LN_EPS)
        act(mk, st8[:, 4:5], st8[:, 3:4], AF.Exp, scale=-0.5)
        ts(mk, "dve", zst, zst, st8[:, 4:5], ALU.mult)
        tt(mk, "dve", zst, zst, lnbc[:, 0, :], ALU.mult)
        for n in range(2):
            mm(mk, PS_B[:, n * 512:(n + 1) * 512], onesb[0:1, :],
               lnrow2[:, 0, n * 512:(n + 1) * 512])
            tt(mk, "dve", zst[:, n * 512:(n + 1) * 512], zst[:, n * 512:(n + 1) * 512],
               PS_B[:, n * 512:(n + 1) * 512], ALU.add)
        reduce(mk, st8[:, 5:6], xnt1[b], ALU.add)
        ts(mk, "dve", st8[:, 5:6], st8[:, 5:6], 1.0 / D, ALU.mult)
        ts(mk, "dve", g1st, xnt1[b], st8[:, 5:6], ALU.subtract)
        act(mk, outst[b], g1st, AF.Square, accum_out=st8[:, 6:7])
        act(mk, st8[:, 6:7], st8[:, 6:7], AF.Ln, scale=1.0 / D, bias=LN_EPS)
        act(mk, st8[:, 7:8], st8[:, 6:7], AF.Exp, scale=-0.5)
        ts(mk, "dve", g1st, g1st, st8[:, 7:8], ALU.mult)
        tt(mk, "dve", g1st, g1st, lnbc[:, 1, :], ALU.mult)
        for n in range(2):
            mm(mk, PS_B[:, n * 512:(n + 1) * 512], onesb[0:1, :],
               lnrow2[:, 1, n * 512:(n + 1) * 512])
            tt(mk, "dve", g1st[:, n * 512:(n + 1) * 512], g1st[:, n * 512:(n + 1) * 512],
               PS_B[:, n * 512:(n + 1) * 512], ALU.add)
        act(mk, g1st, g1st, AF.Silu)
        tt(mk, "dve", outst[b], zst, g1st, ALU.mult)
        tt(mk, "dve", outst[b], outst[b], x1t[b], ALU.add)
        mk.dma("sp", io["out"][t0:t0 + C, :], outst[b], sem_key=f"out{b}", tag="outstore")
        if io.get("dbg") is not None and c == 0:
            for k_, t_ in ((8, y_sb[:, 0, :]), (9, zst[:, 0:128]), (10, g1st[:, 0:128]),
                           (11, outst[b][:, 0:128]), (12, y_sb[:, 7, :])):
                copy(mk, "dve", io["_dbgs"][k_ - 8] if False else io["_dbgs2"][k_ - 8], t_)
                mk.dma("sp", io["dbg"][k_, :, :], io["_dbgs2"][k_ - 8], sem_key="dbg", tag="dbg")


def build_l1(T=4096, phases=("e", "e2", "f", "g"), ab_elems=50176, af_elems=21632, dbg=False):
    nc = bass.Bass("TRN2", target_bir_lowering=False)
    io = {}
    EI, EO = "ExternalInput", "ExternalOutput"
    mk = Mk(nc)
    _mk_shared(mk, nc, io, T, ab_elems=ab_elems, af_elems=af_elems)
    dt_ = nc.dram_tensor
    if "e" in phases:
        io["attnf"] = dt_("attnf", [D, T], BF16, kind=EI).ap()
        io["wm"] = dt_("wm", [128, 8, 1024], F32, kind=EI).ap()
        io["bm_row"] = dt_("bm_row", [1, D], F32, kind=EI).ap()
        io["g32"] = dt_("g32", [T, D], F32, kind=EI).ap()
        io["x"] = dt_("x", [T, D], F32, kind=EI).ap()
        io["x1"] = dt_("x1", [T, D], F32, kind=EO).ap()
    else:
        io["x1"] = dt_("x1", [T, D], F32, kind=EI).ap()
    if "e2" in phases:
        io["xnf"] = dt_("xnf1", [D, T], BF16, kind=EO).ap()
        io["xn1t"] = dt_("xn1t", [T, D], F32, kind=EO).ap()
    else:
        io["xnf"] = dt_("xnf1", [D, T], BF16, kind=EI).ap()
        io["xn1t"] = dt_("xn1t", [T, D], F32, kind=EI).ap()
    if "f" in phases:
        for nm in ("wq1", "wk1", "wv1"):
            io[nm] = dt_(nm, [128, 8, 1024], F32, kind=EI).ap()
        io["wbm"] = dt_("wbm", [128, 2, 8, 64], F32, kind=EI).ap()
        io["bbm_col"] = dt_("bbm_col", [64, 2], F32, kind=EI).ap()
        io["permT"] = dt_("permT", [128, 128], F32, kind=EI).ap()
        io["cosx"] = dt_("cosx", [128, T], F32, kind=EI).ap()
        io["sinx"] = dt_("sinx", [128, T], F32, kind=EI).ap()
        io["e2"] = dt_("e2c", [64, 64], F32, kind=EI).ap()
        io["phq"] = dt_("phq", [D, T], F32, kind=EO).ap()
        io["phk"] = dt_("phk", [D, T], F32, kind=EO).ap()
        io["v1"] = dt_("v1", [8, T, 132], F32, kind=EO).ap()
        io["mixh"] = dt_("mixh", [64, T], F32, kind=EO).ap()
        io["ph_"] = dt_("ph_", [64, T], F32, kind=EO).ap()
    else:
        io["phq"] = dt_("phq", [D, T], F32, kind=EI).ap()
        io["phk"] = dt_("phk", [D, T], F32, kind=EI).ap()
        io["v1"] = dt_("v1", [8, T, 132], F32, kind=EI).ap()
        io["mixh"] = dt_("mixh", [64, T], F32, kind=EI).ap()
        io["ph_"] = dt_("ph_", [64, T], F32, kind=EI).ap()
    if "g" in phases:
        io["bo"] = dt_("boc", [128, 128], F32, kind=EI).ap()
        io["wout"] = dt_("wout", [128, 8, 1024], F32, kind=EI).ap()
        io["bout_row"] = dt_("bout_row", [1, D], F32, kind=EI).ap()
        io["lnrows"] = dt_("lnrows", [4, D], F32, kind=EI).ap()
        io["out"] = dt_("outp", [T, D], F32, kind=EO).ap()
        if dbg:
            io["dbg"] = dt_("dbg", [16, 128, 128], F32, kind=EO).ap()

    if "e" in phases:
        ph_l0post(mk, io, T)
        io["ab"].reset(); io["af"].reset()
    if "e2" in phases:
        ph_ln0(mk, io, T, layer=1)
        io["xnf_sb"] = mk._xnf_sb
    if "f" in phases:
        if "e2" not in phases:
            xnf_sb = io["ab"].alloc(8, T)
            for cc in range(8):
                mk.dma("sp", xnf_sb[:, cc, :], io["xnf"][128 * cc:128 * (cc + 1), :],
                       sem_key="xnf1ld", tag="xnf1_load")
            io["xnf_sb"] = xnf_sb
        ph_l1prep(mk, io, T)
    if "g" in phases:
        io["ab"].reset(); io["af"].reset()
        ph_l1gla(mk, io, T)
    mk.emit_program()
    return nc



def _host_pack(inputs, T):
    f32 = np.float32
    pk = {}
    pack8 = lambda W: np.ascontiguousarray(W.reshape(8, 128, 1024).transpose(1, 0, 2)).astype(f32)
    packcol = lambda b: np.ascontiguousarray(b.reshape(8, 128).T).astype(f32)
    pk["wq"] = pack8(inputs["L0_attn_Wq"])
    pk["wk"] = pack8(inputs["L0_attn_Wk"])
    pk["wv"] = pack8(inputs["L0_attn_Wv"])
    pk["wm"] = pack8(inputs["L0_attn_Wm"])
    pk["bq"] = packcol(inputs["L0_attn_bq"])
    pk["bk"] = packcol(inputs["L0_attn_bk"])
    pk["bv"] = packcol(inputs["L0_attn_bv"])
    pk["bv_row"] = inputs["L0_attn_bv"].reshape(1, -1).astype(f32)
    pk["bm_row"] = inputs["L0_attn_bm"].reshape(1, -1).astype(f32)
    pk["w_router"] = inputs["L0_ffn_w_router"].astype(f32)
    w1, b1 = inputs["L0_ffn_w1"], inputs["L0_ffn_b1"]
    w2, b2 = inputs["L0_ffn_w2"], inputs["L0_ffn_b2"]
    pk["w1h"] = np.ascontiguousarray(w1.reshape(8, 8, 128, 4096).transpose(0, 2, 1, 3)).astype(f32)
    pk["w2h"] = np.ascontiguousarray(w2.reshape(8, 8, 4, 128, 1024).transpose(0, 3, 1, 2, 4)).astype(f32)
    pk["b1c"] = np.ascontiguousarray(b1.reshape(8, 32, 128).transpose(2, 0, 1)).astype(f32)
    pk["b2r"] = b2.astype(f32)
    packT = lambda W: np.ascontiguousarray(W.T.reshape(8, 128, 1024).transpose(1, 0, 2)).astype(f32)
    pk["wq1"] = packT(inputs["L1_attn_Wq"])
    pk["wk1"] = packT(inputs["L1_attn_Wk"])
    pk["wv1"] = packT(inputs["L1_attn_Wv"])
    pk["wout"] = packT(inputs["L1_attn_Wout"])
    perm = np.zeros(64, np.int64)
    for m in range(2):
        for g in range(8):
            for j in range(4):
                perm[m * 32 + g * 4 + j] = (4 * g + j) * 2 + m
    Wb_p = inputs["L1_attn_Wbeta"][perm]
    Wm_p = inputs["L1_attn_Wmix"][perm]
    pk["wbm"] = np.ascontiguousarray(
        np.stack([Wb_p.T.reshape(8, 128, 64).transpose(1, 0, 2),
                  Wm_p.T.reshape(8, 128, 64).transpose(1, 0, 2)], axis=1)).astype(f32)
    pk["bbm_col"] = np.stack([inputs["L1_attn_bbeta"][perm],
                              inputs["L1_attn_bmix"][perm]], 1).astype(f32)
    pk["bout_row"] = inputs["L1_attn_bout"].reshape(1, -1).astype(f32)
    pk["lnrows"] = np.stack([inputs["L1_attn_ln_g"], inputs["L1_attn_ln_b"],
                             inputs["L1_ffn_ln_g"], inputs["L1_ffn_ln_b"]]).astype(f32)
    pk["cident"] = np.eye(128, dtype=f32)
    kk = np.arange(128)[:, None]
    qq = np.arange(512)[None, :]
    dmask = np.stack([(kk + 128 * j <= qq).astype(f32) for j in range(4)])
    pk["dmask"] = np.ascontiguousarray(np.transpose(dmask, (1, 0, 2)))
    Pm = np.zeros((128, 128), f32)
    for p in range(128):
        blk, d_ = (p // 32) * 32, p % 32
        if d_ < 16:
            Pm[p, blk + d_ + 16] = -1.0
        else:
            Pm[p, blk + d_ - 16] = 1.0
    pk["permT"] = np.ascontiguousarray(Pm.T)
    invf = 1.0 / 10000.0 ** (np.arange(16, dtype=f32) / 16.0)
    t = np.arange(T, dtype=f32)
    lane = np.arange(128) % 16
    pk["cosx"] = np.cos(t[None, :] * invf[lane][:, None]).astype(f32)
    pk["sinx"] = np.sin(t[None, :] * invf[lane][:, None]).astype(f32)
    e2 = np.zeros((64, 64), f32)
    for r in range(64):
        e2[r, r % 32] = 1.0
        e2[r, r % 32 + 32] = 1.0
    pk["e2c"] = np.ascontiguousarray(e2)
    s_, t_ = np.arange(128)[:, None], np.arange(128)[None, :]
    pk["gmaskc"] = (s_ <= t_).astype(f32)
    pk["ubc"] = (s_ <= t_).astype(f32)
    bo = np.zeros((128, 128), f32)
    for p in range(128):
        bo[p, (p // 32) * 32:(p // 32) * 32 + 32] = 1.0
    pk["boc"] = bo
    return pk


_L0_KEYS = ("x", "w_router", "wq", "wk", "wv", "bq", "bk", "bv", "bv_row", "ubc",
            "w1h", "w2h", "b1c", "b2r", "cident")
_L1_KEYS = ("attnf", "wm", "bm_row", "g32", "x", "x1", "wq1", "wk1", "wv1", "wbm",
            "bbm_col", "permT", "cosx", "sinx", "e2c", "wout", "bout_row", "lnrows",
            "ubc", "boc", "cident")


def kernel_unfused(**inputs):
    T = 4096
    n = inputs["L0_attn_x"].shape[0]
    pk = _host_pack(inputs, T)
    xs = [np.ascontiguousarray(inputs["L0_attn_x"][b]).astype(np.float32) for b in range(n)]

    nc0 = build_l0(T, phases=("a", "b", "c", "d"))
    maps0 = []
    for b in range(n):
        m = {k: pk[k] for k in _L0_KEYS if k in pk}
        m["x"] = xs[b]
        maps0.append(m)
    res0 = run_bass_kernel_spmd(nc0, maps0, core_ids=list(range(n)))
    global _dbg_res0
    _dbg_res0 = res0.results

    nc1 = build_l1(T, phases=("e", "e2", "f", "g"))
    maps1 = []
    for b in range(n):
        m = {k: pk[k] for k in _L1_KEYS if k in pk}
        m["x"] = xs[b]
        m["attnf"] = res0.results[b]["attnf"]
        m["g32"] = res0.results[b]["g32"]
        maps1.append(m)
    res1 = run_bass_kernel_spmd(nc1, maps1, core_ids=list(range(n)))
    global _dbg_res1
    _dbg_res1 = res1.results
    out = np.stack([res1.results[b]["outp"] for b in range(n)], axis=0)
    return out


def build_full(T=4096, ab_elems=71680, af_elems=13312):
    nc = bass.Bass("TRN2", target_bir_lowering=False)
    io = {}
    EI = "ExternalInput"
    mk = Mk(nc)
    _mk_shared(mk, nc, io, T, ab_elems=ab_elems, af_elems=af_elems)
    dt_ = nc.dram_tensor
    io["x"] = dt_("x", [T, D], F32, kind=EI).ap()
    io["w_router"] = dt_("w_router", [D, E0], F32, kind=EI).ap()
    for nm in ("wq", "wk", "wv", "wm", "wq1", "wk1", "wv1", "wout"):
        io[nm] = dt_(nm, [128, 8, 1024], F32, kind=EI).ap()
    for nm in ("bq", "bk", "bv"):
        io[nm] = dt_(nm, [128, 8], F32, kind=EI).ap()
    io["bv_row"] = dt_("bv_row", [1, D], F32, kind=EI).ap()
    io["bm_row"] = dt_("bm_row", [1, D], F32, kind=EI).ap()
    io["w1h"] = dt_("w1h", [8, 128, 8, 4096], F32, kind=EI).ap()
    io["w2h"] = dt_("w2h", [8, 128, 8, 4, 1024], F32, kind=EI).ap()
    io["b1c"] = dt_("b1c", [128, 8, 32], F32, kind=EI).ap()
    io["b2r"] = dt_("b2r", [8, 1024], F32, kind=EI).ap()
    io["wbm"] = dt_("wbm", [128, 2, 8, 64], F32, kind=EI).ap()
    io["bbm_col"] = dt_("bbm_col", [64, 2], F32, kind=EI).ap()
    io["permT"] = dt_("permT", [128, 128], F32, kind=EI).ap()
    io["cosx"] = dt_("cosx", [128, T], F32, kind=EI).ap()
    io["sinx"] = dt_("sinx", [128, T], F32, kind=EI).ap()
    io["e2"] = dt_("e2c", [64, 64], F32, kind=EI).ap()
    io["bo"] = dt_("boc", [128, 128], F32, kind=EI).ap()
    io["bout_row"] = dt_("bout_row", [1, D], F32, kind=EI).ap()
    io["lnrows"] = dt_("lnrows", [4, D], F32, kind=EI).ap()
    IN = "Internal"
    xnf0 = dt_("i_xnf0", [D, T], BF16, kind=IN).ap()
    xnf1 = dt_("i_xnf1", [D, T], BF16, kind=IN).ap()
    io["msk"] = dt_("i_msk", [T, E0], F32, kind=IN).ap()
    io["qf"] = dt_("i_qf", [D, T], BF16, kind=IN).ap()
    io["kf"] = dt_("i_kf", [D, T], BF16, kind=IN).ap()
    io["v0"] = dt_("i_v0", [8, T, 130], BF16, kind=IN).ap()
    io["attnf"] = dt_("i_attnf", [D, T], BF16, kind=IN).ap()
    io["g32"] = dt_("i_g32", [T, D], F32, kind=IN).ap()
    io["x1"] = dt_("i_x1", [T, D], F32, kind=IN).ap()
    io["xn1t"] = dt_("i_xn1t", [T, D], F32, kind=IN).ap()
    io["phq"] = dt_("i_phq", [D, T], F32, kind=IN).ap()
    io["phk"] = dt_("i_phk", [D, T], F32, kind=IN).ap()
    io["v1"] = dt_("i_v1", [8, T, 132], F32, kind=IN).ap()
    io["mixh"] = dt_("i_mixh", [64, T], F32, kind=IN).ap()
    io["ph_"] = dt_("i_ph", [64, T], F32, kind=IN).ap()
    io["out"] = dt_("outp", [T, D], F32, kind="ExternalOutput").ap()

    ab, af = io["ab"], io["af"]
    io["xnf"] = xnf0
    ph_ln0(mk, io, T, layer=0)
    io["xnf_sb"] = mk._xnf_sb
    ph_l0proj(mk, io, T)
    ab.reset(); af.reset()
    ph_l0attn(mk, io, T)
    ab.reset(); af.reset()
    ph_l0moe(mk, io, T)
    ab.reset(); af.reset()
    ph_l0post(mk, io, T)
    ab.reset(); af.reset()
    io["xnf"] = xnf1
    ph_ln0(mk, io, T, layer=1)
    io["xnf_sb"] = mk._xnf_sb
    af.reset()
    ph_l1prep(mk, io, T)
    ab.reset(); af.reset()
    ph_l1gla(mk, io, T)
    mk.emit_program()
    return nc


_FULL_KEYS = ("x", "w_router", "wq", "wk", "wv", "wm", "bq", "bk", "bv", "bv_row",
              "bm_row", "w1h", "w2h", "b1c", "b2r", "wq1", "wk1", "wv1",
              "wout", "wbm", "bbm_col", "permT", "cosx", "sinx", "e2c",
              "ubc", "boc", "bout_row", "lnrows", "cident")


def kernel(**inputs):
    T = 4096
    n = inputs["L0_attn_x"].shape[0]
    pk = _host_pack(inputs, T)
    nc = build_full(T)
    maps = []
    for b in range(n):
        m = {k: pk[k] for k in _FULL_KEYS if k in pk}
        m["x"] = np.ascontiguousarray(inputs["L0_attn_x"][b]).astype(np.float32)
        maps.append(m)
    res = run_bass_kernel_spmd(nc, maps, core_ids=list(range(n)))
    return np.stack([res.results[b]["outp"] for b in range(n)], axis=0)


# version 10
# speedup vs baseline: 1.1823x; 1.1498x over previous
import math
from contextlib import ExitStack

import numpy as np
import concourse.bass as bass
import concourse.mybir as mybir
from concourse.bass_utils import run_bass_kernel_spmd

F32 = mybir.dt.float32
BF16 = mybir.dt.bfloat16
AF = mybir.ActivationFunctionType
ALU = mybir.AluOpType
AX = mybir.AxisListType


_ENG_ATTR = {"pe": "tensor", "dve": "vector", "act": "scalar", "pool": "gpsimd", "sp": "sync"}
_ENGINES = list(_ENG_ATTR)

_DT_SIZE = {"dt.float32": 4, "dt.bfloat16": 2, "dt.float16": 2, "dt.int32": 4,
            "dt.uint32": 4, "dt.int8": 1, "dt.uint8": 1, "dt.float32r": 4,
            "dt.int16": 2, "dt.uint16": 2, "dt.int64": 8, "dt.uint64": 8}


def _ap_region(ap):
    if not hasattr(ap, "tensor"):
        ap = ap[:]
    t = ap.tensor
    name = t.name
    tn = type(t).__name__
    if "PSum" in tn:
        space = "psum"
    elif "SB" in tn:
        space = "sbuf"
    else:
        return (name, "dram", 0, 1, 0, 1)
    shape = list(t.shape)
    esz = _DT_SIZE[str(t.dtype)]
    rowpitch = 1
    for s in shape[1:]:
        rowpitch *= s
    pairs = [list(p) for p in ap.ap]
    off = ap.offset
    pidx = None
    for i, (st, ct) in enumerate(pairs):
        if st == rowpitch and rowpitch > 1:
            pidx = i
            break
    if pidx is None:
        if shape[0] == 1 or rowpitch == 1:
            pidx = -1
        else:
            maxspan = off + sum(abs(st) * (ct - 1) for st, ct in pairs)
            if maxspan < rowpitch * (off // rowpitch + 1):
                pidx = -1
            else:
                return (name, space, 0, shape[0], 0, rowpitch * esz)
    plo = off // rowpitch
    f0 = off % rowpitch
    fspan = 0
    pcnt = 1
    for i, (st, ct) in enumerate(pairs):
        if i == pidx:
            pcnt = ct
        else:
            fspan += abs(st) * (ct - 1)
    if f0 + fspan >= rowpitch:
        return (name, space, 0, shape[0], 0, rowpitch * esz)
    blo, bhi = f0 * esz, (f0 + fspan + 1) * esz
    if space == "psum":
        blo = (blo // 2048) * 2048
        bhi = ((bhi + 2047) // 2048) * 2048
    return (name, space, plo, plo + pcnt, blo, bhi)


def _overlap(r1, r2):
    return (r1[0] == r2[0] and r1[2] < r2[3] and r2[2] < r1[3]
            and r1[4] < r2[5] and r2[4] < r1[5])


def _covers(w, r):
    return (w[0] == r[0] and w[2] <= r[2] and w[3] >= r[3]
            and w[4] <= r[4] and w[5] >= r[5])


class _Op:
    __slots__ = ("idx", "eng", "emit", "reads", "writes", "deps", "need_inc",
                 "is_dma", "sem_key", "seq", "tag")

    def __init__(self, idx, eng, emit, reads, writes, is_dma, sem_key, tag):
        self.idx = idx
        self.eng = eng
        self.emit = emit
        self.reads = reads
        self.writes = writes
        self.deps = []
        self.need_inc = is_dma
        self.is_dma = is_dma
        self.sem_key = sem_key
        self.seq = None
        self.tag = tag


class Mk:
    SERIALIZE = False

    def __init__(self, nc):
        self.nc = nc
        self.ops = []
        self.w_recs = {}
        self.r_recs = {}
        self._es = ExitStack()
        self.dma_counts = {}
        self._const_deps = []
        self._const_cols = {}

    def sbuf(self, name, shape, dtype):
        return self._es.enter_context(self.nc.sbuf_tensor(name, shape, dtype))

    def psum(self, name, shape, dtype=F32):
        return self._es.enter_context(self.nc.psum_tensor(name, shape, dtype))

    def dram(self, name, shape, dtype, kind="Internal"):
        return self.nc.dram_tensor(name, shape, dtype, kind=kind).ap()

    def init_consts(self, vals):
        vals = sorted(set(float(v) for v in vals) - {0.0, 1.0})
        if not vals:
            return
        n = len(vals)
        cf = self.sbuf("mk_consts_f32", [128, n], F32)
        cb = self.sbuf("mk_consts_bf16", [128, n], BF16)
        for i, v in enumerate(vals):
            self.nc.const_aps.register(F32, v, cf[:, i:i + 1])
            self.nc.const_aps.register(BF16, v, cb[:, i:i + 1])
            o1 = self.op("pool", lambda g, _cf=cf, _i=i, _v=v: g.memset(_cf[:, _i:_i + 1], _v),
                         writes=[cf[:, i:i + 1]], tag="const")
            o2 = self.op("pool", lambda g, _cb=cb, _i=i, _v=v: g.memset(_cb[:, _i:_i + 1], _v),
                         writes=[cb[:, i:i + 1]], tag="const")
            self._const_deps += [o1.idx, o2.idx]

    def _record(self, eng, emit, reads, writes, is_dma=False, sem_key=None, tag=""):
        idx = len(self.ops)
        rr = [_ap_region(a) for a in reads if a is not None]
        ww = [_ap_region(a) for a in writes if a is not None]
        op = _Op(idx, eng, emit, rr, ww, is_dma, sem_key, tag)
        deps = set()
        if not is_dma and tag != "const":
            deps.update(self._const_deps)
        for r in rr:
            for rec in self.w_recs.get(r[0], ()):
                if _overlap(r, rec[0]):
                    deps.add(rec[1])
            if r[1] == "psum" and eng in ("dve", "act"):
                other = "act" if eng == "dve" else "dve"
                rec = self.r_recs.get(r[0], {}).get(other)
                if rec is not None and _overlap(r, rec[0]):
                    deps.add(rec[1])
        for w in ww:
            for rec in self.w_recs.get(w[0], ()):
                if _overlap(w, rec[0]):
                    deps.add(rec[1])
            for rec in self.r_recs.get(w[0], {}).values():
                if _overlap(w, rec[0]):
                    deps.add(rec[1])
        if Mk.SERIALIZE and idx > 0:
            deps.add(idx - 1)
        op.deps = sorted(deps)
        key = sem_key if is_dma else eng
        for r in rr:
            d = self.r_recs.setdefault(r[0], {})
            if key in d:
                old = d[key][0]
                d[key] = [(r[0], r[1], min(old[2], r[2]), max(old[3], r[3]),
                           min(old[4], r[4]), max(old[5], r[5])), idx]
            else:
                d[key] = [r, idx]
        for w in ww:
            lst = self.w_recs.setdefault(w[0], [])
            lst[:] = [rec for rec in lst if not _covers(w, rec[0])]
            lst.append([w, idx])
            if len(lst) > 96:
                self._consolidate(lst)
            d = self.r_recs.get(w[0])
            if d:
                for k in list(d):
                    if _covers(w, d[k][0]):
                        del d[k]
        self.ops.append(op)
        return op

    def _consolidate(self, lst):
        byk = {}
        out = []
        half = lst[: len(lst) // 2]
        for rec in half:
            op = self.ops[rec[1]]
            k = op.sem_key if op.is_dma else op.eng
            if k in byk:
                o = byk[k]
                r0, r1 = o[0], rec[0]
                o[0] = (r0[0], r0[1], min(r0[2], r1[2]), max(r0[3], r1[3]),
                        min(r0[4], r1[4]), max(r0[5], r1[5]))
                o[1] = max(o[1], rec[1])
            else:
                byk[k] = rec
                out.append(rec)
        lst[:] = out + lst[len(lst) // 2:]

    def op(self, eng, emit, reads=(), writes=(), tag=""):
        return self._record(eng, emit, list(reads), list(writes), tag=tag)

    def dma(self, queue, out, in_, sem_key=None, tag="", **kw):
        if sem_key is None:
            sem_key = (queue, _ap_region(out)[0])
        else:
            sem_key = (queue, sem_key)
        emit = lambda e, _o=out, _i=in_, _k=dict(kw): e.dma_start(out=_o, in_=_i, **_k)
        return self._record(queue, emit, [in_], [out], is_dma=True, sem_key=sem_key, tag=tag)

    def emit_program(self):
        nc = self.nc
        ops = self.ops
        for op in ops:
            for d in op.deps:
                ops[d].need_inc = True
        eng_seq = {e: 0 for e in _ENGINES}
        cnt = {}
        for op in ops:
            if op.is_dma:
                cnt[op.sem_key] = cnt.get(op.sem_key, 0) + 1
                op.seq = cnt[op.sem_key]
            elif op.need_inc:
                eng_seq[op.eng] += 1
                op.seq = eng_seq[op.eng]
        self.dma_counts = cnt
        es = self._es
        eng_sems = {e: es.enter_context(nc.semaphore(f"se_{e}")) for e in _ENGINES}
        sem_keys = sorted(cnt, key=str)
        dma_sems = {k: es.enter_context(nc.semaphore(f"sd_{i}")) for i, k in enumerate(sem_keys)}

        per_eng = {e: [] for e in _ENGINES}
        for op in ops:
            per_eng[op.eng].append(op)

        block = es.enter_context(nc.Block())
        mkops = self.ops

        def make_body(ename, elist, final=False):
            def body(eng):
                clock = {}
                for op in elist:
                    if op.is_dma and op.seq >= 2:
                        kk = ("d", op.sem_key)
                        v = 16 * (op.seq - 1)
                        if clock.get(kk, -1) < v:
                            eng.wait_ge(dma_sems[op.sem_key], v)
                            clock[kk] = v
                    for d in op.deps:
                        p = mkops[d]
                        if (not p.is_dma) and p.eng == "pe" and ename == "pe":
                            continue
                        if p.is_dma:
                            kk, v, sem = ("d", p.sem_key), 16 * p.seq, dma_sems[p.sem_key]
                        else:
                            kk, v, sem = ("e", p.eng), p.seq, eng_sems[p.eng]
                        if clock.get(kk, -1) < v:
                            eng.wait_ge(sem, v)
                            clock[kk] = v
                    inst = op.emit(eng)
                    if op.is_dma:
                        inst.then_inc(dma_sems[op.sem_key], 16)
                    elif op.need_inc:
                        inst.then_inc(eng_sems[op.eng], 1)
                if final:
                    for k in sem_keys:
                        eng.wait_ge(dma_sems[k], 16 * cnt[k])
            return body

        for ename in _ENGINES:
            if per_eng[ename] or ename == "sp":
                body = make_body(ename, per_eng[ename], final=(ename == "sp"))
                getattr(block, _ENG_ATTR[ename])(body)
        es.close()
        return nc


def mm(mk, out, lhsT, rhs, start=True, stop=True, tile_position=None, tag="mm", skip=False):
    return mk.op("pe", lambda t, _o=out, _l=lhsT, _r=rhs, _s=start, _e=stop, _p=tile_position,
                 _k=skip:
                 t.matmul(_o, _l, _r, start=_s, stop=_e, tile_position=_p, skip_group_check=_k),
                 reads=[lhsT, rhs], writes=[out], tag=tag)


def transpose(mk, out, in_, ident, tag="tr"):
    return mk.op("pe", lambda t, _o=out, _i=in_, _d=ident: t.transpose(_o, _i, _d),
                 reads=[in_, ident], writes=[out], tag=tag)


def act(mk, out, in_, func, bias=None, scale=None, accum_out=None, tag="act"):
    kw = {}
    if bias is not None:
        kw["bias"] = bias
    if scale is not None:
        kw["scale"] = scale
    if accum_out is not None:
        kw["accum_out"] = accum_out
    reads = [in_] + ([bias] if hasattr(bias, "tensor") else [])
    writes = [out] + ([accum_out] if accum_out is not None else [])
    return mk.op("act", lambda s, _o=out, _i=in_, _f=func, _k=dict(kw):
                 s.activation(out=_o, in_=_i, func=_f, **_k), reads=reads, writes=writes, tag=tag)


def tt(mk, eng, out, in0, in1, op, tag="tt"):
    return mk.op(eng, lambda v, _o=out, _a=in0, _b=in1, _p=op: v.tensor_tensor(_o, _a, _b, _p),
                 reads=[in0, in1], writes=[out], tag=tag)


def ts(mk, eng, out, in0, s1, op0, s2=None, op1=None, tag="ts"):
    def emitfn(v, _o=out, _i=in0, _s1=s1, _s2=s2, _o0=op0, _o1=op1):
        if _s2 is None:
            return v.tensor_scalar(_o, _i, _s1, None, _o0)
        return v.tensor_scalar(_o, _i, _s1, _s2, _o0, _o1)
    reads = [in0] + [s for s in (s1, s2) if hasattr(s, "tensor")]
    return mk.op(eng, emitfn, reads=reads, writes=[out], tag=tag)


def copy(mk, eng, out, in_, tag="copy"):
    if eng == "act":
        return mk.op(eng, lambda s, _o=out, _i=in_: s.activation(out=_o, in_=_i, func=AF.Identity),
                     reads=[in_], writes=[out], tag=tag)
    return mk.op(eng, lambda v, _o=out, _i=in_: v.tensor_copy(_o, _i), reads=[in_], writes=[out], tag=tag)


def reduce(mk, out, in_, op, tag="red"):
    return mk.op("dve", lambda v, _o=out, _i=in_, _p=op:
                 v.tensor_reduce(_o, _i, AX.X, _p), reads=[in_], writes=[out], tag=tag)


def recip(mk, out, in_, tag="recip"):
    return mk.op("dve", lambda v, _o=out, _i=in_: v.reciprocal(_o, _i),
                 reads=[in_], writes=[out], tag=tag)


def recipf(mk, out, in_, tag="recipf"):
    return mk.op("dve", lambda v, _o=out, _i=in_: v.reciprocal_approx_fast(_o, _i),
                 reads=[in_], writes=[out], tag=tag)


def scan(mk, out, d0, d1, init, op0, op1, tag="scan"):
    return mk.op("dve", lambda v, _o=out, _a=d0, _b=d1, _n=init, _p=op0, _q=op1:
                 v.tensor_tensor_scan(_o, _a, _b, _n, _p, _q),
                 reads=[d0, d1], writes=[out], tag=tag)


def stt(mk, out, in0, scalar, in1, op0, op1, tag="stt"):
    reads = [in0, in1] + ([scalar] if hasattr(scalar, "tensor") else [])
    return mk.op("dve", lambda v, _o=out, _a=in0, _s=scalar, _b=in1, _p=op0, _q=op1:
                 v.scalar_tensor_tensor(_o, _a, _s, _b, _p, _q),
                 reads=reads, writes=[out], tag=tag)


def memset(mk, eng, ap_, val, tag="memset"):
    return mk.op(eng, lambda v, _o=ap_, _v=val: v.memset(_o, _v), writes=[ap_], tag=tag)



D = 1024
NH0, HS0 = 16, 64
NH1, HS1, M1 = 32, 32, 2
E0, F0 = 8, 4096
LN_EPS = 1e-5
SOFT_M = 40.0
GLA_EPS = 1e-6
BMIN, BMAX = 0.85, 0.9995


ALL_CONSTS = [1.0 / D, LN_EPS, -1.0, -0.5, -SOFT_M, 0.125, GLA_EPS, BMIN, BMAX,
              2.0, 1.0 / F0, 0.5]


def ph_ln0(mk, io, T, layer=0):
    nt = T // 128
    sfx = f"_l{layer}"
    xsrc = io["x"] if layer == 0 else io["x1"]
    x_t = io["af"].alloc(2, D)
    xc = io["af"].alloc(2, D)
    sq = io["af"].alloc(D)
    st = mk.sbuf(f"ln_st{sfx}", [128, 8], F32)
    io["_st8_shared"] = st
    xnt32 = io["af"].alloc(D)
    xnf = io["ab"].alloc(D // 128, T)
    ps_tr = io["PS_F"]
    if layer == 0:
        wr = mk.sbuf("ln_wr", [128, 8, E0], BF16)
        mk.dma("pool", wr[:], io["w_router"].rearrange("(c p) e -> p c e", p=128), tag="wr")
        ps_r = io["PS_E"][:, 0:E0]
        lg = mk.sbuf("ln_lg", [128, E0], F32)
        eq = mk.sbuf("ln_eq", [128, E0], F32)
        cum = mk.sbuf("ln_cum", [128, E0], F32)
        zer8 = mk.sbuf("ln_z8", [128, E0], F32)
        msk = mk.sbuf("ln_msk", [128, nt, E0], F32)
        io["_msk_sb"] = msk
        memset(mk, "pool", zer8[:], 0.0)

    for i in range(nt):
        xi = x_t[:, i % 2, :]
        xci = xc[:, i % 2, :]
        mk.dma("sp", xi, xsrc[i * 128:(i + 1) * 128, :], sem_key=f"x{sfx}{i % 2}", tag="xload")
        reduce(mk, st[:, 0:1], xi, ALU.add)
        ts(mk, "dve", st[:, 1:2], st[:, 0:1], 1.0 / D, ALU.mult)
        ts(mk, "dve", xci, xi, st[:, 1:2], ALU.subtract)
        act(mk, sq[:], xci, AF.Square, accum_out=st[:, 2:3])
        act(mk, st[:, 3:4], st[:, 2:3], AF.Ln, scale=1.0 / D, bias=LN_EPS)
        act(mk, st[:, 4:5], st[:, 3:4], AF.Exp, scale=-0.5)
        ts(mk, "dve", xnt32[:], xci, st[:, 4:5], ALU.mult)
        if layer == 1 and io.get("xn1t") is not None:
            mk.dma("sp", io["xn1t"][i * 128:(i + 1) * 128, :], xnt32[:],
                   sem_key=f"xn1t{i % 2}", tag="xn1t")
        for hb in range(2):
            for b2 in range(4):
                b = hb * 4 + b2
                transpose(mk, ps_tr[:, 128 * b2:128 * (b2 + 1)],
                          xnt32[:, 128 * b:128 * (b + 1)], io["identf_sb"][:])
            for b2 in range(4):
                b = hb * 4 + b2
                copy(mk, "dve" if b % 2 == 0 else "act", xnf[:, b, i * 128:(i + 1) * 128],
                     ps_tr[:, 128 * b2:128 * (b2 + 1)])
        if layer == 0:
            for c in range(8):
                mm(mk, ps_r, xnf[:, c, i * 128:(i + 1) * 128], wr[:, c, :],
                   start=(c == 0), stop=(c == 7))
            reduce(mk, st[:, 5:6], ps_r, ALU.max)
            ts(mk, "dve", st[:, 6:7], st[:, 5:6], -1.0, ALU.mult)
            act(mk, lg[:], ps_r, AF.Exp, bias=st[:, 6:7])
            reduce(mk, st[:, 7:8], lg[:], ALU.add)
            recip(mk, st[:, 7:8], st[:, 7:8])
            tt(mk, "dve", eq[:], ps_r, st[:, 5:6].broadcast_to((128, E0)), ALU.is_equal)
            scan(mk, cum[:], eq[:], zer8[:], 0.0, ALU.add, ALU.add)
            ts(mk, "dve", cum[:], cum[:], 1.0, ALU.is_equal)
            tt(mk, "dve", eq[:], eq[:], cum[:], ALU.mult)
            ts(mk, "dve", msk[:, i, :], eq[:], st[:, 7:8], ALU.mult)

    mk._xnf_sb = xnf
    if "xnf" in io and io["xnf"] is not None:
        for b in range(8):
            mk.dma("sp", io["xnf"][128 * b:128 * (b + 1), :], xnf[:, b, :],
                   sem_key=f"xnf{sfx}", tag="xnf_store")
    if layer == 0:
        mk.dma("sp", io["msk"].rearrange("(i p) e -> p i e", p=128), msk[:], tag="msk_store")


def build_phase_a(T=4096):
    nc = bass.Bass("TRN2", target_bir_lowering=False)
    io = {}
    mk = Mk(nc)
    io["x"] = nc.dram_tensor("x", [T, D], F32, kind="ExternalInput").ap()
    io["w_router"] = nc.dram_tensor("w_router", [D, E0], F32, kind="ExternalInput").ap()
    io["xnf"] = nc.dram_tensor("xnf", [D, T], BF16, kind="ExternalOutput").ap()
    io["msk"] = nc.dram_tensor("msk", [T, E0], F32, kind="ExternalOutput").ap()
    _mk_shared(mk, nc, io, T)
    ph_ln0(mk, io, T, layer=0)
    mk.emit_program()
    return nc



class Arena:
    def __init__(self, mk, name, n_elems, dtype):
        self.t = mk.sbuf(name, [128, n_elems], dtype)
        self.n = n_elems
        self.off = 0

    def reset(self):
        self.off = 0

    def alloc(self, *shape):
        n = 1
        for s in shape:
            n *= s
        assert self.off + n <= self.n, f"arena overflow: {self.off}+{n} > {self.n}"
        v = self.t[:, self.off:self.off + n]
        self.off += n
        if len(shape) > 1:
            dims = " ".join(f"d{i}" for i in range(len(shape)))
            v = v.rearrange(f"p ({dims}) -> p {dims}",
                            **{f"d{i}": s for i, s in enumerate(shape)})
        return v


def ph_l0proj(mk, io, T):
    ab, af = io["ab"], io["af"]
    xnf = io["xnf_sb"]
    wbuf = [ab.alloc(8, 1024) for _ in range(2)]
    bq = mk.sbuf("b_qkv", [128, 3, 8], F32)
    for j, nm in enumerate(("bq", "bk", "bv")):
        mk.dma("sp", bq[:, j, :], io[nm], tag="bias")
    onesb = io["onesb_sb"]
    bvrow = io["rowb_sb"]
    mk.dma("pool", bvrow[:], io["bv_row"], tag="bvrow")
    ps = io["PS_A"]
    ps2 = io["PS_B"]
    qstage = ab.alloc(2, 512)

    for j, (wname, out_h) in enumerate((("wq", "qf"), ("wk", "kf"))):
        wb = wbuf[j % 2]
        mk.dma("pool", wb, io[wname], tag=f"w_{wname}")
        for c in range(8):
            for tj in range(T // 512):
                p = ps[:, 0:512] if (c * (T // 512) + tj) % 2 == 0 else ps2[:, 0:512]
                for dc in range(8):
                    mm(mk, p, wb[:, dc, c * 128:(c + 1) * 128],
                       xnf[:, dc, tj * 512:(tj + 1) * 512], start=(dc == 0), stop=(dc == 7))
                stg = qstage[:, tj % 2, :]
                if tj % 2 == 0:
                    ts(mk, "dve", stg, p, bq[:, j, c:c + 1], ALU.add)
                else:
                    act(mk, stg, p, AF.Identity, bias=bq[:, j, c:c + 1])
                mk.dma("sp", io[out_h][c * 128:(c + 1) * 128, tj * 512:(tj + 1) * 512], stg,
                       sem_key=f"{out_h}{tj % 2}", tag=f"{out_h}_store")

    wb = wbuf[0]
    mk.dma("pool", wb, io["wv"], tag="w_wv")
    vstage = ab.alloc(2, 1040)
    for ti in range(T // 128):
        p = ps[:, 0:512] if ti % 2 == 0 else ps2[:, 0:512]
        stg = vstage[:, ti % 2, :].rearrange("p (h d) -> p h d", d=65)
        for o in range(2):
            for dc in range(8):
                mm(mk, p, xnf[:, dc, ti * 128:(ti + 1) * 128], wb[:, dc, o * 512:(o + 1) * 512],
                   start=(dc == 0), stop=False)
            mm(mk, p, onesb[:], bvrow[:, o * 512:(o + 1) * 512], start=False, stop=True)
            copy(mk, "dve" if ti % 2 == 0 else "act",
                 stg[:, o * 8:(o + 1) * 8, 0:64],
                 p.rearrange("p (h d) -> p h d", d=64))
        memset(mk, "pool", stg[:, :, 64:65], 1.0)
        mk.dma("sp", io["v0"].rearrange("h (i p) c -> p i h c", p=128)[:, ti, :, :],
               vstage[:, ti % 2, :].rearrange("p (h c) -> p h c", c=130),
               sem_key=f"v0{ti % 2}", tag="v0_store")


def ph_l0attn(mk, io, T):
    ab, af = io["ab"], io["af"]
    qp = [ab.alloc(T) for _ in range(2)]
    kp = [ab.alloc(T) for _ in range(2)]
    vp = [ab.alloc(T // 128, 130) for _ in range(2)]
    pt = [ab.alloc(4, 512) for _ in range(2)]
    stage = [ab.alloc(512) for _ in range(2)]
    rden = af.alloc(512)
    rdb = af.alloc(2, 512)
    onesf = io["onesf_sb"]
    PS_A, PS_B = io["PS_A"], io["PS_B"]
    PS_O = (io["PS_C"][0:65, :], io["PS_D"][0:65, :])
    PS_E = io["PS_E"]
    nq = T // 512
    ubb = io["ubb_sb"]

    for hp in range(8):
        b = hp % 2
        mk.dma("sp", qp[b], io["qf"][128 * hp:128 * (hp + 1), :], sem_key=f"qp{b}", tag="qload")
        mk.dma("sp", kp[b], io["kf"][128 * hp:128 * (hp + 1), :], sem_key=f"kp{b}", tag="kload")
        mk.dma("pool", vp[b], io["v0"].rearrange("h (i p) c -> p i h c", p=128)[:, :, hp, :],
               sem_key=f"vp{b}", tag="vload")
        for qb in range(nq):
            nk = 4 * (qb + 1)
            qA = qp[b][0:64, qb * 512:(qb + 1) * 512]
            qB = qp[b][64:128, qb * 512:(qb + 1) * 512]
            for kb in range(nk):
                pA = PS_A[:, (kb % 2) * 512:(kb % 2 + 1) * 512]
                pB = PS_B[:, (kb % 2) * 512:(kb % 2 + 1) * 512]
                mm(mk, pA, kp[b][0:64, kb * 128:(kb + 1) * 128], qA, tile_position=(0, 0))
                mm(mk, pB, kp[b][64:128, kb * 128:(kb + 1) * 128], qB, tile_position=(64, 0))
                if kb % 2 == 1:
                    for h, ps_ in ((0, PS_A), (1, PS_B)):
                        r = (kb - 1) % 4
                        act(mk, pt[h][:, r:r + 2, :].rearrange("p a t -> p (a t)"),
                            ps_[:], AF.Exp, scale=0.125, bias=-SOFT_M)
                        for kk in (kb - 1, kb):
                            j = kk - 4 * qb
                            if 0 <= j < 4:
                                sl_ = pt[h][:, kk % 4, :]
                                if j > 0:
                                    memset(mk, "dve", sl_[:, 0:128 * j], 0.0, tag="diagz")
                                tt(mk, "dve", sl_[:, 128 * j:128 * (j + 1)],
                                   sl_[:, 128 * j:128 * (j + 1)], ubb[:],
                                   ALU.mult, tag="diagmask")
                        for kk in (kb - 1, kb):
                            mm(mk, PS_O[h], vp[b][:, kk, 65 * h:65 * h + 65],
                               pt[h][:, kk % 4, :], start=(kk == 0), stop=(kk == nk - 1))
            for h in range(2):
                psO = PS_O[h]
                recip(mk, rden[64:65, :], psO[64:65, :])
                mm(mk, PS_E[0:65, :], onesf[64:65, 0:65], rden[64:65, :],
                   tile_position=(64, 0))
                copy(mk, "act", rdb[0:65, h, :], PS_E[0:65, :])
                tt(mk, "dve", stage[h][0:65, :], psO[0:65, :], rdb[0:65, h, :], ALU.mult)
                mk.dma("sp", io["attnf"][64 * (2 * hp + h):64 * (2 * hp + h + 1),
                                         qb * 512:(qb + 1) * 512],
                       stage[h][0:64, :], sem_key=f"at{h}", tag="attn_store")


def ph_l0moe(mk, io, T):
    ab, af = io["ab"], io["af"]
    nt = T // 256
    w1a = ab.alloc(8, 2048)
    w1b = ab.alloc(8, 2048)
    w2e = ab.alloc(8, 4, 1024)
    _x2 = ab.alloc(8, 256)
    xnt2 = [_x2, _x2]
    h_sb = [ab.alloc(4, 256) for _ in range(2)]
    ystage = [af.alloc(2, 1024) for _ in range(2)]
    b1c = mk.sbuf("moe_b1", [128, 8, 32], F32)
    b2r = io["rowb_sb"].rearrange("a (u d) -> a u d", u=1)
    onesb = io["onesb_sb"]
    if "_msk_sb" in io:
        mskb = io["_msk_sb"]
    else:
        mskb = mk.sbuf("moe_msk", [128, T // 128, 8], F32)
        mk.dma("sp", mskb[:], io["msk"].rearrange("(i p) e -> p i e", p=128), tag="mskload")
    mk.dma("sp", b1c[:], io["b1c"], tag="b1load")
    PS_Y = (io["PS_A"], io["PS_B"])
    PS_H4 = (io["PS_C"][:, 0:256], io["PS_E"][:, 0:256],
             io["PS_D"][:, 0:256], io["PS_F"][:, 0:256])
    h4 = tuple(h_sb[0][:, k, :] for k in range(4)) + tuple(h_sb[1][:, k, :] for k in range(4))

    def emit_h(e, xnf, w1a, w1b, fc):
        w1h = w1a if fc < 4 else w1b
        fo = (fc % 4) * 512
        for fs in range(4):
            ph = PS_H4[(fc % 2) * 2 + (fs % 2)]
            for dc in range(8):
                mm(mk, ph, w1h[:, dc, fo + fs * 128:fo + (fs + 1) * 128],
                   xnf[:, dc, :], start=(dc == 0), stop=(dc == 7))
            col = b1c[:, e, fc * 4 + fs:fc * 4 + fs + 1]
            hv = h4[(fc % 2) * 4 + fs]
            if fs % 2 == 0:
                ts(mk, "dve", hv, ph, col, ALU.add, 0.0, ALU.max, tag="relu")
            else:
                act(mk, hv, ph, AF.Relu, bias=col, tag="relu")

    def emit_y(e, fc):
        for fs in range(4):
            hv = h4[(fc % 2) * 4 + fs]
            for tsub in range(2):
                for n in range(2):
                    mm(mk, PS_Y[tsub][:, n * 512:(n + 1) * 512],
                       hv[:, tsub * 128:(tsub + 1) * 128],
                       w2e[:, fc, fs, n * 512:(n + 1) * 512],
                       start=(fc == 0 and fs == 0), stop=False)

    for e in range(8):
        mk.dma("pool", b2r[:, 0, :], io["b2r"][e:e + 1, :], sem_key="b2r", tag="b2load")
        mk.dma("pool", w1a, io["w1h"][e, :, :, 0:2048], sem_key="w1a", tag="w1a")
        mk.dma("pool", w1b, io["w1h"][e, :, :, 2048:4096], sem_key="w1b", tag="w1b")
        mk.dma("pool", w2e, io["w2h"][e, :, :, :, :], sem_key="w2e", tag="w2e")
        for i in range(nt):
            t0 = i * 256
            xnf = xnt2[i % 2]
            mk.dma("sp", xnf,
                   io["xnf"].rearrange("(c p) t -> p c t", p=128)[:, :, t0:t0 + 256],
                   sem_key=f"xnt2{i % 2}", tag="xnf_tile")
            emit_h(e, xnf, w1a, w1b, 0)
            for fc in range(8):
                if fc < 7:
                    emit_h(e, xnf, w1a, w1b, fc + 1)
                emit_y(e, fc)
            for tsub in range(2):
                for n in range(2):
                    mm(mk, PS_Y[tsub][:, n * 512:(n + 1) * 512], onesb[:],
                       b2r[:, 0, n * 512:(n + 1) * 512], start=False, stop=True)
                ts(mk, "dve", ystage[i % 2][:, tsub, :], PS_Y[tsub][:],
                   mskb[:, 2 * i + tsub, e:e + 1], ALU.mult, tag="ymask")
            kw = {} if e == 0 else {"accum_op": ALU.add}
            mk.dma("pool", io["g32"].rearrange("(i p) d -> p i d", p=128)[:, 2 * i:2 * i + 2, :],
                   ystage[i % 2][:], sem_key=f"g{i % 3}", tag="gaccum", **kw)


def _mk_shared(mk, nc, io, T, ab_elems=71680, af_elems=14336):
    mk.init_consts(ALL_CONSTS)
    io["ab"] = Arena(mk, "pool_bf16", ab_elems, BF16)
    io["af"] = Arena(mk, "pool_f32", af_elems, F32)
    io["cident"] = nc.dram_tensor("cident", [128, 128], F32, kind="ExternalInput").ap()
    identf = mk.sbuf("identf_sb", [128, 128], F32)
    mk.dma("sp", identf[:], io["cident"], tag="identf")
    io["identf_sb"] = identf
    onesf = mk.sbuf("onesf_sb", [128, 128], F32)
    memset(mk, "pool", onesf[:], 1.0)
    io["onesf_sb"] = onesf
    onesb = mk.sbuf("onesb_sb", [1, 128], BF16)
    memset(mk, "pool", onesb[:], 1.0)
    io["onesb_sb"] = onesb
    io["ubc"] = nc.dram_tensor("ubc", [128, 128], F32, kind="ExternalInput").ap()
    ubf = mk.sbuf("ub_sb", [128, 128], F32)
    mk.dma("sp", ubf[:], io["ubc"], tag="ubf")
    io["ub_sb"] = ubf
    ubb = mk.sbuf("ubb_sb", [128, 128], BF16)
    mk.dma("pool", ubb[:], io["ubc"], tag="ubb")
    io["ubb_sb"] = ubb
    io["rowb_sb"] = mk.sbuf("rowb_sb", [1, D], BF16)
    io["PS_A"] = mk.psum("PS_A", [128, 1024], F32)
    io["PS_B"] = mk.psum("PS_B", [128, 1024], F32)
    io["PS_C"] = mk.psum("PS_C", [128, 512], F32)
    io["PS_D"] = mk.psum("PS_D", [128, 512], F32)
    io["PS_E"] = mk.psum("PS_E", [128, 512], F32)
    io["PS_F"] = mk.psum("PS_F", [128, 512], F32)


def build_l0(T=4096, phases=("a", "b", "c"), ab_elems=71680, af_elems=6144):
    nc = bass.Bass("TRN2", target_bir_lowering=False)
    io = {}
    EI, EO = "ExternalInput", "ExternalOutput"
    mk = Mk(nc)
    _mk_shared(mk, nc, io, T, ab_elems=ab_elems, af_elems=af_elems)
    if "a" in phases:
        io["x"] = nc.dram_tensor("x", [T, D], F32, kind=EI).ap()
        io["w_router"] = nc.dram_tensor("w_router", [D, E0], F32, kind=EI).ap()
        io["xnf"] = nc.dram_tensor("xnf", [D, T], BF16, kind=EO).ap()
        io["msk"] = nc.dram_tensor("msk", [T, E0], F32, kind=EO).ap()
    else:
        io["xnf"] = nc.dram_tensor("xnf", [D, T], BF16, kind=EI).ap()
    if "b" in phases:
        for nm in ("wq", "wk", "wv"):
            io[nm] = nc.dram_tensor(nm, [128, 8, 1024], F32, kind=EI).ap()
        for nm in ("bq", "bk", "bv"):
            io[nm] = nc.dram_tensor(nm, [128, 8], F32, kind=EI).ap()
        io["bv_row"] = nc.dram_tensor("bv_row", [1, D], F32, kind=EI).ap()
        k_qkv = EO if "c" not in phases else EO
        io["qf"] = nc.dram_tensor("qf", [D, T], BF16, kind=k_qkv).ap()
        io["kf"] = nc.dram_tensor("kf", [D, T], BF16, kind=k_qkv).ap()
        io["v0"] = nc.dram_tensor("v0", [8, T, 130], BF16, kind=k_qkv).ap()
    else:
        io["qf"] = nc.dram_tensor("qf", [D, T], BF16, kind=EI).ap()
        io["kf"] = nc.dram_tensor("kf", [D, T], BF16, kind=EI).ap()
        io["v0"] = nc.dram_tensor("v0", [8, T, 130], BF16, kind=EI).ap()
    if "c" in phases:
        io["attnf"] = nc.dram_tensor("attnf", [D, T], BF16, kind=EO).ap()
    if "d" in phases:
        io["w1h"] = nc.dram_tensor("w1h", [8, 128, 8, 4096], F32, kind=EI).ap()
        io["w2h"] = nc.dram_tensor("w2h", [8, 128, 8, 4, 1024], F32, kind=EI).ap()
        io["b1c"] = nc.dram_tensor("b1c", [128, 8, 32], F32, kind=EI).ap()
        io["b2r"] = nc.dram_tensor("b2r", [8, 1024], F32, kind=EI).ap()
        if "a" not in phases:
            io["msk"] = nc.dram_tensor("msk", [T, E0], F32, kind=EI).ap()
        io["g32"] = nc.dram_tensor("g32", [T, D], F32, kind=EO).ap()

    if "a" in phases:
        ph_ln0(mk, io, T, layer=0)
        io["xnf_sb"] = mk._xnf_sb
    if "b" in phases:
        if "a" not in phases:
            xnf_sb = io["ab"].alloc(8, T)
            for c in range(8):
                mk.dma("sp", xnf_sb[:, c, :], io["xnf"][128 * c:128 * (c + 1), :],
                       sem_key="xnfld", tag="xnf_load")
            io["xnf_sb"] = xnf_sb
        ph_l0proj(mk, io, T)
    if "c" in phases:
        io["ab"].reset()
        io["af"].reset()
        ph_l0attn(mk, io, T)
    if "d" in phases:
        io["ab"].reset()
        io["af"].reset()
        ph_l0moe(mk, io, T)
    mk.emit_program()
    return nc


def ph_l0post(mk, io, T):
    ab, af = io["ab"], io["af"]
    at = [ab.alloc(8, 256) for _ in range(2)]
    wm = ab.alloc(8, 1024)
    gx = [af.alloc(2, 1024) for _ in range(2)]
    xx = [af.alloc(2, 1024) for _ in range(2)]
    x1s = [af.alloc(2, 1024) for _ in range(2)]
    bmrow = io["rowb_sb"]
    onesb = io["onesb_sb"]
    mk.dma("pool", wm, io["wm"], tag="wm")
    mk.dma("pool", bmrow[:], io["bm_row"], tag="bm")
    PS_Y = (io["PS_A"], io["PS_B"])
    nt = T // 256
    for i in range(nt):
        t0 = i * 256
        b = i % 2
        mk.dma("sp", at[b], io["attnf"].rearrange("(c p) t -> p c t", p=128)[:, :, t0:t0 + 256],
               sem_key=f"at{b}", tag="atload")
        mk.dma("sp", gx[b], io["g32"].rearrange("(i p) d -> p i d", p=128)[:, 2 * i:2 * i + 2, :],
               sem_key=f"gx{b}", tag="gload")
        mk.dma("sp", xx[b], io["x"].rearrange("(i p) d -> p i d", p=128)[:, 2 * i:2 * i + 2, :],
               sem_key=f"xx{b}", tag="xload")
        for tsub in range(2):
            for n in range(2):
                for dc in range(8):
                    mm(mk, PS_Y[tsub][:, n * 512:(n + 1) * 512],
                       at[b][:, dc, tsub * 128:(tsub + 1) * 128],
                       wm[:, dc, n * 512:(n + 1) * 512], start=(dc == 0), stop=False)
                mm(mk, PS_Y[tsub][:, n * 512:(n + 1) * 512], onesb[:],
                   bmrow[:, n * 512:(n + 1) * 512], start=False, stop=True)
            act(mk, gx[b][:, tsub, :], gx[b][:, tsub, :], AF.Silu)
            for n in range(2):
                sl_ = slice(n * 512, (n + 1) * 512)
                tt(mk, "dve", x1s[b][:, tsub, sl_], PS_Y[tsub][:, sl_],
                   gx[b][:, tsub, sl_], ALU.mult)
            tt(mk, "dve", x1s[b][:, tsub, :], x1s[b][:, tsub, :], xx[b][:, tsub, :], ALU.add)
        mk.dma("sp", io["x1"].rearrange("(i p) d -> p i d", p=128)[:, 2 * i:2 * i + 2, :],
               x1s[b], sem_key=f"x1{b}", tag="x1store")


def ph_l1prep(mk, io, T):
    ab, af = io["ab"], io["af"]
    xnf = io["xnf_sb"]
    wbuf = [ab.alloc(8, 1024) for _ in range(2)]
    permT = mk.sbuf("permT_sb", [128, 128], F32)
    mk.dma("sp", permT[:], io["permT"], tag="permT")
    cosx = [af.alloc(512) for _ in range(2)]
    sinx = [af.alloc(512) for _ in range(2)]
    raw32 = [af.alloc(512) for _ in range(2)]
    rot = af.alloc(512)
    rot2 = af.alloc(512)
    phs = [af.alloc(512) for _ in range(2)]
    PS_A, PS_B, PS_E, PS_F = io["PS_A"], io["PS_B"], io["PS_E"], io["PS_F"]

    for j, (wname, outh) in enumerate((("wq1", "phq"), ("wk1", "phk"))):
        wb = wbuf[j % 2]
        mk.dma("pool", wb, io[wname], tag=wname)
        for tj in range(T // 512):
            ts_ = tj * 512
            u2 = tj % 2
            mk.dma("sp", cosx[u2], io["cosx"][:, ts_:ts_ + 512],
                   sem_key=f"cosx{u2}", tag="cosx")
            mk.dma("sp", sinx[u2], io["sinx"][:, ts_:ts_ + 512],
                   sem_key=f"sinx{u2}", tag="sinx")
            for c in range(8):
                u = (c + tj) % 2
                p = PS_A[:, (u % 2) * 512:(u % 2) * 512 + 512]
                for dc in range(8):
                    mm(mk, p, wb[:, dc, c * 128:(c + 1) * 128],
                       xnf[:, dc, tj * 512:(tj + 1) * 512], start=(dc == 0), stop=(dc == 7))
                rb = raw32[u]
                copy(mk, "act", rb, p)
                mm(mk, PS_B[:, 0:512], permT[:], rb)
                tt(mk, "dve", rot, p, cosx[u2], ALU.mult)
                tt(mk, "dve", rot2, PS_B[:, 0:512], sinx[u2], ALU.mult)
                tt(mk, "dve", rot, rot, rot2, ALU.add)
                ts(mk, "dve", rot2, rot, 0.0, ALU.min)
                act(mk, rot2, rot2, AF.Exp)
                ts(mk, "dve", rot, rot, 0.0, ALU.max)
                tt(mk, "dve", phs[u], rot, rot2, ALU.add)
                mk.dma("sp", io[outh][c * 128:(c + 1) * 128, ts_:ts_ + 512], phs[u],
                       sem_key=f"{outh}{u}", tag=outh)

    wb = wbuf[0]
    mk.dma("pool", wb, io["wv1"], tag="wv1")
    vstage = af.alloc(2, 1056)
    for ti in range(T // 128):
        b = ti % 2
        p = PS_A[:, 0:512] if b == 0 else PS_A[:, 512:1024]
        stg = vstage[:, b, :].rearrange("p (h d) -> p h d", d=33)
        for o in range(2):
            for dc in range(8):
                mm(mk, p, xnf[:, dc, ti * 128:(ti + 1) * 128], wb[:, dc, o * 512:(o + 1) * 512],
                   start=(dc == 0), stop=(dc == 7))
            copy(mk, "dve" if b == 0 else "act",
                 stg[:, o * 16:(o + 1) * 16, 0:32],
                 p.rearrange("p (h d) -> p h d", d=32))
        memset(mk, "pool", stg[:, :, 32:33], 1.0)
        mk.dma("sp", io["v1"].rearrange("g (i p) c -> p i g c", p=128)[:, ti, :, :],
               vstage[:, b, :].rearrange("p (g c) -> p g c", c=132),
               sem_key=f"v1{b}", tag="v1store")

    wbm = ab.alloc(2, 8, 64)
    mk.dma("pool", wbm, io["wbm"], tag="wbm")
    bcol = mk.sbuf("l1_bcol", [64, 2], F32)
    mk.dma("sp", bcol[:], io["bbm_col"], tag="bbm")
    e2 = mk.sbuf("e2_sb", [64, 64], F32)
    mk.dma("sp", e2[:], io["e2"], tag="e2")
    beta = [af.alloc(512) for _ in range(2)]
    mixe = [af.alloc(512) for _ in range(2)]
    Pst = [af.alloc(512) for _ in range(2)]
    zer = af.alloc(128)
    memset(mk, "pool", zer, 0.0)
    for tj in range(T // 512):
        sl = slice(tj * 512, (tj + 1) * 512)
        u = tj % 2
        bt, mx, Ps = beta[u][0:64, :], mixe[u][0:64, :], Pst[u][0:64, :]
        for a in range(2):
            p = PS_E[0:64, :]
            for dc in range(8):
                mm(mk, p, wbm[:, a, dc, :], xnf[:, dc, sl],
                   start=(dc == 0), stop=(dc == 7))
            if a == 0:
                act(mk, bt, p, AF.Sigmoid, bias=bcol[:, 0:1])
                ts(mk, "dve", bt, bt, BMIN, ALU.max, BMAX, ALU.min)
            else:
                act(mk, mx, p, AF.Exp, bias=bcol[:, 1:2])
        mm(mk, PS_E[0:64, :], e2[:], mx)
        recip(mk, rot[0:64, :], PS_E[0:64, :])
        tt(mk, "dve", mx, mx, rot[0:64, :], ALU.mult)
        for cc in range(4):
            sc = slice(cc * 128, (cc + 1) * 128)
            scan(mk, Ps[:, sc], bt[:, sc], zer[0:64, 0:128], 1.0, ALU.mult, ALU.add)
        mk.dma("sp", io["mixh"][:, sl], mx, sem_key=f"mixh{u}", tag="mixh")
        mk.dma("sp", io["ph_"][:, sl], Ps, sem_key=f"ph{u}", tag="ph")
        recip(mk, bt, Ps)
        mk.dma("sp", io["iph"][:, sl], bt, sem_key=f"iph{u}", tag="iph")


def ph_l1gla(mk, io, T):
    ab, af = io["ab"], io["af"]
    C = 128
    ncu = T // C
    onesf = io["onesf_sb"]
    PS_A, PS_B, PS_C, PS_D, PS_E, PS_F = (io["PS_A"], io["PS_B"], io["PS_C"],
                                          io["PS_D"], io["PS_E"], io["PS_F"])
    gmask = io["ub_sb"]
    ub = io["ub_sb"]
    bo32 = mk.sbuf("bo_sb", [128, 128], F32)
    mk.dma("sp", bo32[:], io["bo"], tag="bo")
    Pb = [af.alloc(128) for _ in range(2)]
    iPb = [af.alloc(128) for _ in range(2)]
    _mx1 = af.alloc(128)
    mxb = [_mx1, _mx1]
    if io.get("dbg") is not None:
        io["_dbgs"] = [af.alloc(128) for _ in range(8)]
        io["_dbgs2"] = [af.alloc(128) for _ in range(5)]
    wout = ab.alloc(8, 1024)
    mk.dma("pool", wout, io["wout"], tag="wout")
    onesb = io["onesb_sb"]
    lnrow2 = mk.sbuf("lnrow2_sb", [1, 3, D], BF16)
    lnbc = af.alloc(2, 1024)
    for k in range(2):
        mk.dma("sp", lnbc[:, k, :],
               io["lnrows"][2 * k:2 * k + 1, :].broadcast_to((128, D)),
               sem_key="lnrow", tag="lnbc")
    mk.dma("pool", lnrow2[:, 0, :], io["lnrows"][1:2, :], sem_key="lnrow", tag="lnb")
    mk.dma("pool", lnrow2[:, 1, :], io["lnrows"][3:4, :], sem_key="lnrow", tag="fb")
    mk.dma("pool", lnrow2[:, 2, :], io["bout_row"], sem_key="lnrow", tag="bout")

    state = mk.sbuf("gla_state", [128, 8, 2, 33], F32)
    memset(mk, "dve", state[:], 0.0)
    io.setdefault("_st8_shared", None)

    _pq = af.alloc(8, C)
    _pk = af.alloc(8, C)
    _v1 = af.alloc(8, 132)
    _x1 = af.alloc(1024)
    _xn = af.alloc(1024)
    phq_c = [_pq, _pq]
    phk_c = [_pk, _pk]
    v1_c = [_v1, _v1]
    x1t = [_x1, _x1]
    xnt1 = [_xn, _xn]
    qt32_ = [af.alloc(C) for _ in range(2)]
    _t1 = af.alloc(C)
    t1b32_ = [_t1, _t1]
    zq32_ = [af.alloc(C) for _ in range(2)]
    kt32_ = [af.alloc(C) for _ in range(2)]
    _kT = af.alloc(C)
    ktT32_ = [_kT, _kT]
    A_sb_ = [af.alloc(4, C) for _ in range(2)]
    y_sb = ab.alloc(8, C)
    _wf = af.alloc(C)
    w_f_ = [_wf, _wf]
    ytmp = af.alloc(C)
    pend = af.alloc(2)
    st8 = io.get("_st8_shared")
    if st8 is None:
        st8 = mk.sbuf("gla_st8", [128, 8], F32)
    zst = af.alloc(1024)
    g1st = af.alloc(1024)
    _ou = af.alloc(1024)
    outst = [_ou, _ou]

    G4 = (PS_A[:, 0:128], PS_A[:, 512:640], PS_C[:, 0:128], PS_D[:, 0:128])

    for c in range(ncu):
        b = c % 2
        t0 = c * C
        for g in range(8):
            mk.dma("sp", phq_c[b][:, g, :], io["phq"][128 * g:128 * (g + 1), t0:t0 + C],
                   sem_key=f"phq{b}", tag="phq")
            mk.dma("sp", phk_c[b][:, g, :], io["phk"][128 * g:128 * (g + 1), t0:t0 + C],
                   sem_key=f"phk{b}", tag="phk")
        mk.dma("sp", v1_c[b], io["v1"].rearrange("g (i p) c -> p i g c", p=128)[:, c, :, :],
               sem_key=f"v1c{b}", tag="v1c")
        mk.dma("sp", x1t[b], io["x1"][t0:t0 + C, :], sem_key=f"x1t{b}", tag="x1t")
        mk.dma("sp", xnt1[b], io["xn1t"][t0:t0 + C, :], sem_key=f"xnt1{b}", tag="xnt1")

        for g in range(8):
            for m in range(2):
                ug = (g * 2 + m) % 2
                qt32, t1b32, zq32 = qt32_[ug], t1b32_[ug], zq32_[ug]
                kt32, ktT32, A_sb, w_f = kt32_[ug], ktT32_[ug], A_sb_[ug], w_f_[ug]
                ytmp2 = zq32
                r0 = m * 32 + g * 4
                NUM = PS_B[:, 512 * ug:512 * ug + 128]
                KCD = (PS_C if ug == 0 else PS_D)
                DEN = (PS_E if ug == 0 else PS_F)
                mk.dma("sp", Pb[ug],
                       io["ph_"][r0:r0 + 4, t0:t0 + C].rearrange("r (u t) -> r u t", u=1)
                       .broadcast_to((4, 32, C)),
                       sem_key=f"pb{ug}", tag="pb")
                mk.dma("sp", iPb[ug],
                       io["iph"][r0:r0 + 4, t0:t0 + C].rearrange("r (u t) -> r u t", u=1)
                       .broadcast_to((4, 32, C)),
                       sem_key=f"ipb{ug}", tag="ipb")
                mk.dma("sp", mxb[ug],
                       io["mixh"][r0:r0 + 4, t0:t0 + C].rearrange("r (u t) -> r u t", u=1)
                       .broadcast_to((4, 32, C)),
                       sem_key=f"mxb{ug}", tag="mxb")
                copy(mk, "dve", pend[:, m:m + 1], Pb[ug][:, 127:128])
                tt(mk, "dve", qt32, phq_c[b][:, g, :], Pb[ug], ALU.mult)
                tt(mk, "dve", kt32, phk_c[b][:, g, :], iPb[ug], ALU.mult)
                transpose(mk, KCD[:, 256:384], kt32, io["identf_sb"][:])
                copy(mk, "act", ktT32, KCD[:, 256:384])
                for j in range(4):
                    mm(mk, G4[j], kt32[32 * j:32 * (j + 1), :], qt32[32 * j:32 * (j + 1), :],
                       tile_position=(32 * j, 0), skip=True)
                for j in range(4):
                    tt(mk, "dve", A_sb[:, j, :], G4[j], gmask[:], ALU.mult, tag="amask")
                for j in range(4):
                    mm(mk, NUM[32 * j:32 * (j + 1), :],
                       state[32 * j:32 * (j + 1), g, m, 0:32],
                       qt32[32 * j:32 * (j + 1), :],
                       start=True, stop=False, tile_position=(32 * j, 32 * j), skip=True)
                for j in range(4):
                    mm(mk, NUM[32 * j:32 * (j + 1), :],
                       v1_c[b][:, g, 33 * j:33 * j + 32],
                       A_sb[:, j, :], start=False, stop=True, tile_position=(0, 32 * j), skip=True)
                for j in range(4):
                    mm(mk, KCD[32 * j:32 * (j + 1), 128:256],
                       ktT32[:, 32 * j:32 * (j + 1)], ub[:],
                       start=True, stop=True, tile_position=(0, 32 * j), skip=True)
                act(mk, t1b32, KCD[:, 128:256], AF.Identity, bias=state[:, g, m, 32:33])
                tt(mk, "dve", zq32, qt32, t1b32, ALU.mult)
                mm(mk, DEN[:, 0:128], bo32[:], zq32)
                ts(mk, "dve", w_f, DEN[:, 0:128], GLA_EPS, ALU.add)
                recip(mk, w_f, w_f)
                tt(mk, "dve", w_f, w_f, mxb[ug], ALU.mult)
                if m == 0:
                    tt(mk, "dve", ytmp, NUM[:, 0:128], w_f, ALU.mult)
                else:
                    tt(mk, "dve", ytmp2, NUM[:, 0:128], w_f, ALU.mult)
                    tt(mk, "dve", y_sb[:, g, :], ytmp, ytmp2, ALU.add)
                for j in range(4):
                    mm(mk, DEN[32 * j:32 * (j + 1), 384:417],
                       ktT32[:, 32 * j:32 * (j + 1)],
                       v1_c[b][:, g, 33 * j:33 * j + 33],
                       start=True, stop=True, tile_position=(0, 32 * j), skip=True)
                tt(mk, "dve", state[:, g, m, :], state[:, g, m, :], DEN[:, 384:417], ALU.add)
                ts(mk, "dve", state[:, g, m, :], state[:, g, m, :], pend[:, m:m + 1], ALU.mult)
                if io.get("dbg") is not None and c == 0 and g == 7 and m == 0:
                    dm = io["dbg"]
                    for k_, t_ in ((0, qt), (1, kt), (2, A_sb[:, 0, :]), (3, A_sb[:, 1, :]),
                                   (4, A_sb[:, 2, :]), (5, A_sb[:, 3, :]), (6, w_f), (7, Pb[ug])):
                        copy(mk, "dve", io["_dbgs"][k_], t_)
                        mk.dma("sp", dm[k_, :, :], io["_dbgs"][k_], sem_key="dbg", tag="dbg")
                    copy(mk, "dve", io["_dbgs"][0], PS_C[:, 0:128])
                    mk.dma("sp", dm[13, :, :], io["_dbgs"][0], sem_key="dbg", tag="dbg")
        for n in range(2):
            for g in range(8):
                mm(mk, PS_A[:, n * 512:(n + 1) * 512], y_sb[:, g, :],
                   wout[:, g, n * 512:(n + 1) * 512], start=(g == 0), stop=False)
            mm(mk, PS_A[:, n * 512:(n + 1) * 512], onesb[0:1, :],
               lnrow2[:, 2, n * 512:(n + 1) * 512], start=False, stop=True)
        for n in range(2):
            tt(mk, "dve", zst[:, n * 512:(n + 1) * 512], PS_A[:, n * 512:(n + 1) * 512],
               xnt1[b][:, n * 512:(n + 1) * 512], ALU.add)
        reduce(mk, st8[:, 0:1], zst, ALU.add)
        ts(mk, "dve", st8[:, 1:2], st8[:, 0:1], 1.0 / D, ALU.mult)
        ts(mk, "dve", zst, zst, st8[:, 1:2], ALU.subtract)
        act(mk, g1st, zst, AF.Square, accum_out=st8[:, 2:3])
        act(mk, st8[:, 3:4], st8[:, 2:3], AF.Ln, scale=1.0 / D, bias=LN_EPS)
        act(mk, st8[:, 4:5], st8[:, 3:4], AF.Exp, scale=-0.5)
        ts(mk, "dve", zst, zst, st8[:, 4:5], ALU.mult)
        tt(mk, "dve", zst, zst, lnbc[:, 0, :], ALU.mult)
        for n in range(2):
            mm(mk, PS_B[:, n * 512:(n + 1) * 512], onesb[0:1, :],
               lnrow2[:, 0, n * 512:(n + 1) * 512])
            tt(mk, "dve", zst[:, n * 512:(n + 1) * 512], zst[:, n * 512:(n + 1) * 512],
               PS_B[:, n * 512:(n + 1) * 512], ALU.add)
        reduce(mk, st8[:, 5:6], xnt1[b], ALU.add)
        ts(mk, "dve", st8[:, 5:6], st8[:, 5:6], 1.0 / D, ALU.mult)
        ts(mk, "dve", g1st, xnt1[b], st8[:, 5:6], ALU.subtract)
        act(mk, outst[b], g1st, AF.Square, accum_out=st8[:, 6:7])
        act(mk, st8[:, 6:7], st8[:, 6:7], AF.Ln, scale=1.0 / D, bias=LN_EPS)
        act(mk, st8[:, 7:8], st8[:, 6:7], AF.Exp, scale=-0.5)
        ts(mk, "dve", g1st, g1st, st8[:, 7:8], ALU.mult)
        tt(mk, "dve", g1st, g1st, lnbc[:, 1, :], ALU.mult)
        for n in range(2):
            mm(mk, PS_B[:, n * 512:(n + 1) * 512], onesb[0:1, :],
               lnrow2[:, 1, n * 512:(n + 1) * 512])
            tt(mk, "dve", g1st[:, n * 512:(n + 1) * 512], g1st[:, n * 512:(n + 1) * 512],
               PS_B[:, n * 512:(n + 1) * 512], ALU.add)
        act(mk, g1st, g1st, AF.Silu)
        tt(mk, "dve", outst[b], zst, g1st, ALU.mult)
        tt(mk, "dve", outst[b], outst[b], x1t[b], ALU.add)
        mk.dma("sp", io["out"][t0:t0 + C, :], outst[b], sem_key=f"out{b}", tag="outstore")
        if io.get("dbg") is not None and c == 0:
            for k_, t_ in ((8, y_sb[:, 0, :]), (9, zst[:, 0:128]), (10, g1st[:, 0:128]),
                           (11, outst[b][:, 0:128]), (12, y_sb[:, 7, :])):
                copy(mk, "dve", io["_dbgs"][k_ - 8] if False else io["_dbgs2"][k_ - 8], t_)
                mk.dma("sp", io["dbg"][k_, :, :], io["_dbgs2"][k_ - 8], sem_key="dbg", tag="dbg")


def build_l1(T=4096, phases=("e", "e2", "f", "g"), ab_elems=50176, af_elems=21632, dbg=False):
    nc = bass.Bass("TRN2", target_bir_lowering=False)
    io = {}
    EI, EO = "ExternalInput", "ExternalOutput"
    mk = Mk(nc)
    _mk_shared(mk, nc, io, T, ab_elems=ab_elems, af_elems=af_elems)
    dt_ = nc.dram_tensor
    if "e" in phases:
        io["attnf"] = dt_("attnf", [D, T], BF16, kind=EI).ap()
        io["wm"] = dt_("wm", [128, 8, 1024], F32, kind=EI).ap()
        io["bm_row"] = dt_("bm_row", [1, D], F32, kind=EI).ap()
        io["g32"] = dt_("g32", [T, D], F32, kind=EI).ap()
        io["x"] = dt_("x", [T, D], F32, kind=EI).ap()
        io["x1"] = dt_("x1", [T, D], F32, kind=EO).ap()
    else:
        io["x1"] = dt_("x1", [T, D], F32, kind=EI).ap()
    if "e2" in phases:
        io["xnf"] = dt_("xnf1", [D, T], BF16, kind=EO).ap()
        io["xn1t"] = dt_("xn1t", [T, D], F32, kind=EO).ap()
    else:
        io["xnf"] = dt_("xnf1", [D, T], BF16, kind=EI).ap()
        io["xn1t"] = dt_("xn1t", [T, D], F32, kind=EI).ap()
    if "f" in phases:
        for nm in ("wq1", "wk1", "wv1"):
            io[nm] = dt_(nm, [128, 8, 1024], F32, kind=EI).ap()
        io["wbm"] = dt_("wbm", [128, 2, 8, 64], F32, kind=EI).ap()
        io["bbm_col"] = dt_("bbm_col", [64, 2], F32, kind=EI).ap()
        io["permT"] = dt_("permT", [128, 128], F32, kind=EI).ap()
        io["cosx"] = dt_("cosx", [128, T], F32, kind=EI).ap()
        io["sinx"] = dt_("sinx", [128, T], F32, kind=EI).ap()
        io["e2"] = dt_("e2c", [64, 64], F32, kind=EI).ap()
        io["phq"] = dt_("phq", [D, T], F32, kind=EO).ap()
        io["phk"] = dt_("phk", [D, T], F32, kind=EO).ap()
        io["v1"] = dt_("v1", [8, T, 132], F32, kind=EO).ap()
        io["mixh"] = dt_("mixh", [64, T], F32, kind=EO).ap()
        io["ph_"] = dt_("ph_", [64, T], F32, kind=EO).ap()
        io["iph"] = dt_("iph", [64, T], F32, kind=EO).ap()
    else:
        io["phq"] = dt_("phq", [D, T], F32, kind=EI).ap()
        io["phk"] = dt_("phk", [D, T], F32, kind=EI).ap()
        io["v1"] = dt_("v1", [8, T, 132], F32, kind=EI).ap()
        io["mixh"] = dt_("mixh", [64, T], F32, kind=EI).ap()
        io["ph_"] = dt_("ph_", [64, T], F32, kind=EI).ap()
        io["iph"] = dt_("iph", [64, T], F32, kind=EI).ap()
    if "g" in phases:
        io["bo"] = dt_("boc", [128, 128], F32, kind=EI).ap()
        io["wout"] = dt_("wout", [128, 8, 1024], F32, kind=EI).ap()
        io["bout_row"] = dt_("bout_row", [1, D], F32, kind=EI).ap()
        io["lnrows"] = dt_("lnrows", [4, D], F32, kind=EI).ap()
        io["out"] = dt_("outp", [T, D], F32, kind=EO).ap()
        if dbg:
            io["dbg"] = dt_("dbg", [16, 128, 128], F32, kind=EO).ap()

    if "e" in phases:
        ph_l0post(mk, io, T)
        io["ab"].reset(); io["af"].reset()
    if "e2" in phases:
        ph_ln0(mk, io, T, layer=1)
        io["xnf_sb"] = mk._xnf_sb
    if "f" in phases:
        if "e2" not in phases:
            xnf_sb = io["ab"].alloc(8, T)
            for cc in range(8):
                mk.dma("sp", xnf_sb[:, cc, :], io["xnf"][128 * cc:128 * (cc + 1), :],
                       sem_key="xnf1ld", tag="xnf1_load")
            io["xnf_sb"] = xnf_sb
        ph_l1prep(mk, io, T)
    if "g" in phases:
        io["ab"].reset(); io["af"].reset()
        ph_l1gla(mk, io, T)
    mk.emit_program()
    return nc



def _host_pack(inputs, T):
    f32 = np.float32
    pk = {}
    pack8 = lambda W: np.ascontiguousarray(W.reshape(8, 128, 1024).transpose(1, 0, 2)).astype(f32)
    packcol = lambda b: np.ascontiguousarray(b.reshape(8, 128).T).astype(f32)
    pk["wq"] = pack8(inputs["L0_attn_Wq"])
    pk["wk"] = pack8(inputs["L0_attn_Wk"])
    pk["wv"] = pack8(inputs["L0_attn_Wv"])
    pk["wm"] = pack8(inputs["L0_attn_Wm"])
    pk["bq"] = packcol(inputs["L0_attn_bq"])
    pk["bk"] = packcol(inputs["L0_attn_bk"])
    pk["bv"] = packcol(inputs["L0_attn_bv"])
    pk["bv_row"] = inputs["L0_attn_bv"].reshape(1, -1).astype(f32)
    pk["bm_row"] = inputs["L0_attn_bm"].reshape(1, -1).astype(f32)
    pk["w_router"] = inputs["L0_ffn_w_router"].astype(f32)
    w1, b1 = inputs["L0_ffn_w1"], inputs["L0_ffn_b1"]
    w2, b2 = inputs["L0_ffn_w2"], inputs["L0_ffn_b2"]
    pk["w1h"] = np.ascontiguousarray(w1.reshape(8, 8, 128, 4096).transpose(0, 2, 1, 3)).astype(f32)
    pk["w2h"] = np.ascontiguousarray(w2.reshape(8, 8, 4, 128, 1024).transpose(0, 3, 1, 2, 4)).astype(f32)
    pk["b1c"] = np.ascontiguousarray(b1.reshape(8, 32, 128).transpose(2, 0, 1)).astype(f32)
    pk["b2r"] = b2.astype(f32)
    packT = lambda W: np.ascontiguousarray(W.T.reshape(8, 128, 1024).transpose(1, 0, 2)).astype(f32)
    pk["wq1"] = packT(inputs["L1_attn_Wq"])
    pk["wk1"] = packT(inputs["L1_attn_Wk"])
    pk["wv1"] = packT(inputs["L1_attn_Wv"])
    pk["wout"] = packT(inputs["L1_attn_Wout"])
    perm = np.zeros(64, np.int64)
    for m in range(2):
        for g in range(8):
            for j in range(4):
                perm[m * 32 + g * 4 + j] = (4 * g + j) * 2 + m
    Wb_p = inputs["L1_attn_Wbeta"][perm]
    Wm_p = inputs["L1_attn_Wmix"][perm]
    pk["wbm"] = np.ascontiguousarray(
        np.stack([Wb_p.T.reshape(8, 128, 64).transpose(1, 0, 2),
                  Wm_p.T.reshape(8, 128, 64).transpose(1, 0, 2)], axis=1)).astype(f32)
    pk["bbm_col"] = np.stack([inputs["L1_attn_bbeta"][perm],
                              inputs["L1_attn_bmix"][perm]], 1).astype(f32)
    pk["bout_row"] = inputs["L1_attn_bout"].reshape(1, -1).astype(f32)
    pk["lnrows"] = np.stack([inputs["L1_attn_ln_g"], inputs["L1_attn_ln_b"],
                             inputs["L1_ffn_ln_g"], inputs["L1_ffn_ln_b"]]).astype(f32)
    pk["cident"] = np.eye(128, dtype=f32)
    kk = np.arange(128)[:, None]
    qq = np.arange(512)[None, :]
    dmask = np.stack([(kk + 128 * j <= qq).astype(f32) for j in range(4)])
    pk["dmask"] = np.ascontiguousarray(np.transpose(dmask, (1, 0, 2)))
    Pm = np.zeros((128, 128), f32)
    for p in range(128):
        blk, d_ = (p // 32) * 32, p % 32
        if d_ < 16:
            Pm[p, blk + d_ + 16] = -1.0
        else:
            Pm[p, blk + d_ - 16] = 1.0
    pk["permT"] = np.ascontiguousarray(Pm.T)
    invf = 1.0 / 10000.0 ** (np.arange(16, dtype=f32) / 16.0)
    t = np.arange(T, dtype=f32)
    lane = np.arange(128) % 16
    pk["cosx"] = np.cos(t[None, :] * invf[lane][:, None]).astype(f32)
    pk["sinx"] = np.sin(t[None, :] * invf[lane][:, None]).astype(f32)
    e2 = np.zeros((64, 64), f32)
    for r in range(64):
        e2[r, r % 32] = 1.0
        e2[r, r % 32 + 32] = 1.0
    pk["e2c"] = np.ascontiguousarray(e2)
    s_, t_ = np.arange(128)[:, None], np.arange(128)[None, :]
    pk["gmaskc"] = (s_ <= t_).astype(f32)
    pk["ubc"] = (s_ <= t_).astype(f32)
    bo = np.zeros((128, 128), f32)
    for p in range(128):
        bo[p, (p // 32) * 32:(p // 32) * 32 + 32] = 1.0
    pk["boc"] = bo
    return pk


_L0_KEYS = ("x", "w_router", "wq", "wk", "wv", "bq", "bk", "bv", "bv_row", "ubc",
            "w1h", "w2h", "b1c", "b2r", "cident")
_L1_KEYS = ("attnf", "wm", "bm_row", "g32", "x", "x1", "wq1", "wk1", "wv1", "wbm",
            "bbm_col", "permT", "cosx", "sinx", "e2c", "wout", "bout_row", "lnrows",
            "ubc", "boc", "cident")


def kernel_unfused(**inputs):
    T = 4096
    n = inputs["L0_attn_x"].shape[0]
    pk = _host_pack(inputs, T)
    xs = [np.ascontiguousarray(inputs["L0_attn_x"][b]).astype(np.float32) for b in range(n)]

    nc0 = build_l0(T, phases=("a", "b", "c", "d"))
    maps0 = []
    for b in range(n):
        m = {k: pk[k] for k in _L0_KEYS if k in pk}
        m["x"] = xs[b]
        maps0.append(m)
    res0 = run_bass_kernel_spmd(nc0, maps0, core_ids=list(range(n)))
    global _dbg_res0
    _dbg_res0 = res0.results

    nc1 = build_l1(T, phases=("e", "e2", "f", "g"))
    maps1 = []
    for b in range(n):
        m = {k: pk[k] for k in _L1_KEYS if k in pk}
        m["x"] = xs[b]
        m["attnf"] = res0.results[b]["attnf"]
        m["g32"] = res0.results[b]["g32"]
        maps1.append(m)
    res1 = run_bass_kernel_spmd(nc1, maps1, core_ids=list(range(n)))
    global _dbg_res1
    _dbg_res1 = res1.results
    out = np.stack([res1.results[b]["outp"] for b in range(n)], axis=0)
    return out


def build_full(T=4096, ab_elems=71680, af_elems=13312):
    nc = bass.Bass("TRN2", target_bir_lowering=False)
    io = {}
    EI = "ExternalInput"
    mk = Mk(nc)
    _mk_shared(mk, nc, io, T, ab_elems=ab_elems, af_elems=af_elems)
    dt_ = nc.dram_tensor
    io["x"] = dt_("x", [T, D], F32, kind=EI).ap()
    io["w_router"] = dt_("w_router", [D, E0], F32, kind=EI).ap()
    for nm in ("wq", "wk", "wv", "wm", "wq1", "wk1", "wv1", "wout"):
        io[nm] = dt_(nm, [128, 8, 1024], F32, kind=EI).ap()
    for nm in ("bq", "bk", "bv"):
        io[nm] = dt_(nm, [128, 8], F32, kind=EI).ap()
    io["bv_row"] = dt_("bv_row", [1, D], F32, kind=EI).ap()
    io["bm_row"] = dt_("bm_row", [1, D], F32, kind=EI).ap()
    io["w1h"] = dt_("w1h", [8, 128, 8, 4096], F32, kind=EI).ap()
    io["w2h"] = dt_("w2h", [8, 128, 8, 4, 1024], F32, kind=EI).ap()
    io["b1c"] = dt_("b1c", [128, 8, 32], F32, kind=EI).ap()
    io["b2r"] = dt_("b2r", [8, 1024], F32, kind=EI).ap()
    io["wbm"] = dt_("wbm", [128, 2, 8, 64], F32, kind=EI).ap()
    io["bbm_col"] = dt_("bbm_col", [64, 2], F32, kind=EI).ap()
    io["permT"] = dt_("permT", [128, 128], F32, kind=EI).ap()
    io["cosx"] = dt_("cosx", [128, T], F32, kind=EI).ap()
    io["sinx"] = dt_("sinx", [128, T], F32, kind=EI).ap()
    io["e2"] = dt_("e2c", [64, 64], F32, kind=EI).ap()
    io["bo"] = dt_("boc", [128, 128], F32, kind=EI).ap()
    io["bout_row"] = dt_("bout_row", [1, D], F32, kind=EI).ap()
    io["lnrows"] = dt_("lnrows", [4, D], F32, kind=EI).ap()
    IN = "Internal"
    xnf0 = dt_("i_xnf0", [D, T], BF16, kind=IN).ap()
    xnf1 = dt_("i_xnf1", [D, T], BF16, kind=IN).ap()
    io["msk"] = dt_("i_msk", [T, E0], F32, kind=IN).ap()
    io["qf"] = dt_("i_qf", [D, T], BF16, kind=IN).ap()
    io["kf"] = dt_("i_kf", [D, T], BF16, kind=IN).ap()
    io["v0"] = dt_("i_v0", [8, T, 130], BF16, kind=IN).ap()
    io["attnf"] = dt_("i_attnf", [D, T], BF16, kind=IN).ap()
    io["g32"] = dt_("i_g32", [T, D], F32, kind=IN).ap()
    io["x1"] = dt_("i_x1", [T, D], F32, kind=IN).ap()
    io["xn1t"] = dt_("i_xn1t", [T, D], F32, kind=IN).ap()
    io["phq"] = dt_("i_phq", [D, T], F32, kind=IN).ap()
    io["phk"] = dt_("i_phk", [D, T], F32, kind=IN).ap()
    io["v1"] = dt_("i_v1", [8, T, 132], F32, kind=IN).ap()
    io["mixh"] = dt_("i_mixh", [64, T], F32, kind=IN).ap()
    io["ph_"] = dt_("i_ph", [64, T], F32, kind=IN).ap()
    io["iph"] = dt_("i_iph", [64, T], F32, kind=IN).ap()
    io["out"] = dt_("outp", [T, D], F32, kind="ExternalOutput").ap()

    ab, af = io["ab"], io["af"]
    io["xnf"] = xnf0
    ph_ln0(mk, io, T, layer=0)
    io["xnf_sb"] = mk._xnf_sb
    ph_l0proj(mk, io, T)
    ab.reset(); af.reset()
    ph_l0attn(mk, io, T)
    ab.reset(); af.reset()
    ph_l0moe(mk, io, T)
    ab.reset(); af.reset()
    ph_l0post(mk, io, T)
    ab.reset(); af.reset()
    io["xnf"] = xnf1
    ph_ln0(mk, io, T, layer=1)
    io["xnf_sb"] = mk._xnf_sb
    af.reset()
    ph_l1prep(mk, io, T)
    ab.reset(); af.reset()
    ph_l1gla(mk, io, T)
    mk.emit_program()
    return nc


_FULL_KEYS = ("x", "w_router", "wq", "wk", "wv", "wm", "bq", "bk", "bv", "bv_row",
              "bm_row", "w1h", "w2h", "b1c", "b2r", "wq1", "wk1", "wv1",
              "wout", "wbm", "bbm_col", "permT", "cosx", "sinx", "e2c",
              "ubc", "boc", "bout_row", "lnrows", "cident")


def kernel(**inputs):
    T = 4096
    n = inputs["L0_attn_x"].shape[0]
    pk = _host_pack(inputs, T)
    nc = build_full(T)
    maps = []
    for b in range(n):
        m = {k: pk[k] for k in _FULL_KEYS if k in pk}
        m["x"] = np.ascontiguousarray(inputs["L0_attn_x"][b]).astype(np.float32)
        maps.append(m)
    res = run_bass_kernel_spmd(nc, maps, core_ids=list(range(n)))
    return np.stack([res.results[b]["outp"] for b in range(n)], axis=0)


# version 12 (final)
# speedup vs baseline: 1.2172x; 1.0295x over previous
import math
from contextlib import ExitStack

import numpy as np
import concourse.bass as bass
import concourse.mybir as mybir
from concourse.bass_utils import run_bass_kernel_spmd

F32 = mybir.dt.float32
BF16 = mybir.dt.bfloat16
AF = mybir.ActivationFunctionType
ALU = mybir.AluOpType
AX = mybir.AxisListType


_ENG_ATTR = {"pe": "tensor", "dve": "vector", "act": "scalar", "pool": "gpsimd", "sp": "sync"}
_ENGINES = list(_ENG_ATTR)

_DT_SIZE = {"dt.float32": 4, "dt.bfloat16": 2, "dt.float16": 2, "dt.int32": 4,
            "dt.uint32": 4, "dt.int8": 1, "dt.uint8": 1, "dt.float32r": 4,
            "dt.int16": 2, "dt.uint16": 2, "dt.int64": 8, "dt.uint64": 8}


def _ap_region(ap):
    if not hasattr(ap, "tensor"):
        ap = ap[:]
    t = ap.tensor
    name = t.name
    tn = type(t).__name__
    if "PSum" in tn:
        space = "psum"
    elif "SB" in tn:
        space = "sbuf"
    else:
        return (name, "dram", 0, 1, 0, 1)
    shape = list(t.shape)
    esz = _DT_SIZE[str(t.dtype)]
    rowpitch = 1
    for s in shape[1:]:
        rowpitch *= s
    pairs = [list(p) for p in ap.ap]
    off = ap.offset
    pidx = None
    for i, (st, ct) in enumerate(pairs):
        if st == rowpitch and rowpitch > 1:
            pidx = i
            break
    if pidx is None:
        if shape[0] == 1 or rowpitch == 1:
            pidx = -1
        else:
            maxspan = off + sum(abs(st) * (ct - 1) for st, ct in pairs)
            if maxspan < rowpitch * (off // rowpitch + 1):
                pidx = -1
            else:
                return (name, space, 0, shape[0], 0, rowpitch * esz)
    plo = off // rowpitch
    f0 = off % rowpitch
    fspan = 0
    pcnt = 1
    for i, (st, ct) in enumerate(pairs):
        if i == pidx:
            pcnt = ct
        else:
            fspan += abs(st) * (ct - 1)
    if f0 + fspan >= rowpitch:
        return (name, space, 0, shape[0], 0, rowpitch * esz)
    blo, bhi = f0 * esz, (f0 + fspan + 1) * esz
    if space == "psum":
        blo = (blo // 2048) * 2048
        bhi = ((bhi + 2047) // 2048) * 2048
    return (name, space, plo, plo + pcnt, blo, bhi)


def _overlap(r1, r2):
    return (r1[0] == r2[0] and r1[2] < r2[3] and r2[2] < r1[3]
            and r1[4] < r2[5] and r2[4] < r1[5])


def _covers(w, r):
    return (w[0] == r[0] and w[2] <= r[2] and w[3] >= r[3]
            and w[4] <= r[4] and w[5] >= r[5])


class _Op:
    __slots__ = ("idx", "eng", "emit", "reads", "writes", "deps", "need_inc",
                 "is_dma", "sem_key", "seq", "tag")

    def __init__(self, idx, eng, emit, reads, writes, is_dma, sem_key, tag):
        self.idx = idx
        self.eng = eng
        self.emit = emit
        self.reads = reads
        self.writes = writes
        self.deps = []
        self.need_inc = is_dma
        self.is_dma = is_dma
        self.sem_key = sem_key
        self.seq = None
        self.tag = tag


class Mk:
    SERIALIZE = False

    def __init__(self, nc):
        self.nc = nc
        self.ops = []
        self.w_recs = {}
        self.r_recs = {}
        self._es = ExitStack()
        self.dma_counts = {}
        self._const_deps = []
        self._const_cols = {}

    def sbuf(self, name, shape, dtype):
        return self._es.enter_context(self.nc.sbuf_tensor(name, shape, dtype))

    def psum(self, name, shape, dtype=F32):
        return self._es.enter_context(self.nc.psum_tensor(name, shape, dtype))

    def dram(self, name, shape, dtype, kind="Internal"):
        return self.nc.dram_tensor(name, shape, dtype, kind=kind).ap()

    def init_consts(self, vals):
        vals = sorted(set(float(v) for v in vals) - {0.0, 1.0})
        if not vals:
            return
        n = len(vals)
        cf = self.sbuf("mk_consts_f32", [128, n], F32)
        cb = self.sbuf("mk_consts_bf16", [128, n], BF16)
        for i, v in enumerate(vals):
            self.nc.const_aps.register(F32, v, cf[:, i:i + 1])
            self.nc.const_aps.register(BF16, v, cb[:, i:i + 1])
            o1 = self.op("pool", lambda g, _cf=cf, _i=i, _v=v: g.memset(_cf[:, _i:_i + 1], _v),
                         writes=[cf[:, i:i + 1]], tag="const")
            o2 = self.op("pool", lambda g, _cb=cb, _i=i, _v=v: g.memset(_cb[:, _i:_i + 1], _v),
                         writes=[cb[:, i:i + 1]], tag="const")
            self._const_deps += [o1.idx, o2.idx]

    def _record(self, eng, emit, reads, writes, is_dma=False, sem_key=None, tag=""):
        idx = len(self.ops)
        rr = [_ap_region(a) for a in reads if a is not None]
        ww = [_ap_region(a) for a in writes if a is not None]
        op = _Op(idx, eng, emit, rr, ww, is_dma, sem_key, tag)
        deps = set()
        if not is_dma and tag != "const":
            deps.update(self._const_deps)
        for r in rr:
            for rec in self.w_recs.get(r[0], ()):
                if _overlap(r, rec[0]):
                    deps.add(rec[1])
            if r[1] == "psum" and eng in ("dve", "act"):
                other = "act" if eng == "dve" else "dve"
                rec = self.r_recs.get(r[0], {}).get(other)
                if rec is not None and _overlap(r, rec[0]):
                    deps.add(rec[1])
        for w in ww:
            for rec in self.w_recs.get(w[0], ()):
                if _overlap(w, rec[0]):
                    deps.add(rec[1])
            for rec in self.r_recs.get(w[0], {}).values():
                if _overlap(w, rec[0]):
                    deps.add(rec[1])
        if Mk.SERIALIZE and idx > 0:
            deps.add(idx - 1)
        op.deps = sorted(deps)
        key = sem_key if is_dma else eng
        for r in rr:
            d = self.r_recs.setdefault(r[0], {})
            if key in d:
                old = d[key][0]
                d[key] = [(r[0], r[1], min(old[2], r[2]), max(old[3], r[3]),
                           min(old[4], r[4]), max(old[5], r[5])), idx]
            else:
                d[key] = [r, idx]
        for w in ww:
            lst = self.w_recs.setdefault(w[0], [])
            lst[:] = [rec for rec in lst if not _covers(w, rec[0])]
            lst.append([w, idx])
            if len(lst) > 96:
                self._consolidate(lst)
            d = self.r_recs.get(w[0])
            if d:
                for k in list(d):
                    if _covers(w, d[k][0]):
                        del d[k]
        self.ops.append(op)
        return op

    def _consolidate(self, lst):
        byk = {}
        out = []
        half = lst[: len(lst) // 2]
        for rec in half:
            op = self.ops[rec[1]]
            k = op.sem_key if op.is_dma else op.eng
            if k in byk:
                o = byk[k]
                r0, r1 = o[0], rec[0]
                o[0] = (r0[0], r0[1], min(r0[2], r1[2]), max(r0[3], r1[3]),
                        min(r0[4], r1[4]), max(r0[5], r1[5]))
                o[1] = max(o[1], rec[1])
            else:
                byk[k] = rec
                out.append(rec)
        lst[:] = out + lst[len(lst) // 2:]

    def op(self, eng, emit, reads=(), writes=(), tag=""):
        return self._record(eng, emit, list(reads), list(writes), tag=tag)

    def dma(self, queue, out, in_, sem_key=None, tag="", **kw):
        if sem_key is None:
            sem_key = (queue, _ap_region(out)[0])
        else:
            sem_key = (queue, sem_key)
        emit = lambda e, _o=out, _i=in_, _k=dict(kw): e.dma_start(out=_o, in_=_i, **_k)
        return self._record(queue, emit, [in_], [out], is_dma=True, sem_key=sem_key, tag=tag)

    def emit_program(self):
        nc = self.nc
        ops = self.ops
        for op in ops:
            for d in op.deps:
                ops[d].need_inc = True
        eng_seq = {e: 0 for e in _ENGINES}
        cnt = {}
        for op in ops:
            if op.is_dma:
                cnt[op.sem_key] = cnt.get(op.sem_key, 0) + 1
                op.seq = cnt[op.sem_key]
            elif op.need_inc:
                eng_seq[op.eng] += 1
                op.seq = eng_seq[op.eng]
        self.dma_counts = cnt
        es = self._es
        eng_sems = {e: es.enter_context(nc.semaphore(f"se_{e}")) for e in _ENGINES}
        sem_keys = sorted(cnt, key=str)
        dma_sems = {k: es.enter_context(nc.semaphore(f"sd_{i}")) for i, k in enumerate(sem_keys)}

        per_eng = {e: [] for e in _ENGINES}
        for op in ops:
            per_eng[op.eng].append(op)

        block = es.enter_context(nc.Block())
        mkops = self.ops

        def make_body(ename, elist, final=False):
            def body(eng):
                clock = {}
                for op in elist:
                    if op.is_dma and op.seq >= 2:
                        kk = ("d", op.sem_key)
                        v = 16 * (op.seq - 1)
                        if clock.get(kk, -1) < v:
                            eng.wait_ge(dma_sems[op.sem_key], v)
                            clock[kk] = v
                    for d in op.deps:
                        p = mkops[d]
                        if (not p.is_dma) and p.eng == "pe" and ename == "pe":
                            continue
                        if p.is_dma:
                            kk, v, sem = ("d", p.sem_key), 16 * p.seq, dma_sems[p.sem_key]
                        else:
                            kk, v, sem = ("e", p.eng), p.seq, eng_sems[p.eng]
                        if clock.get(kk, -1) < v:
                            eng.wait_ge(sem, v)
                            clock[kk] = v
                    inst = op.emit(eng)
                    if op.is_dma:
                        inst.then_inc(dma_sems[op.sem_key], 16)
                    elif op.need_inc:
                        inst.then_inc(eng_sems[op.eng], 1)
                if final:
                    for k in sem_keys:
                        eng.wait_ge(dma_sems[k], 16 * cnt[k])
            return body

        for ename in _ENGINES:
            if per_eng[ename] or ename == "sp":
                body = make_body(ename, per_eng[ename], final=(ename == "sp"))
                getattr(block, _ENG_ATTR[ename])(body)
        es.close()
        return nc


def mm(mk, out, lhsT, rhs, start=True, stop=True, tile_position=None, tag="mm", skip=False):
    return mk.op("pe", lambda t, _o=out, _l=lhsT, _r=rhs, _s=start, _e=stop, _p=tile_position,
                 _k=skip:
                 t.matmul(_o, _l, _r, start=_s, stop=_e, tile_position=_p, skip_group_check=_k),
                 reads=[lhsT, rhs], writes=[out], tag=tag)


def transpose(mk, out, in_, ident, tag="tr"):
    return mk.op("pe", lambda t, _o=out, _i=in_, _d=ident: t.transpose(_o, _i, _d),
                 reads=[in_, ident], writes=[out], tag=tag)


def act(mk, out, in_, func, bias=None, scale=None, accum_out=None, tag="act"):
    kw = {}
    if bias is not None:
        kw["bias"] = bias
    if scale is not None:
        kw["scale"] = scale
    if accum_out is not None:
        kw["accum_out"] = accum_out
    reads = [in_] + ([bias] if hasattr(bias, "tensor") else [])
    writes = [out] + ([accum_out] if accum_out is not None else [])
    return mk.op("act", lambda s, _o=out, _i=in_, _f=func, _k=dict(kw):
                 s.activation(out=_o, in_=_i, func=_f, **_k), reads=reads, writes=writes, tag=tag)


def tt(mk, eng, out, in0, in1, op, tag="tt"):
    return mk.op(eng, lambda v, _o=out, _a=in0, _b=in1, _p=op: v.tensor_tensor(_o, _a, _b, _p),
                 reads=[in0, in1], writes=[out], tag=tag)


def ts(mk, eng, out, in0, s1, op0, s2=None, op1=None, tag="ts"):
    def emitfn(v, _o=out, _i=in0, _s1=s1, _s2=s2, _o0=op0, _o1=op1):
        if _s2 is None:
            return v.tensor_scalar(_o, _i, _s1, None, _o0)
        return v.tensor_scalar(_o, _i, _s1, _s2, _o0, _o1)
    reads = [in0] + [s for s in (s1, s2) if hasattr(s, "tensor")]
    return mk.op(eng, emitfn, reads=reads, writes=[out], tag=tag)


def copy(mk, eng, out, in_, tag="copy"):
    if eng == "act":
        return mk.op(eng, lambda s, _o=out, _i=in_: s.activation(out=_o, in_=_i, func=AF.Identity),
                     reads=[in_], writes=[out], tag=tag)
    return mk.op(eng, lambda v, _o=out, _i=in_: v.tensor_copy(_o, _i), reads=[in_], writes=[out], tag=tag)


def reduce(mk, out, in_, op, tag="red"):
    return mk.op("dve", lambda v, _o=out, _i=in_, _p=op:
                 v.tensor_reduce(_o, _i, AX.X, _p), reads=[in_], writes=[out], tag=tag)


def recip(mk, out, in_, tag="recip"):
    return mk.op("dve", lambda v, _o=out, _i=in_: v.reciprocal(_o, _i),
                 reads=[in_], writes=[out], tag=tag)


def recipf(mk, out, in_, tag="recipf"):
    return mk.op("dve", lambda v, _o=out, _i=in_: v.reciprocal_approx_fast(_o, _i),
                 reads=[in_], writes=[out], tag=tag)


def scan(mk, out, d0, d1, init, op0, op1, tag="scan"):
    return mk.op("dve", lambda v, _o=out, _a=d0, _b=d1, _n=init, _p=op0, _q=op1:
                 v.tensor_tensor_scan(_o, _a, _b, _n, _p, _q),
                 reads=[d0, d1], writes=[out], tag=tag)


def stt(mk, out, in0, scalar, in1, op0, op1, tag="stt"):
    reads = [in0, in1] + ([scalar] if hasattr(scalar, "tensor") else [])
    return mk.op("dve", lambda v, _o=out, _a=in0, _s=scalar, _b=in1, _p=op0, _q=op1:
                 v.scalar_tensor_tensor(_o, _a, _s, _b, _p, _q),
                 reads=reads, writes=[out], tag=tag)


def memset(mk, eng, ap_, val, tag="memset"):
    return mk.op(eng, lambda v, _o=ap_, _v=val: v.memset(_o, _v), writes=[ap_], tag=tag)



D = 1024
NH0, HS0 = 16, 64
NH1, HS1, M1 = 32, 32, 2
E0, F0 = 8, 4096
LN_EPS = 1e-5
SOFT_M = 40.0
GLA_EPS = 1e-6
BMIN, BMAX = 0.85, 0.9995


ALL_CONSTS = [1.0 / D, LN_EPS, -1.0, -0.5, -SOFT_M, 0.125, GLA_EPS, BMIN, BMAX,
              2.0, 1.0 / F0, 0.5]


def ph_ln0(mk, io, T, layer=0):
    nt = T // 128
    sfx = f"_l{layer}"
    xsrc = io["x"] if layer == 0 else io["x1"]
    x_t = io["af"].alloc(2, D)
    xc = io["af"].alloc(2, D)
    sq = io["af"].alloc(D)
    st = mk.sbuf(f"ln_st{sfx}", [128, 8], F32)
    io["_st8_shared"] = st
    xnt32 = io["af"].alloc(D)
    xnf = io["ab"].alloc(D // 128, T)
    ps_tr = io["PS_F"]
    if layer == 0:
        wr = mk.sbuf("ln_wr", [128, 8, E0], BF16)
        mk.dma("pool", wr[:], io["w_router"].rearrange("(c p) e -> p c e", p=128), tag="wr")
        ps_r = io["PS_E"][:, 0:E0]
        lg = mk.sbuf("ln_lg", [128, E0], F32)
        eq = mk.sbuf("ln_eq", [128, E0], F32)
        cum = mk.sbuf("ln_cum", [128, E0], F32)
        zer8 = mk.sbuf("ln_z8", [128, E0], F32)
        msk = mk.sbuf("ln_msk", [128, nt, E0], F32)
        io["_msk_sb"] = msk
        memset(mk, "pool", zer8[:], 0.0)

    for i in range(nt):
        xi = x_t[:, i % 2, :]
        xci = xc[:, i % 2, :]
        mk.dma("sp", xi, xsrc[i * 128:(i + 1) * 128, :], sem_key=f"x{sfx}{i % 2}", tag="xload")
        reduce(mk, st[:, 0:1], xi, ALU.add)
        ts(mk, "dve", st[:, 1:2], st[:, 0:1], 1.0 / D, ALU.mult)
        ts(mk, "dve", xci, xi, st[:, 1:2], ALU.subtract)
        act(mk, sq[:], xci, AF.Square, accum_out=st[:, 2:3])
        act(mk, st[:, 3:4], st[:, 2:3], AF.Ln, scale=1.0 / D, bias=LN_EPS)
        act(mk, st[:, 4:5], st[:, 3:4], AF.Exp, scale=-0.5)
        ts(mk, "dve", xnt32[:], xci, st[:, 4:5], ALU.mult)
        if layer == 1 and io.get("xn1t") is not None:
            mk.dma("sp", io["xn1t"][i * 128:(i + 1) * 128, :], xnt32[:],
                   sem_key=f"xn1t{i % 2}", tag="xn1t")
        for hb in range(2):
            for b2 in range(4):
                b = hb * 4 + b2
                transpose(mk, ps_tr[:, 128 * b2:128 * (b2 + 1)],
                          xnt32[:, 128 * b:128 * (b + 1)], io["identf_sb"][:])
            for b2 in range(4):
                b = hb * 4 + b2
                copy(mk, "dve" if b % 2 == 0 else "act", xnf[:, b, i * 128:(i + 1) * 128],
                     ps_tr[:, 128 * b2:128 * (b2 + 1)])
        if layer == 0:
            for c in range(8):
                mm(mk, ps_r, xnf[:, c, i * 128:(i + 1) * 128], wr[:, c, :],
                   start=(c == 0), stop=(c == 7))
            reduce(mk, st[:, 5:6], ps_r, ALU.max)
            ts(mk, "dve", st[:, 6:7], st[:, 5:6], -1.0, ALU.mult)
            act(mk, lg[:], ps_r, AF.Exp, bias=st[:, 6:7])
            reduce(mk, st[:, 7:8], lg[:], ALU.add)
            recip(mk, st[:, 7:8], st[:, 7:8])
            tt(mk, "dve", eq[:], ps_r, st[:, 5:6].broadcast_to((128, E0)), ALU.is_equal)
            scan(mk, cum[:], eq[:], zer8[:], 0.0, ALU.add, ALU.add)
            ts(mk, "dve", cum[:], cum[:], 1.0, ALU.is_equal)
            tt(mk, "dve", eq[:], eq[:], cum[:], ALU.mult)
            ts(mk, "dve", msk[:, i, :], eq[:], st[:, 7:8], ALU.mult)

    mk._xnf_sb = xnf
    if "xnf" in io and io["xnf"] is not None:
        for b in range(8):
            mk.dma("sp", io["xnf"][128 * b:128 * (b + 1), :], xnf[:, b, :],
                   sem_key=f"xnf{sfx}", tag="xnf_store")
    if layer == 0:
        mk.dma("sp", io["msk"].rearrange("(i p) e -> p i e", p=128), msk[:], tag="msk_store")


def build_phase_a(T=4096):
    nc = bass.Bass("TRN2", target_bir_lowering=False)
    io = {}
    mk = Mk(nc)
    io["x"] = nc.dram_tensor("x", [T, D], F32, kind="ExternalInput").ap()
    io["w_router"] = nc.dram_tensor("w_router", [D, E0], F32, kind="ExternalInput").ap()
    io["xnf"] = nc.dram_tensor("xnf", [D, T], BF16, kind="ExternalOutput").ap()
    io["msk"] = nc.dram_tensor("msk", [T, E0], F32, kind="ExternalOutput").ap()
    _mk_shared(mk, nc, io, T)
    ph_ln0(mk, io, T, layer=0)
    mk.emit_program()
    return nc



class Arena:
    def __init__(self, mk, name, n_elems, dtype):
        self.t = mk.sbuf(name, [128, n_elems], dtype)
        self.n = n_elems
        self.off = 0

    def reset(self):
        self.off = 0

    def alloc(self, *shape):
        n = 1
        for s in shape:
            n *= s
        assert self.off + n <= self.n, f"arena overflow: {self.off}+{n} > {self.n}"
        v = self.t[:, self.off:self.off + n]
        self.off += n
        if len(shape) > 1:
            dims = " ".join(f"d{i}" for i in range(len(shape)))
            v = v.rearrange(f"p ({dims}) -> p {dims}",
                            **{f"d{i}": s for i, s in enumerate(shape)})
        return v


def ph_l0proj(mk, io, T):
    ab, af = io["ab"], io["af"]
    xnf = io["xnf_sb"]
    wbuf = [ab.alloc(8, 1024) for _ in range(2)]
    bq = mk.sbuf("b_qkv", [128, 3, 8], F32)
    for j, nm in enumerate(("bq", "bk", "bv")):
        mk.dma("sp", bq[:, j, :], io[nm], tag="bias")
    onesb = io["onesb_sb"]
    bvrow = io["rowb_sb"]
    mk.dma("pool", bvrow[:], io["bv_row"], tag="bvrow")
    ps = io["PS_A"]
    ps2 = io["PS_B"]
    qstage = ab.alloc(2, 512)

    for j, (wname, out_h) in enumerate((("wq", "qf"), ("wk", "kf"))):
        wb = wbuf[j % 2]
        mk.dma("pool", wb, io[wname], tag=f"w_{wname}")
        for c in range(8):
            for tj in range(T // 512):
                p = ps[:, 0:512] if (c * (T // 512) + tj) % 2 == 0 else ps2[:, 0:512]
                for dc in range(8):
                    mm(mk, p, wb[:, dc, c * 128:(c + 1) * 128],
                       xnf[:, dc, tj * 512:(tj + 1) * 512], start=(dc == 0), stop=(dc == 7))
                stg = qstage[:, tj % 2, :]
                if tj % 2 == 0:
                    ts(mk, "dve", stg, p, bq[:, j, c:c + 1], ALU.add)
                else:
                    act(mk, stg, p, AF.Identity, bias=bq[:, j, c:c + 1])
                mk.dma("sp", io[out_h][c * 128:(c + 1) * 128, tj * 512:(tj + 1) * 512], stg,
                       sem_key=f"{out_h}{tj % 2}", tag=f"{out_h}_store")

    wb = wbuf[0]
    mk.dma("pool", wb, io["wv"], tag="w_wv")
    vstage = ab.alloc(2, 1040)
    for ti in range(T // 128):
        p = ps[:, 0:512] if ti % 2 == 0 else ps2[:, 0:512]
        stg = vstage[:, ti % 2, :].rearrange("p (h d) -> p h d", d=65)
        for o in range(2):
            for dc in range(8):
                mm(mk, p, xnf[:, dc, ti * 128:(ti + 1) * 128], wb[:, dc, o * 512:(o + 1) * 512],
                   start=(dc == 0), stop=False)
            mm(mk, p, onesb[:], bvrow[:, o * 512:(o + 1) * 512], start=False, stop=True)
            copy(mk, "dve" if ti % 2 == 0 else "act",
                 stg[:, o * 8:(o + 1) * 8, 0:64],
                 p.rearrange("p (h d) -> p h d", d=64))
        memset(mk, "pool", stg[:, :, 64:65], 1.0)
        mk.dma("sp", io["v0"].rearrange("h (i p) c -> p i h c", p=128)[:, ti, :, :],
               vstage[:, ti % 2, :].rearrange("p (h c) -> p h c", c=130),
               sem_key=f"v0{ti % 2}", tag="v0_store")


def ph_l0attn(mk, io, T):
    ab, af = io["ab"], io["af"]
    qp = [ab.alloc(T) for _ in range(2)]
    kp = [ab.alloc(T) for _ in range(2)]
    vp = [ab.alloc(T // 128, 130) for _ in range(2)]
    pt = [ab.alloc(4, 512) for _ in range(2)]
    stage = [ab.alloc(512) for _ in range(2)]
    rden = af.alloc(512)
    rdb = af.alloc(2, 512)
    onesf = io["onesf_sb"]
    PS_A, PS_B = io["PS_A"], io["PS_B"]
    PS_O = (io["PS_C"][0:65, :], io["PS_D"][0:65, :])
    PS_E = io["PS_E"]
    nq = T // 512
    ubb = io["ubb_sb"]

    for hp in range(8):
        b = hp % 2
        mk.dma("sp", qp[b], io["qf"][128 * hp:128 * (hp + 1), :], sem_key=f"qp{b}", tag="qload")
        mk.dma("sp", kp[b], io["kf"][128 * hp:128 * (hp + 1), :], sem_key=f"kp{b}", tag="kload")
        mk.dma("pool", vp[b], io["v0"].rearrange("h (i p) c -> p i h c", p=128)[:, :, hp, :],
               sem_key=f"vp{b}", tag="vload")
        for qb in range(nq):
            nk = 4 * (qb + 1)
            qA = qp[b][0:64, qb * 512:(qb + 1) * 512]
            qB = qp[b][64:128, qb * 512:(qb + 1) * 512]
            for kb in range(nk):
                pA = PS_A[:, (kb % 2) * 512:(kb % 2 + 1) * 512]
                pB = PS_B[:, (kb % 2) * 512:(kb % 2 + 1) * 512]
                mm(mk, pA, kp[b][0:64, kb * 128:(kb + 1) * 128], qA, tile_position=(0, 0))
                mm(mk, pB, kp[b][64:128, kb * 128:(kb + 1) * 128], qB, tile_position=(64, 0))
                if kb % 2 == 1:
                    for h, ps_ in ((0, PS_A), (1, PS_B)):
                        r = (kb - 1) % 4
                        act(mk, pt[h][:, r:r + 2, :].rearrange("p a t -> p (a t)"),
                            ps_[:], AF.Exp, scale=0.125, bias=-SOFT_M)
                        for kk in (kb - 1, kb):
                            j = kk - 4 * qb
                            if 0 <= j < 4:
                                sl_ = pt[h][:, kk % 4, :]
                                if j > 0:
                                    memset(mk, "dve", sl_[:, 0:128 * j], 0.0, tag="diagz")
                                tt(mk, "dve", sl_[:, 128 * j:128 * (j + 1)],
                                   sl_[:, 128 * j:128 * (j + 1)], ubb[:],
                                   ALU.mult, tag="diagmask")
                        for kk in (kb - 1, kb):
                            mm(mk, PS_O[h], vp[b][:, kk, 65 * h:65 * h + 65],
                               pt[h][:, kk % 4, :], start=(kk == 0), stop=(kk == nk - 1))
            for h in range(2):
                psO = PS_O[h]
                recip(mk, rden[64:65, :], psO[64:65, :])
                mm(mk, PS_E[0:65, :], onesf[64:65, 0:65], rden[64:65, :],
                   tile_position=(64, 0))
                copy(mk, "act", rdb[0:65, h, :], PS_E[0:65, :])
                tt(mk, "dve", stage[h][0:65, :], psO[0:65, :], rdb[0:65, h, :], ALU.mult)
                mk.dma("sp", io["attnf"][64 * (2 * hp + h):64 * (2 * hp + h + 1),
                                         qb * 512:(qb + 1) * 512],
                       stage[h][0:64, :], sem_key=f"at{h}", tag="attn_store")


def ph_l0moe(mk, io, T):
    ab, af = io["ab"], io["af"]
    nt = T // 256
    w1a = ab.alloc(8, 2048)
    w1b = ab.alloc(8, 2048)
    w2e = ab.alloc(8, 4, 1024)
    _x2 = ab.alloc(8, 256)
    xnt2 = [_x2, _x2]
    h_sb = [ab.alloc(4, 256) for _ in range(2)]
    ystage = [af.alloc(2, 1024) for _ in range(2)]
    b1c = mk.sbuf("moe_b1", [128, 8, 32], F32)
    b2r = io["rowb_sb"].rearrange("a (u d) -> a u d", u=1)
    onesb = io["onesb_sb"]
    if "_msk_sb" in io:
        mskb = io["_msk_sb"]
    else:
        mskb = mk.sbuf("moe_msk", [128, T // 128, 8], F32)
        mk.dma("sp", mskb[:], io["msk"].rearrange("(i p) e -> p i e", p=128), tag="mskload")
    mk.dma("sp", b1c[:], io["b1c"], tag="b1load")
    PS_Y = (io["PS_A"], io["PS_B"])
    PS_H4 = (io["PS_C"][:, 0:256], io["PS_E"][:, 0:256],
             io["PS_D"][:, 0:256], io["PS_F"][:, 0:256])
    h4 = tuple(h_sb[0][:, k, :] for k in range(4)) + tuple(h_sb[1][:, k, :] for k in range(4))

    def emit_h(e, xnf, w1a, w1b, fc):
        w1h = w1a if fc < 4 else w1b
        fo = (fc % 4) * 512
        for fs in range(4):
            ph = PS_H4[(fc % 2) * 2 + (fs % 2)]
            for dc in range(8):
                mm(mk, ph, w1h[:, dc, fo + fs * 128:fo + (fs + 1) * 128],
                   xnf[:, dc, :], start=(dc == 0), stop=(dc == 7))
            col = b1c[:, e, fc * 4 + fs:fc * 4 + fs + 1]
            hv = h4[(fc % 2) * 4 + fs]
            if fs % 2 == 0:
                ts(mk, "dve", hv, ph, col, ALU.add, 0.0, ALU.max, tag="relu")
            else:
                act(mk, hv, ph, AF.Relu, bias=col, tag="relu")

    def emit_y(e, fc):
        for fs in range(4):
            hv = h4[(fc % 2) * 4 + fs]
            for tsub in range(2):
                for n in range(2):
                    mm(mk, PS_Y[tsub][:, n * 512:(n + 1) * 512],
                       hv[:, tsub * 128:(tsub + 1) * 128],
                       w2e[:, fc, fs, n * 512:(n + 1) * 512],
                       start=(fc == 0 and fs == 0), stop=False)

    for e in range(8):
        mk.dma("pool", b2r[:, 0, :], io["b2r"][e:e + 1, :], sem_key="b2r", tag="b2load")
        mk.dma("pool", w1a, io["w1h"][e, :, :, 0:2048], sem_key="w1a", tag="w1a")
        mk.dma("pool", w1b, io["w1h"][e, :, :, 2048:4096], sem_key="w1b", tag="w1b")
        mk.dma("pool", w2e, io["w2h"][e, :, :, :, :], sem_key="w2e", tag="w2e")
        for i in range(nt):
            t0 = i * 256
            xnf = xnt2[i % 2]
            mk.dma("sp", xnf,
                   io["xnf"].rearrange("(c p) t -> p c t", p=128)[:, :, t0:t0 + 256],
                   sem_key=f"xnt2{i % 2}", tag="xnf_tile")
            emit_h(e, xnf, w1a, w1b, 0)
            for fc in range(8):
                if fc < 7:
                    emit_h(e, xnf, w1a, w1b, fc + 1)
                emit_y(e, fc)
            for tsub in range(2):
                for n in range(2):
                    mm(mk, PS_Y[tsub][:, n * 512:(n + 1) * 512], onesb[:],
                       b2r[:, 0, n * 512:(n + 1) * 512], start=False, stop=True)
                ts(mk, "dve", ystage[i % 2][:, tsub, :], PS_Y[tsub][:],
                   mskb[:, 2 * i + tsub, e:e + 1], ALU.mult, tag="ymask")
            kw = {} if e == 0 else {"accum_op": ALU.add}
            mk.dma("pool", io["g32"].rearrange("(i p) d -> p i d", p=128)[:, 2 * i:2 * i + 2, :],
                   ystage[i % 2][:], sem_key=f"g{i % 3}", tag="gaccum", **kw)


def _mk_shared(mk, nc, io, T, ab_elems=71680, af_elems=14336):
    mk.init_consts(ALL_CONSTS)
    io["ab"] = Arena(mk, "pool_bf16", ab_elems, BF16)
    io["af"] = Arena(mk, "pool_f32", af_elems, F32)
    io["cident"] = nc.dram_tensor("cident", [128, 128], F32, kind="ExternalInput").ap()
    identf = mk.sbuf("identf_sb", [128, 128], F32)
    mk.dma("sp", identf[:], io["cident"], tag="identf")
    io["identf_sb"] = identf
    onesf = mk.sbuf("onesf_sb", [128, 128], F32)
    memset(mk, "pool", onesf[:], 1.0)
    io["onesf_sb"] = onesf
    onesb = mk.sbuf("onesb_sb", [1, 128], BF16)
    memset(mk, "pool", onesb[:], 1.0)
    io["onesb_sb"] = onesb
    io["ubc"] = nc.dram_tensor("ubc", [128, 128], F32, kind="ExternalInput").ap()
    ubf = mk.sbuf("ub_sb", [128, 128], F32)
    mk.dma("sp", ubf[:], io["ubc"], tag="ubf")
    io["ub_sb"] = ubf
    ubb = mk.sbuf("ubb_sb", [128, 128], BF16)
    mk.dma("pool", ubb[:], io["ubc"], tag="ubb")
    io["ubb_sb"] = ubb
    io["rowb_sb"] = mk.sbuf("rowb_sb", [1, D], BF16)
    io["PS_A"] = mk.psum("PS_A", [128, 1024], F32)
    io["PS_B"] = mk.psum("PS_B", [128, 1024], F32)
    io["PS_C"] = mk.psum("PS_C", [128, 512], F32)
    io["PS_D"] = mk.psum("PS_D", [128, 512], F32)
    io["PS_E"] = mk.psum("PS_E", [128, 512], F32)
    io["PS_F"] = mk.psum("PS_F", [128, 512], F32)


def build_l0(T=4096, phases=("a", "b", "c"), ab_elems=71680, af_elems=6144):
    nc = bass.Bass("TRN2", target_bir_lowering=False)
    io = {}
    EI, EO = "ExternalInput", "ExternalOutput"
    mk = Mk(nc)
    _mk_shared(mk, nc, io, T, ab_elems=ab_elems, af_elems=af_elems)
    if "a" in phases:
        io["x"] = nc.dram_tensor("x", [T, D], F32, kind=EI).ap()
        io["w_router"] = nc.dram_tensor("w_router", [D, E0], F32, kind=EI).ap()
        io["xnf"] = nc.dram_tensor("xnf", [D, T], BF16, kind=EO).ap()
        io["msk"] = nc.dram_tensor("msk", [T, E0], F32, kind=EO).ap()
    else:
        io["xnf"] = nc.dram_tensor("xnf", [D, T], BF16, kind=EI).ap()
    if "b" in phases:
        for nm in ("wq", "wk", "wv"):
            io[nm] = nc.dram_tensor(nm, [128, 8, 1024], F32, kind=EI).ap()
        for nm in ("bq", "bk", "bv"):
            io[nm] = nc.dram_tensor(nm, [128, 8], F32, kind=EI).ap()
        io["bv_row"] = nc.dram_tensor("bv_row", [1, D], F32, kind=EI).ap()
        k_qkv = EO if "c" not in phases else EO
        io["qf"] = nc.dram_tensor("qf", [D, T], BF16, kind=k_qkv).ap()
        io["kf"] = nc.dram_tensor("kf", [D, T], BF16, kind=k_qkv).ap()
        io["v0"] = nc.dram_tensor("v0", [8, T, 130], BF16, kind=k_qkv).ap()
    else:
        io["qf"] = nc.dram_tensor("qf", [D, T], BF16, kind=EI).ap()
        io["kf"] = nc.dram_tensor("kf", [D, T], BF16, kind=EI).ap()
        io["v0"] = nc.dram_tensor("v0", [8, T, 130], BF16, kind=EI).ap()
    if "c" in phases:
        io["attnf"] = nc.dram_tensor("attnf", [D, T], BF16, kind=EO).ap()
    if "d" in phases:
        io["w1h"] = nc.dram_tensor("w1h", [8, 128, 8, 4096], F32, kind=EI).ap()
        io["w2h"] = nc.dram_tensor("w2h", [8, 128, 8, 4, 1024], F32, kind=EI).ap()
        io["b1c"] = nc.dram_tensor("b1c", [128, 8, 32], F32, kind=EI).ap()
        io["b2r"] = nc.dram_tensor("b2r", [8, 1024], F32, kind=EI).ap()
        if "a" not in phases:
            io["msk"] = nc.dram_tensor("msk", [T, E0], F32, kind=EI).ap()
        io["g32"] = nc.dram_tensor("g32", [T, D], F32, kind=EO).ap()

    if "a" in phases:
        ph_ln0(mk, io, T, layer=0)
        io["xnf_sb"] = mk._xnf_sb
    if "b" in phases:
        if "a" not in phases:
            xnf_sb = io["ab"].alloc(8, T)
            for c in range(8):
                mk.dma("sp", xnf_sb[:, c, :], io["xnf"][128 * c:128 * (c + 1), :],
                       sem_key="xnfld", tag="xnf_load")
            io["xnf_sb"] = xnf_sb
        ph_l0proj(mk, io, T)
    if "c" in phases:
        io["ab"].reset()
        io["af"].reset()
        ph_l0attn(mk, io, T)
    if "d" in phases:
        io["ab"].reset()
        io["af"].reset()
        ph_l0moe(mk, io, T)
    mk.emit_program()
    return nc


def ph_l0post(mk, io, T):
    ab, af = io["ab"], io["af"]
    at = [ab.alloc(8, 256) for _ in range(2)]
    wm = ab.alloc(8, 1024)
    gx = [af.alloc(2, 1024) for _ in range(2)]
    xx = [af.alloc(2, 1024) for _ in range(2)]
    x1s = [af.alloc(2, 1024) for _ in range(2)]
    bmrow = io["rowb_sb"]
    onesb = io["onesb_sb"]
    mk.dma("pool", wm, io["wm"], tag="wm")
    mk.dma("pool", bmrow[:], io["bm_row"], tag="bm")
    PS_Y = (io["PS_A"], io["PS_B"])
    nt = T // 256
    for i in range(nt):
        t0 = i * 256
        b = i % 2
        mk.dma("sp", at[b], io["attnf"].rearrange("(c p) t -> p c t", p=128)[:, :, t0:t0 + 256],
               sem_key=f"at{b}", tag="atload")
        mk.dma("sp", gx[b], io["g32"].rearrange("(i p) d -> p i d", p=128)[:, 2 * i:2 * i + 2, :],
               sem_key=f"gx{b}", tag="gload")
        mk.dma("sp", xx[b], io["x"].rearrange("(i p) d -> p i d", p=128)[:, 2 * i:2 * i + 2, :],
               sem_key=f"xx{b}", tag="xload")
        for tsub in range(2):
            for n in range(2):
                for dc in range(8):
                    mm(mk, PS_Y[tsub][:, n * 512:(n + 1) * 512],
                       at[b][:, dc, tsub * 128:(tsub + 1) * 128],
                       wm[:, dc, n * 512:(n + 1) * 512], start=(dc == 0), stop=False)
                mm(mk, PS_Y[tsub][:, n * 512:(n + 1) * 512], onesb[:],
                   bmrow[:, n * 512:(n + 1) * 512], start=False, stop=True)
            act(mk, gx[b][:, tsub, :], gx[b][:, tsub, :], AF.Silu)
            for n in range(2):
                sl_ = slice(n * 512, (n + 1) * 512)
                tt(mk, "dve", x1s[b][:, tsub, sl_], PS_Y[tsub][:, sl_],
                   gx[b][:, tsub, sl_], ALU.mult)
            tt(mk, "dve", x1s[b][:, tsub, :], x1s[b][:, tsub, :], xx[b][:, tsub, :], ALU.add)
        mk.dma("sp", io["x1"].rearrange("(i p) d -> p i d", p=128)[:, 2 * i:2 * i + 2, :],
               x1s[b], sem_key=f"x1{b}", tag="x1store")


def ph_l1prep(mk, io, T):
    ab, af = io["ab"], io["af"]
    xnf = io["xnf_sb"]
    wbuf = [ab.alloc(8, 1024) for _ in range(2)]
    permT = mk.sbuf("permT_sb", [128, 128], F32)
    mk.dma("sp", permT[:], io["permT"], tag="permT")
    cosx = [af.alloc(512) for _ in range(2)]
    sinx = [af.alloc(512) for _ in range(2)]
    raw32 = [af.alloc(512) for _ in range(2)]
    rot = af.alloc(512)
    rot2 = af.alloc(512)
    phs = [af.alloc(512) for _ in range(2)]
    PS_A, PS_B, PS_E, PS_F = io["PS_A"], io["PS_B"], io["PS_E"], io["PS_F"]

    for j, (wname, outh) in enumerate((("wq1", "phq"), ("wk1", "phk"))):
        wb = wbuf[j % 2]
        mk.dma("pool", wb, io[wname], tag=wname)
        for tj in range(T // 512):
            ts_ = tj * 512
            u2 = tj % 2
            mk.dma("sp", cosx[u2], io["cosx"][:, ts_:ts_ + 512],
                   sem_key=f"cosx{u2}", tag="cosx")
            mk.dma("sp", sinx[u2], io["sinx"][:, ts_:ts_ + 512],
                   sem_key=f"sinx{u2}", tag="sinx")
            for c in range(8):
                u = (c + tj) % 2
                p = PS_A[:, (u % 2) * 512:(u % 2) * 512 + 512]
                for dc in range(8):
                    mm(mk, p, wb[:, dc, c * 128:(c + 1) * 128],
                       xnf[:, dc, tj * 512:(tj + 1) * 512], start=(dc == 0), stop=(dc == 7))
                rb = raw32[u]
                copy(mk, "act", rb, p)
                mm(mk, PS_B[:, 0:512], permT[:], rb)
                tt(mk, "dve", rot, p, cosx[u2], ALU.mult)
                tt(mk, "dve", rot2, PS_B[:, 0:512], sinx[u2], ALU.mult)
                tt(mk, "dve", rot, rot, rot2, ALU.add)
                ts(mk, "dve", rot2, rot, 0.0, ALU.min)
                act(mk, rot2, rot2, AF.Exp)
                ts(mk, "dve", rot, rot, 0.0, ALU.max)
                tt(mk, "dve", phs[u], rot, rot2, ALU.add)
                mk.dma("sp", io[outh][c * 128:(c + 1) * 128, ts_:ts_ + 512], phs[u],
                       sem_key=f"{outh}{u}", tag=outh)

    wb = wbuf[0]
    mk.dma("pool", wb, io["wv1"], tag="wv1")
    vstage = af.alloc(2, 1056)
    for ti in range(T // 128):
        b = ti % 2
        p = PS_A[:, 0:512] if b == 0 else PS_A[:, 512:1024]
        stg = vstage[:, b, :].rearrange("p (h d) -> p h d", d=33)
        for o in range(2):
            for dc in range(8):
                mm(mk, p, xnf[:, dc, ti * 128:(ti + 1) * 128], wb[:, dc, o * 512:(o + 1) * 512],
                   start=(dc == 0), stop=(dc == 7))
            copy(mk, "dve" if b == 0 else "act",
                 stg[:, o * 16:(o + 1) * 16, 0:32],
                 p.rearrange("p (h d) -> p h d", d=32))
        memset(mk, "pool", stg[:, :, 32:33], 1.0)
        mk.dma("sp", io["v1"].rearrange("g (i p) c -> p i g c", p=128)[:, ti, :, :],
               vstage[:, b, :].rearrange("p (g c) -> p g c", c=132),
               sem_key=f"v1{b}", tag="v1store")

    wbm = ab.alloc(2, 8, 64)
    mk.dma("pool", wbm, io["wbm"], tag="wbm")
    bcol = mk.sbuf("l1_bcol", [64, 2], F32)
    mk.dma("sp", bcol[:], io["bbm_col"], tag="bbm")
    e2 = mk.sbuf("e2_sb", [64, 64], F32)
    mk.dma("sp", e2[:], io["e2"], tag="e2")
    beta = [af.alloc(512) for _ in range(2)]
    mixe = [af.alloc(512) for _ in range(2)]
    Pst = [af.alloc(512) for _ in range(2)]
    zer = af.alloc(128)
    memset(mk, "pool", zer, 0.0)
    for tj in range(T // 512):
        sl = slice(tj * 512, (tj + 1) * 512)
        u = tj % 2
        bt, mx, Ps = beta[u][0:64, :], mixe[u][0:64, :], Pst[u][0:64, :]
        for a in range(2):
            p = PS_E[0:64, :]
            for dc in range(8):
                mm(mk, p, wbm[:, a, dc, :], xnf[:, dc, sl],
                   start=(dc == 0), stop=(dc == 7))
            if a == 0:
                act(mk, bt, p, AF.Sigmoid, bias=bcol[:, 0:1])
                ts(mk, "dve", bt, bt, BMIN, ALU.max, BMAX, ALU.min)
            else:
                act(mk, mx, p, AF.Exp, bias=bcol[:, 1:2])
        mm(mk, PS_E[0:64, :], e2[:], mx)
        recip(mk, rot[0:64, :], PS_E[0:64, :])
        tt(mk, "dve", mx, mx, rot[0:64, :], ALU.mult)
        for cc in range(4):
            sc = slice(cc * 128, (cc + 1) * 128)
            scan(mk, Ps[:, sc], bt[:, sc], zer[0:64, 0:128], 1.0, ALU.mult, ALU.add)
        mk.dma("sp", io["mixh"][:, sl], mx, sem_key=f"mixh{u}", tag="mixh")
        mk.dma("sp", io["ph_"][:, sl], Ps, sem_key=f"ph{u}", tag="ph")
        recip(mk, bt, Ps)
        mk.dma("sp", io["iph"][:, sl], bt, sem_key=f"iph{u}", tag="iph")


def ph_l1gla(mk, io, T):
    ab, af = io["ab"], io["af"]
    C = 128
    ncu = T // C
    onesf = io["onesf_sb"]
    PS_A, PS_B, PS_C, PS_D, PS_E, PS_F = (io["PS_A"], io["PS_B"], io["PS_C"],
                                          io["PS_D"], io["PS_E"], io["PS_F"])
    gmask = io["ub_sb"]
    ub = io["ub_sb"]
    bo32 = mk.sbuf("bo_sb", [128, 128], F32)
    mk.dma("sp", bo32[:], io["bo"], tag="bo")
    Pb = [af.alloc(128) for _ in range(2)]
    iPb = [af.alloc(128) for _ in range(2)]
    mxb = [af.alloc(128) for _ in range(2)]
    if io.get("dbg") is not None:
        io["_dbgs"] = [af.alloc(128) for _ in range(8)]
        io["_dbgs2"] = [af.alloc(128) for _ in range(5)]
    wout = ab.alloc(8, 1024)
    mk.dma("pool", wout, io["wout"], tag="wout")
    onesb = io["onesb_sb"]
    lnrow2 = mk.sbuf("lnrow2_sb", [1, 3, D], BF16)
    lnbc = af.alloc(2, 1024)
    for k in range(2):
        mk.dma("sp", lnbc[:, k, :],
               io["lnrows"][2 * k:2 * k + 1, :].broadcast_to((128, D)),
               sem_key="lnrow", tag="lnbc")
    mk.dma("pool", lnrow2[:, 0, :], io["lnrows"][1:2, :], sem_key="lnrow", tag="lnb")
    mk.dma("pool", lnrow2[:, 1, :], io["lnrows"][3:4, :], sem_key="lnrow", tag="fb")
    mk.dma("pool", lnrow2[:, 2, :], io["bout_row"], sem_key="lnrow", tag="bout")

    state = mk.sbuf("gla_state", [128, 8, 2, 33], F32)
    memset(mk, "dve", state[:], 0.0)
    io.setdefault("_st8_shared", None)

    _pq = af.alloc(8, C)
    _pk = af.alloc(8, C)
    _v1 = af.alloc(8, 132)
    _x1 = af.alloc(1024)
    _xn = af.alloc(1024)
    phq_c = [_pq, _pq]
    phk_c = [_pk, _pk]
    v1_c = [_v1, _v1]
    x1t = [_x1, _x1]
    xnt1 = [_xn, _xn]
    qt32_ = [af.alloc(C) for _ in range(2)]
    t1b32_ = [af.alloc(C) for _ in range(2)]
    zq32_ = [af.alloc(C) for _ in range(2)]
    kt32_ = [af.alloc(C) for _ in range(2)]
    ktT32_ = [af.alloc(C) for _ in range(2)]
    _A1 = af.alloc(4, C)
    A_sb_ = [_A1, _A1]
    y_sb = ab.alloc(8, C)
    w_f_ = [af.alloc(C) for _ in range(2)]
    ytmp_ = [af.alloc(C) for _ in range(2)]
    pend2 = [af.alloc(2) for _ in range(2)]
    st8 = io.get("_st8_shared")
    if st8 is None:
        st8 = mk.sbuf("gla_st8", [128, 8], F32)
    zst = af.alloc(1024)
    g1st = af.alloc(1024)
    _ou = af.alloc(512)
    outst = [_ou, _ou]

    G4 = (PS_A[:, 0:128], PS_A[:, 512:640], PS_C[:, 0:128], PS_D[:, 0:128])

    for c in range(ncu):
        b = c % 2
        t0 = c * C
        for g in range(8):
            mk.dma("sp", phq_c[b][:, g, :], io["phq"][128 * g:128 * (g + 1), t0:t0 + C],
                   sem_key=f"phq{b}", tag="phq")
            mk.dma("sp", phk_c[b][:, g, :], io["phk"][128 * g:128 * (g + 1), t0:t0 + C],
                   sem_key=f"phk{b}", tag="phk")
        mk.dma("sp", v1_c[b], io["v1"].rearrange("g (i p) c -> p i g c", p=128)[:, c, :, :],
               sem_key=f"v1c{b}", tag="v1c")
        mk.dma("sp", x1t[b], io["x1"][t0:t0 + C, :], sem_key=f"x1t{b}", tag="x1t")
        mk.dma("sp", xnt1[b], io["xn1t"][t0:t0 + C, :], sem_key=f"xnt1{b}", tag="xnt1")

        units = [(g, m) for g in range(8) for m in range(2)]

        def s1(idx):
            g, m = units[idx]
            ug = idx % 2
            qt32, kt32 = qt32_[ug], kt32_[ug]
            r0 = m * 32 + g * 4
            mk.dma("sp", Pb[ug],
                   io["ph_"][r0:r0 + 4, t0:t0 + C].rearrange("r (u t) -> r u t", u=1)
                   .broadcast_to((4, 32, C)),
                   sem_key=f"pb{ug}", tag="pb")
            mk.dma("act", iPb[ug],
                   io["iph"][r0:r0 + 4, t0:t0 + C].rearrange("r (u t) -> r u t", u=1)
                   .broadcast_to((4, 32, C)),
                   sem_key=f"ipb{ug}", tag="ipb")
            mk.dma("sp", mxb[ug],
                   io["mixh"][r0:r0 + 4, t0:t0 + C].rearrange("r (u t) -> r u t", u=1)
                   .broadcast_to((4, 32, C)),
                   sem_key=f"mxb{ug}", tag="mxb")
            tt(mk, "dve", qt32, phq_c[b][:, g, :], Pb[ug], ALU.mult)
            tt(mk, "dve", kt32, phk_c[b][:, g, :], iPb[ug], ALU.mult)

        def s2(idx):
            g, m = units[idx]
            ug = idx % 2
            qt32, t1b32, zq32 = qt32_[ug], t1b32_[ug], zq32_[ug]
            kt32, ktT32, A_sb, w_f = kt32_[ug], ktT32_[ug], A_sb_[ug], w_f_[ug]
            ytmp = ytmp_[g % 2]
            ytmp2 = zq32
            NUM = PS_B[:, 512 * ug:512 * ug + 128]
            KCD = (PS_C if ug == 0 else PS_D)
            DEN = (PS_E if ug == 0 else PS_F)
            transpose(mk, KCD[:, 256:384], kt32, io["identf_sb"][:])
            copy(mk, "act", ktT32, KCD[:, 256:384])
            for j in range(4):
                mm(mk, G4[j], kt32[32 * j:32 * (j + 1), :], qt32[32 * j:32 * (j + 1), :],
                   tile_position=(32 * j, 0), skip=True)
            for j in range(4):
                tt(mk, "dve", A_sb[:, j, :], G4[j], gmask[:], ALU.mult, tag="amask")
            for j in range(4):
                mm(mk, NUM[32 * j:32 * (j + 1), :],
                   state[32 * j:32 * (j + 1), g, m, 0:32],
                   qt32[32 * j:32 * (j + 1), :],
                   start=True, stop=False, tile_position=(32 * j, 32 * j), skip=True)
            for j in range(4):
                mm(mk, NUM[32 * j:32 * (j + 1), :],
                   v1_c[b][:, g, 33 * j:33 * j + 32],
                   A_sb[:, j, :], start=False, stop=True, tile_position=(0, 32 * j), skip=True)
            for j in range(4):
                mm(mk, KCD[32 * j:32 * (j + 1), 128:256],
                   ktT32[:, 32 * j:32 * (j + 1)], ub[:],
                   start=True, stop=True, tile_position=(0, 32 * j), skip=True)
            act(mk, t1b32, KCD[:, 128:256], AF.Identity, bias=state[:, g, m, 32:33])
            tt(mk, "dve", zq32, qt32, t1b32, ALU.mult)
            mm(mk, DEN[:, 0:128], bo32[:], zq32)
            act(mk, w_f, DEN[:, 0:128], AF.Identity, bias=GLA_EPS)
            recip(mk, w_f, w_f)
            tt(mk, "dve", w_f, w_f, mxb[ug], ALU.mult)
            if m == 0:
                tt(mk, "dve", ytmp, NUM[:, 0:128], w_f, ALU.mult)
            else:
                tt(mk, "dve", ytmp2, NUM[:, 0:128], w_f, ALU.mult)
                tt(mk, "dve", y_sb[:, g, :], ytmp, ytmp2, ALU.add)
            for j in range(4):
                mm(mk, DEN[32 * j:32 * (j + 1), 384:417],
                   ktT32[:, 32 * j:32 * (j + 1)],
                   v1_c[b][:, g, 33 * j:33 * j + 33],
                   start=True, stop=True, tile_position=(0, 32 * j), skip=True)
            tt(mk, "dve", state[:, g, m, :], state[:, g, m, :], DEN[:, 384:417], ALU.add)
            ts(mk, "dve", state[:, g, m, :], state[:, g, m, :],
               Pb[ug][:, 127:128], ALU.mult)

        s1(0)
        for idx in range(16):
            if idx + 1 < 16:
                s1(idx + 1)
            s2(idx)
        for n in range(2):
            for g in range(8):
                mm(mk, PS_A[:, n * 512:(n + 1) * 512], y_sb[:, g, :],
                   wout[:, g, n * 512:(n + 1) * 512], start=(g == 0), stop=False)
            mm(mk, PS_A[:, n * 512:(n + 1) * 512], onesb[0:1, :],
               lnrow2[:, 2, n * 512:(n + 1) * 512], start=False, stop=True)
        for n in range(2):
            tt(mk, "dve", zst[:, n * 512:(n + 1) * 512], PS_A[:, n * 512:(n + 1) * 512],
               xnt1[b][:, n * 512:(n + 1) * 512], ALU.add)
        reduce(mk, st8[:, 0:1], zst, ALU.add)
        ts(mk, "dve", st8[:, 1:2], st8[:, 0:1], 1.0 / D, ALU.mult)
        ts(mk, "dve", zst, zst, st8[:, 1:2], ALU.subtract)
        act(mk, g1st, zst, AF.Square, accum_out=st8[:, 2:3])
        act(mk, st8[:, 3:4], st8[:, 2:3], AF.Ln, scale=1.0 / D, bias=LN_EPS)
        act(mk, st8[:, 4:5], st8[:, 3:4], AF.Exp, scale=-0.5)
        ts(mk, "dve", zst, zst, st8[:, 4:5], ALU.mult)
        tt(mk, "dve", zst, zst, lnbc[:, 0, :], ALU.mult)
        for n in range(2):
            mm(mk, PS_B[:, n * 512:(n + 1) * 512], onesb[0:1, :],
               lnrow2[:, 0, n * 512:(n + 1) * 512])
            tt(mk, "dve", zst[:, n * 512:(n + 1) * 512], zst[:, n * 512:(n + 1) * 512],
               PS_B[:, n * 512:(n + 1) * 512], ALU.add)
        reduce(mk, st8[:, 5:6], xnt1[b], ALU.add)
        ts(mk, "dve", st8[:, 5:6], st8[:, 5:6], 1.0 / D, ALU.mult)
        ts(mk, "dve", g1st, xnt1[b], st8[:, 5:6], ALU.subtract)
        act(mk, phq_c[b].rearrange("p a b -> p (a b)"), g1st, AF.Square,
            accum_out=st8[:, 6:7])
        act(mk, st8[:, 6:7], st8[:, 6:7], AF.Ln, scale=1.0 / D, bias=LN_EPS)
        act(mk, st8[:, 7:8], st8[:, 6:7], AF.Exp, scale=-0.5)
        ts(mk, "dve", g1st, g1st, st8[:, 7:8], ALU.mult)
        tt(mk, "dve", g1st, g1st, lnbc[:, 1, :], ALU.mult)
        for n in range(2):
            mm(mk, PS_B[:, n * 512:(n + 1) * 512], onesb[0:1, :],
               lnrow2[:, 1, n * 512:(n + 1) * 512])
            tt(mk, "dve", g1st[:, n * 512:(n + 1) * 512], g1st[:, n * 512:(n + 1) * 512],
               PS_B[:, n * 512:(n + 1) * 512], ALU.add)
        act(mk, g1st, g1st, AF.Silu)
        for n in range(2):
            sl_ = slice(n * 512, (n + 1) * 512)
            tt(mk, "dve", outst[b], zst[:, sl_], g1st[:, sl_], ALU.mult)
            tt(mk, "dve", outst[b], outst[b], x1t[b][:, sl_], ALU.add)
            mk.dma("sp", io["out"][t0:t0 + C, n * 512:(n + 1) * 512], outst[b],
                   sem_key=f"out{b}", tag="outstore")
        if io.get("dbg") is not None and c == 0:
            for k_, t_ in ((8, y_sb[:, 0, :]), (9, zst[:, 0:128]), (10, g1st[:, 0:128]),
                           (11, outst[b][:, 0:128]), (12, y_sb[:, 7, :])):
                copy(mk, "dve", io["_dbgs"][k_ - 8] if False else io["_dbgs2"][k_ - 8], t_)
                mk.dma("sp", io["dbg"][k_, :, :], io["_dbgs2"][k_ - 8], sem_key="dbg", tag="dbg")


def build_l1(T=4096, phases=("e", "e2", "f", "g"), ab_elems=50176, af_elems=21632, dbg=False):
    nc = bass.Bass("TRN2", target_bir_lowering=False)
    io = {}
    EI, EO = "ExternalInput", "ExternalOutput"
    mk = Mk(nc)
    _mk_shared(mk, nc, io, T, ab_elems=ab_elems, af_elems=af_elems)
    dt_ = nc.dram_tensor
    if "e" in phases:
        io["attnf"] = dt_("attnf", [D, T], BF16, kind=EI).ap()
        io["wm"] = dt_("wm", [128, 8, 1024], F32, kind=EI).ap()
        io["bm_row"] = dt_("bm_row", [1, D], F32, kind=EI).ap()
        io["g32"] = dt_("g32", [T, D], F32, kind=EI).ap()
        io["x"] = dt_("x", [T, D], F32, kind=EI).ap()
        io["x1"] = dt_("x1", [T, D], F32, kind=EO).ap()
    else:
        io["x1"] = dt_("x1", [T, D], F32, kind=EI).ap()
    if "e2" in phases:
        io["xnf"] = dt_("xnf1", [D, T], BF16, kind=EO).ap()
        io["xn1t"] = dt_("xn1t", [T, D], F32, kind=EO).ap()
    else:
        io["xnf"] = dt_("xnf1", [D, T], BF16, kind=EI).ap()
        io["xn1t"] = dt_("xn1t", [T, D], F32, kind=EI).ap()
    if "f" in phases:
        for nm in ("wq1", "wk1", "wv1"):
            io[nm] = dt_(nm, [128, 8, 1024], F32, kind=EI).ap()
        io["wbm"] = dt_("wbm", [128, 2, 8, 64], F32, kind=EI).ap()
        io["bbm_col"] = dt_("bbm_col", [64, 2], F32, kind=EI).ap()
        io["permT"] = dt_("permT", [128, 128], F32, kind=EI).ap()
        io["cosx"] = dt_("cosx", [128, T], F32, kind=EI).ap()
        io["sinx"] = dt_("sinx", [128, T], F32, kind=EI).ap()
        io["e2"] = dt_("e2c", [64, 64], F32, kind=EI).ap()
        io["phq"] = dt_("phq", [D, T], F32, kind=EO).ap()
        io["phk"] = dt_("phk", [D, T], F32, kind=EO).ap()
        io["v1"] = dt_("v1", [8, T, 132], F32, kind=EO).ap()
        io["mixh"] = dt_("mixh", [64, T], F32, kind=EO).ap()
        io["ph_"] = dt_("ph_", [64, T], F32, kind=EO).ap()
        io["iph"] = dt_("iph", [64, T], F32, kind=EO).ap()
    else:
        io["phq"] = dt_("phq", [D, T], F32, kind=EI).ap()
        io["phk"] = dt_("phk", [D, T], F32, kind=EI).ap()
        io["v1"] = dt_("v1", [8, T, 132], F32, kind=EI).ap()
        io["mixh"] = dt_("mixh", [64, T], F32, kind=EI).ap()
        io["ph_"] = dt_("ph_", [64, T], F32, kind=EI).ap()
        io["iph"] = dt_("iph", [64, T], F32, kind=EI).ap()
    if "g" in phases:
        io["bo"] = dt_("boc", [128, 128], F32, kind=EI).ap()
        io["wout"] = dt_("wout", [128, 8, 1024], F32, kind=EI).ap()
        io["bout_row"] = dt_("bout_row", [1, D], F32, kind=EI).ap()
        io["lnrows"] = dt_("lnrows", [4, D], F32, kind=EI).ap()
        io["out"] = dt_("outp", [T, D], F32, kind=EO).ap()
        if dbg:
            io["dbg"] = dt_("dbg", [16, 128, 128], F32, kind=EO).ap()

    if "e" in phases:
        ph_l0post(mk, io, T)
        io["ab"].reset(); io["af"].reset()
    if "e2" in phases:
        ph_ln0(mk, io, T, layer=1)
        io["xnf_sb"] = mk._xnf_sb
    if "f" in phases:
        if "e2" not in phases:
            xnf_sb = io["ab"].alloc(8, T)
            for cc in range(8):
                mk.dma("sp", xnf_sb[:, cc, :], io["xnf"][128 * cc:128 * (cc + 1), :],
                       sem_key="xnf1ld", tag="xnf1_load")
            io["xnf_sb"] = xnf_sb
        ph_l1prep(mk, io, T)
    if "g" in phases:
        io["ab"].reset(); io["af"].reset()
        ph_l1gla(mk, io, T)
    mk.emit_program()
    return nc



def _host_pack(inputs, T):
    f32 = np.float32
    pk = {}
    pack8 = lambda W: np.ascontiguousarray(W.reshape(8, 128, 1024).transpose(1, 0, 2)).astype(f32)
    packcol = lambda b: np.ascontiguousarray(b.reshape(8, 128).T).astype(f32)
    pk["wq"] = pack8(inputs["L0_attn_Wq"])
    pk["wk"] = pack8(inputs["L0_attn_Wk"])
    pk["wv"] = pack8(inputs["L0_attn_Wv"])
    pk["wm"] = pack8(inputs["L0_attn_Wm"])
    pk["bq"] = packcol(inputs["L0_attn_bq"])
    pk["bk"] = packcol(inputs["L0_attn_bk"])
    pk["bv"] = packcol(inputs["L0_attn_bv"])
    pk["bv_row"] = inputs["L0_attn_bv"].reshape(1, -1).astype(f32)
    pk["bm_row"] = inputs["L0_attn_bm"].reshape(1, -1).astype(f32)
    pk["w_router"] = inputs["L0_ffn_w_router"].astype(f32)
    w1, b1 = inputs["L0_ffn_w1"], inputs["L0_ffn_b1"]
    w2, b2 = inputs["L0_ffn_w2"], inputs["L0_ffn_b2"]
    pk["w1h"] = np.ascontiguousarray(w1.reshape(8, 8, 128, 4096).transpose(0, 2, 1, 3)).astype(f32)
    pk["w2h"] = np.ascontiguousarray(w2.reshape(8, 8, 4, 128, 1024).transpose(0, 3, 1, 2, 4)).astype(f32)
    pk["b1c"] = np.ascontiguousarray(b1.reshape(8, 32, 128).transpose(2, 0, 1)).astype(f32)
    pk["b2r"] = b2.astype(f32)
    packT = lambda W: np.ascontiguousarray(W.T.reshape(8, 128, 1024).transpose(1, 0, 2)).astype(f32)
    pk["wq1"] = packT(inputs["L1_attn_Wq"])
    pk["wk1"] = packT(inputs["L1_attn_Wk"])
    pk["wv1"] = packT(inputs["L1_attn_Wv"])
    pk["wout"] = packT(inputs["L1_attn_Wout"])
    perm = np.zeros(64, np.int64)
    for m in range(2):
        for g in range(8):
            for j in range(4):
                perm[m * 32 + g * 4 + j] = (4 * g + j) * 2 + m
    Wb_p = inputs["L1_attn_Wbeta"][perm]
    Wm_p = inputs["L1_attn_Wmix"][perm]
    pk["wbm"] = np.ascontiguousarray(
        np.stack([Wb_p.T.reshape(8, 128, 64).transpose(1, 0, 2),
                  Wm_p.T.reshape(8, 128, 64).transpose(1, 0, 2)], axis=1)).astype(f32)
    pk["bbm_col"] = np.stack([inputs["L1_attn_bbeta"][perm],
                              inputs["L1_attn_bmix"][perm]], 1).astype(f32)
    pk["bout_row"] = inputs["L1_attn_bout"].reshape(1, -1).astype(f32)
    pk["lnrows"] = np.stack([inputs["L1_attn_ln_g"], inputs["L1_attn_ln_b"],
                             inputs["L1_ffn_ln_g"], inputs["L1_ffn_ln_b"]]).astype(f32)
    pk["cident"] = np.eye(128, dtype=f32)
    kk = np.arange(128)[:, None]
    qq = np.arange(512)[None, :]
    dmask = np.stack([(kk + 128 * j <= qq).astype(f32) for j in range(4)])
    pk["dmask"] = np.ascontiguousarray(np.transpose(dmask, (1, 0, 2)))
    Pm = np.zeros((128, 128), f32)
    for p in range(128):
        blk, d_ = (p // 32) * 32, p % 32
        if d_ < 16:
            Pm[p, blk + d_ + 16] = -1.0
        else:
            Pm[p, blk + d_ - 16] = 1.0
    pk["permT"] = np.ascontiguousarray(Pm.T)
    invf = 1.0 / 10000.0 ** (np.arange(16, dtype=f32) / 16.0)
    t = np.arange(T, dtype=f32)
    lane = np.arange(128) % 16
    pk["cosx"] = np.cos(t[None, :] * invf[lane][:, None]).astype(f32)
    pk["sinx"] = np.sin(t[None, :] * invf[lane][:, None]).astype(f32)
    e2 = np.zeros((64, 64), f32)
    for r in range(64):
        e2[r, r % 32] = 1.0
        e2[r, r % 32 + 32] = 1.0
    pk["e2c"] = np.ascontiguousarray(e2)
    s_, t_ = np.arange(128)[:, None], np.arange(128)[None, :]
    pk["gmaskc"] = (s_ <= t_).astype(f32)
    pk["ubc"] = (s_ <= t_).astype(f32)
    bo = np.zeros((128, 128), f32)
    for p in range(128):
        bo[p, (p // 32) * 32:(p // 32) * 32 + 32] = 1.0
    pk["boc"] = bo
    return pk


_L0_KEYS = ("x", "w_router", "wq", "wk", "wv", "bq", "bk", "bv", "bv_row", "ubc",
            "w1h", "w2h", "b1c", "b2r", "cident")
_L1_KEYS = ("attnf", "wm", "bm_row", "g32", "x", "x1", "wq1", "wk1", "wv1", "wbm",
            "bbm_col", "permT", "cosx", "sinx", "e2c", "wout", "bout_row", "lnrows",
            "ubc", "boc", "cident")


def kernel_unfused(**inputs):
    T = 4096
    n = inputs["L0_attn_x"].shape[0]
    pk = _host_pack(inputs, T)
    xs = [np.ascontiguousarray(inputs["L0_attn_x"][b]).astype(np.float32) for b in range(n)]

    nc0 = build_l0(T, phases=("a", "b", "c", "d"))
    maps0 = []
    for b in range(n):
        m = {k: pk[k] for k in _L0_KEYS if k in pk}
        m["x"] = xs[b]
        maps0.append(m)
    res0 = run_bass_kernel_spmd(nc0, maps0, core_ids=list(range(n)))
    global _dbg_res0
    _dbg_res0 = res0.results

    nc1 = build_l1(T, phases=("e", "e2", "f", "g"))
    maps1 = []
    for b in range(n):
        m = {k: pk[k] for k in _L1_KEYS if k in pk}
        m["x"] = xs[b]
        m["attnf"] = res0.results[b]["attnf"]
        m["g32"] = res0.results[b]["g32"]
        maps1.append(m)
    res1 = run_bass_kernel_spmd(nc1, maps1, core_ids=list(range(n)))
    global _dbg_res1
    _dbg_res1 = res1.results
    out = np.stack([res1.results[b]["outp"] for b in range(n)], axis=0)
    return out


def build_full(T=4096, ab_elems=71680, af_elems=13312):
    nc = bass.Bass("TRN2", target_bir_lowering=False)
    io = {}
    EI = "ExternalInput"
    mk = Mk(nc)
    _mk_shared(mk, nc, io, T, ab_elems=ab_elems, af_elems=af_elems)
    dt_ = nc.dram_tensor
    io["x"] = dt_("x", [T, D], F32, kind=EI).ap()
    io["w_router"] = dt_("w_router", [D, E0], F32, kind=EI).ap()
    for nm in ("wq", "wk", "wv", "wm", "wq1", "wk1", "wv1", "wout"):
        io[nm] = dt_(nm, [128, 8, 1024], F32, kind=EI).ap()
    for nm in ("bq", "bk", "bv"):
        io[nm] = dt_(nm, [128, 8], F32, kind=EI).ap()
    io["bv_row"] = dt_("bv_row", [1, D], F32, kind=EI).ap()
    io["bm_row"] = dt_("bm_row", [1, D], F32, kind=EI).ap()
    io["w1h"] = dt_("w1h", [8, 128, 8, 4096], F32, kind=EI).ap()
    io["w2h"] = dt_("w2h", [8, 128, 8, 4, 1024], F32, kind=EI).ap()
    io["b1c"] = dt_("b1c", [128, 8, 32], F32, kind=EI).ap()
    io["b2r"] = dt_("b2r", [8, 1024], F32, kind=EI).ap()
    io["wbm"] = dt_("wbm", [128, 2, 8, 64], F32, kind=EI).ap()
    io["bbm_col"] = dt_("bbm_col", [64, 2], F32, kind=EI).ap()
    io["permT"] = dt_("permT", [128, 128], F32, kind=EI).ap()
    io["cosx"] = dt_("cosx", [128, T], F32, kind=EI).ap()
    io["sinx"] = dt_("sinx", [128, T], F32, kind=EI).ap()
    io["e2"] = dt_("e2c", [64, 64], F32, kind=EI).ap()
    io["bo"] = dt_("boc", [128, 128], F32, kind=EI).ap()
    io["bout_row"] = dt_("bout_row", [1, D], F32, kind=EI).ap()
    io["lnrows"] = dt_("lnrows", [4, D], F32, kind=EI).ap()
    IN = "Internal"
    xnf0 = dt_("i_xnf0", [D, T], BF16, kind=IN).ap()
    xnf1 = dt_("i_xnf1", [D, T], BF16, kind=IN).ap()
    io["msk"] = dt_("i_msk", [T, E0], F32, kind=IN).ap()
    io["qf"] = dt_("i_qf", [D, T], BF16, kind=IN).ap()
    io["kf"] = dt_("i_kf", [D, T], BF16, kind=IN).ap()
    io["v0"] = dt_("i_v0", [8, T, 130], BF16, kind=IN).ap()
    io["attnf"] = dt_("i_attnf", [D, T], BF16, kind=IN).ap()
    io["g32"] = dt_("i_g32", [T, D], F32, kind=IN).ap()
    io["x1"] = dt_("i_x1", [T, D], F32, kind=IN).ap()
    io["xn1t"] = dt_("i_xn1t", [T, D], F32, kind=IN).ap()
    io["phq"] = dt_("i_phq", [D, T], F32, kind=IN).ap()
    io["phk"] = dt_("i_phk", [D, T], F32, kind=IN).ap()
    io["v1"] = dt_("i_v1", [8, T, 132], F32, kind=IN).ap()
    io["mixh"] = dt_("i_mixh", [64, T], F32, kind=IN).ap()
    io["ph_"] = dt_("i_ph", [64, T], F32, kind=IN).ap()
    io["iph"] = dt_("i_iph", [64, T], F32, kind=IN).ap()
    io["out"] = dt_("outp", [T, D], F32, kind="ExternalOutput").ap()

    ab, af = io["ab"], io["af"]
    io["xnf"] = xnf0
    ph_ln0(mk, io, T, layer=0)
    io["xnf_sb"] = mk._xnf_sb
    ph_l0proj(mk, io, T)
    ab.reset(); af.reset()
    ph_l0attn(mk, io, T)
    ab.reset(); af.reset()
    ph_l0moe(mk, io, T)
    ab.reset(); af.reset()
    ph_l0post(mk, io, T)
    ab.reset(); af.reset()
    io["xnf"] = xnf1
    ph_ln0(mk, io, T, layer=1)
    io["xnf_sb"] = mk._xnf_sb
    af.reset()
    ph_l1prep(mk, io, T)
    ab.reset(); af.reset()
    ph_l1gla(mk, io, T)
    mk.emit_program()
    return nc


_FULL_KEYS = ("x", "w_router", "wq", "wk", "wv", "wm", "bq", "bk", "bv", "bv_row",
              "bm_row", "w1h", "w2h", "b1c", "b2r", "wq1", "wk1", "wv1",
              "wout", "wbm", "bbm_col", "permT", "cosx", "sinx", "e2c",
              "ubc", "boc", "bout_row", "lnrows", "cident")


def kernel(**inputs):
    T = 4096
    n = inputs["L0_attn_x"].shape[0]
    pk = _host_pack(inputs, T)
    nc = build_full(T)
    maps = []
    for b in range(n):
        m = {k: pk[k] for k in _FULL_KEYS if k in pk}
        m["x"] = np.ascontiguousarray(inputs["L0_attn_x"][b]).astype(np.float32)
        maps.append(m)
    res = run_bass_kernel_spmd(nc, maps, core_ids=list(range(n)))
    return np.stack([res.results[b]["outp"] for b in range(n)], axis=0)
